# version 14
# baseline (speedup 1.0000x reference)
import numpy as np
import ml_dtypes
import concourse.bass as bass
import concourse.mybir as mybir
from concourse.bass_utils import run_bass_kernel_spmd

F32 = mybir.dt.float32
BF16 = mybir.dt.bfloat16
FP8 = mybir.dt.float8e4
I32 = mybir.dt.int32
AF = mybir.ActivationFunctionType
OP = mybir.AluOpType
AX = mybir.AxisListType

B, T, D, S = 2, 8192, 2048, 64
NT = 2048
NTILES = 16
KC = 16
I = 8192
ICH = 64
STILE = 512
NST = 4
TPS = 4
NMST = 8

C448I = float(np.float32(1.0) / np.float32(448.0))
EPS30 = 1e-30
LNEPS = 1e-5
RD = float(np.float32(1.0 / 2048.0))
JSH = 3072


class SemT:
    __slots__ = ("h", "n")

    def __init__(self, nc, name):
        self.h = nc.alloc_semaphore(name=name)
        self.n = 0


def build_nc():
    nc = bass.Bass("TRN2", target_bir_lowering=False)

    def di(name, shape, dt=F32):
        return nc.dram_tensor(name, shape, dt, kind="ExternalInput").ap()

    x_own = di("x_own", [NT, D])
    x_oth = di("x_oth", [NT, D])
    wtsT = di("wtsT", [D, D])
    wkvT_in = di("wkvT_in", [D, 128])
    woT_in = di("woT_in", [S, D])
    gate_r = di("gate_r", [1, D])
    lnw_r = di("lnw_r", [1, D])
    lnb_r = di("lnb_r", [1, D])
    tdc = di("tdc", [S, 1])
    tfB = di("tfB", [128, S])
    tdB = di("tdB", [4, S])
    maskM = di("maskM", [4, S])
    maskE = di("maskE", [4, S])
    cT_in = di("cT_in", [128, KC])
    wmod_sh = di("wmod_sh", [D, JSH])
    gw = di("gw", [I, D])
    uw = di("uw", [I, D])
    dw = di("dw", [D, I])
    eyeF = di("eyeF", [128, 128], F32)
    ones_r = di("ones_r", [1, 128])
    ones_c = di("ones_c", [128, 1])

    y = nc.dram_tensor("y", [NT, D], F32, kind="ExternalOutput").ap()

    def dint(name, shape, dt):
        return nc.dram_tensor(name, shape, dt, kind="Internal").ap()

    scr_g = dint("scr_g", [I, D], BF16)
    scr_u = dint("scr_u", [I, D], BF16)
    scr_d = dint("scr_d", [D, I], BF16)
    scr_wg = dint("scr_wg", [D, D], BF16)
    scrT_g = dint("scrT_g", [ICH, D, 128], BF16)
    scrT_u = dint("scrT_u", [ICH, D, 128], BF16)
    scrT_d = dint("scrT_d", [KC, I, 128], BF16)
    ag_mod_src = dint("ag_mod_src", [1, JSH], F32)
    ag_mod_dst = dint("ag_mod_dst", [4, JSH], F32)
    ag_car_src = dint("ag_car_src", [S, 1], F32)
    ag_car_dst = dint("ag_car_dst", [4, S], F32)

    sb = nc.alloc_sbuf_tensor
    eyeF_s = sb("eyeF_s", [128, 128], F32)
    ones_rs = sb("ones_rs", [1, 128], F32)
    ones_cs = sb("ones_cs", [128, 1], F32)
    gbT = sb("gbT", [128, KC], F32)
    WoTga = sb("WoTga", [S, D], BF16)
    WkvTb = sb("WkvTb", [128, KC, 128], BF16)
    cT_s = sb("cT_s", [128, KC], F32)
    scT = sb("scT", [128, KC], BF16)
    etf_bc = sb("etf_bc", [128, S], F32)
    tdc_s = sb("tdc_s", [S, 1], F32)
    wcol = sb("wcol", [S, 1], F32)
    tdB_s = sb("tdB_s", [4, S], F32)
    maskM_s = sb("maskM_s", [4, S], F32)
    maskE_s = sb("maskE_s", [4, S], F32)
    carw = sb("carw", [4, S], F32)
    cin_col = sb("cin_col", [S, 1], F32)
    wkvT_s = sb("wkvT_s", [S, NT], F32)
    statesB = sb("statesB", [S, NT], BF16)
    lnstat = sb("lnstat", [128, 256], F32)
    qsc = sb("qsc", [128, 768], F32)
    q8buf = sb("q8buf", [128, 4 * D], FP8)
    mst = [sb("mst0", [1, 512], F32), sb("mst1", [1, 512], F32)]

    ARB = 84480
    arb = sb("arb", [128, ARB], BF16)

    def vB(off, n, p=128):
        assert off + n <= ARB and off % 2 == 0, (off, n)
        return arb[0:p, off:off + n]

    def vF(off, n, p=128):
        assert off + 2 * n <= ARB and off % 2 == 0, (off, n)
        return arb[0:p, off:off + 2 * n].bitcast(F32)

    pre_w = [vF(68096, 2048), vF(76288, 2048)]
    pre_dq = [vB(72192, 2048), vB(74240, 2048)]
    tc_buf = [vB(80384, 2048), vB(82432, 2048)]

    omg_bc = vF(0, 2048)
    A1_bc = vF(4096, 2048)
    A0_bc = vF(8192, 2048)
    lnw_bc = vF(12288, 2048)
    lnb_bc = vF(16384, 2048)
    mx_xo = [vF(20480, 2048), vF(24576, 2048)]
    mx_hos = [vF(28672, 2048), vF(32768, 2048)]
    mx_xs = [vF(36864, 2048), vF(40960, 2048)]
    mx_scr = vF(45056, 2048)
    mx_w1 = vF(49152, 64)
    mx_w2 = vF(49280, 64)
    mxb_hot = vB(49408, 2048)
    mxb_xn = vB(51456, 2048)
    mxb_hsT = vB(53504, 4096).rearrange("p (t k c) -> p t k c", t=2, k=KC)
    mxb_xnT = vB(57600, 2048).rearrange("p (k c) -> p k c", k=KC)
    mxb_w = [vB(59648, 4096).rearrange("p (k c) -> p k c", k=KC),
             vB(63744, 4096).rearrange("p (k c) -> p k c", k=KC)]
    p0_wm = [vB(20480, 8192).rearrange("p (k c) -> p k c", k=KC),
             vB(36864, 8192).rearrange("p (k c) -> p k c", k=KC)]
    p0_wo = vF(53504, 2048, p=64)
    scanbuf = vF(68096, NT, p=S)
    p0_r = [vF(57600, 2048, p=1), vF(61696, 2048, p=1),
            vF(45056, 2048, p=1), vF(49152, 2048, p=1)]
    p0_gbc = vF(28672, 2048)

    mqfm = vB(0, 32768).rearrange("p (i t) -> p i t", i=ICH)
    h2qT = vB(32768, 8192).rearrange("p (k t c) -> p k t c", k=KC, t=TPS)
    e_gu = [vB(40960 + i * 2048, 2048).rearrange("p (k c) -> p k c", k=KC)
            for i in range(4)]
    e_dn = [vB(49152 + i * 2048, 2048).rearrange("p (k c) -> p k c", k=KC)
            for i in range(2)]
    e_dqB = vB(53248, 512)
    d_h2q = vB(53760, 2048)
    B1b = vB(55808, 2048)
    B0b = vB(57856, 2048)
    d_xoh2 = vF(59904, 2048)
    d_x2 = vF(64000, 2048)
    e_sil = vF(59904, 512)
    e_m = vF(60928, 512)
    e_Bv = vF(61952, 512)
    e_rot = vF(62976, 512)
    e_x2r = vF(64000, 512)
    e_fin = vF(65024, 512)
    d_scr = vF(0, 2048)
    d_h2qb = vB(68096, 2048)
    d_xoh2b = vF(70144, 2048)
    d_x2b = vF(74240, 2048)

    pb = [nc.alloc_psum_tensor(f"pb{i}", [128, 512], F32) for i in range(8)]

    MM = SemT(nc, "MM")
    EV = SemT(nc, "EV")
    ACS = SemT(nc, "ACS")
    GP = SemT(nc, "GP")
    CC = SemT(nc, "CC")
    BAR = SemT(nc, "BAR")

    LD_A = SemT(nc, "LD_A")
    MST_S = [SemT(nc, "MST0"), SemT(nc, "MST1")]
    WM_S = [SemT(nc, "WM0"), SemT(nc, "WM1")]
    MODST = [SemT(nc, "MODST0"), SemT(nc, "MODST1")]
    GLD = SemT(nc, "GLD")
    PRE = [SemT(nc, "PRE0"), SemT(nc, "PRE1")]
    SCW = [SemT(nc, "SCW0"), SemT(nc, "SCW1")]
    TCR = [SemT(nc, "TCR0"), SemT(nc, "TCR1")]
    TCW = [SemT(nc, "TCW0"), SemT(nc, "TCW1")]
    MXX = [SemT(nc, "MXX0"), SemT(nc, "MXX1")]
    MXO = [SemT(nc, "MXO0"), SemT(nc, "MXO1")]
    WTS = [SemT(nc, "WTS0"), SemT(nc, "WTS1")]
    HST = SemT(nc, "HST")
    XNT = SemT(nc, "XNT")
    H2QT = SemT(nc, "H2QT")
    H2QS = SemT(nc, "H2QS")
    H2QT1 = SemT(nc, "H2QT1")
    H2QS1 = SemT(nc, "H2QS1")
    DXO = SemT(nc, "DXO")
    GU = [SemT(nc, f"GU{i}") for i in range(4)]
    DN = [SemT(nc, "DN0"), SemT(nc, "DN1")]
    XOD = SemT(nc, "XOD")
    YST = SemT(nc, "YST")
    CAR = SemT(nc, "CAR")

    with nc.Block() as blk:
        barn = [0]

        def barrier():
            barn[0] += 5
            tgt = barn[0]
            for e in ("sync", "scalar", "vector", "tensor", "gpsimd"):
                def f(eng, tgt=tgt, e=e):
                    if e in ("scalar", "vector", "gpsimd", "tensor"):
                        eng.drain()
                    eng.sem_inc(BAR.h, 1)
                    eng.wait_ge(BAR.h, tgt)
                getattr(blk, e)(f)

        @blk.sync
        def _(sync):
            for dst, src in [(eyeF_s[:], eyeF),
                             (ones_rs[:], ones_r), (ones_cs[:], ones_c),
                             (cT_s[:], cT_in), (tdc_s[:], tdc),
                             (tdB_s[:], tdB), (maskM_s[:], maskM),
                             (maskE_s[:], maskE),
                             (qsc[:, 0:S], tfB),
                             (p0_r[0][:], gate_r),
                             (p0_r[1][:], lnw_r), (p0_r[2][:], lnb_r),
                             (p0_wo[:], woT_in)]:
                sync.dma_start(out=dst, in_=src).then_inc(LD_A.h, 16)
        LD_A.n += 13 * 16
        LD0 = LD_A.n

        @blk.gpsimd
        def _(gpsimd):
            gpsimd.dma_start(
                out=WkvTb[:],
                in_=wkvT_in.rearrange("(k p) c -> p k c", p=128),
            ).then_inc(GLD.h, 16)
        GLD.n += 16

        @blk.scalar
        def _(scalar):
            scalar.wait_ge(LD_A.h, LD0)
            scalar.activation(out=scT[:], in_=cT_s[:], func=AF.Silu).then_inc(ACS.h, 1)
            scalar.activation(out=p0_r[3][:], in_=p0_r[0][:], func=AF.Sigmoid)
            scalar.activation(out=etf_bc[:], in_=qsc[:, 0:S], func=AF.Exp)
            scalar.activation(out=wcol[:], in_=tdc_s[:], func=AF.Exp).then_inc(ACS.h, 1)
        ACS.n += 2

        @blk.vector
        def _(vector):
            vector.wait_ge(ACS.h, ACS.n)
            vector.tensor_scalar(out=p0_r[0][:], in0=p0_r[3][:],
                                 scalar1=-1.0, scalar2=1.0,
                                 op0=OP.mult, op1=OP.add).then_inc(EV.h, 1)
        EV.n += 1

        MOD_MM0 = MM.n
        mod_ev = [None, None]
        for n in range(6):
            slot = n % 2

            def fg(gpsimd, n=n, slot=slot):
                if n >= 2:
                    gpsimd.wait_ge(MM.h, MOD_MM0 + n - 1)
                gpsimd.dma_start(
                    out=p0_wm[slot][:],
                    in_=wmod_sh.rearrange("(k p) (n f) -> p k n f",
                                          p=128, f=512)[:, :, n, :],
                ).then_inc(WM_S[slot].h, 16)
            blk.gpsimd(fg)
            WM_S[slot].n += 16
            wmt = WM_S[slot].n

            def ft(tensor, n=n, slot=slot, wmt=wmt, evw=mod_ev[slot]):
                if n == 0:
                    tensor.wait_ge(ACS.h, 1)
                if evw is not None:
                    tensor.wait_ge(EV.h, evw)
                tensor.wait_ge(WM_S[slot].h, wmt)
                for k in range(KC):
                    inst = tensor.matmul(pb[slot][0:1, :], scT[:, k:k + 1],
                                         p0_wm[slot][:, k, :],
                                         start=(k == 0), stop=(k == KC - 1))
                inst.then_inc(MM.h, 1)
            blk.tensor(ft)
            MM.n += 1
            mmt = MM.n

            def fv(vector, n=n, slot=slot, mmt=mmt, stw=MODST[slot].n):
                vector.wait_ge(MM.h, mmt)
                if n >= 2:
                    vector.wait_ge(MODST[slot].h, stw)
                vector.tensor_copy(mst[slot][:], pb[slot][0:1, :]).then_inc(EV.h, 1)
            blk.vector(fv)
            EV.n += 1
            evt = EV.n

            def fst(sync, n=n, slot=slot, evt=evt):
                sync.wait_ge(EV.h, evt)
                sync.dma_start(out=ag_mod_src[0:1, 512 * n:512 * (n + 1)],
                               in_=mst[slot][:]).then_inc(MODST[slot].h, 16)
            blk.sync(fst)
            MODST[slot].n += 16
            mod_ev[slot] = EV.n

        @blk.gpsimd
        def _(gpsimd):
            gpsimd.wait_ge(MODST[0].h, MODST[0].n)
            gpsimd.wait_ge(MODST[1].h, MODST[1].n)
            gpsimd.collective_compute(
                "AllGather", OP.bypass,
                replica_groups=[[0, 1, 2, 3], [4, 5, 6, 7]],
                ins=[ag_mod_src], outs=[ag_mod_dst],
            ).then_inc(CC.h, 1)
        CC.n += 1

        mst_mm = [None, None]
        mst_ev = [None, None]

        def bcast_dram(vec_j0, target, add_one, nparts=128, banks=(2, 3)):
            for c in range(4):
                j = vec_j0 + 512 * c
                r, off = j // JSH, j % JSH
                slot = c % 2

                def fs(sync, r=r, off=off, slot=slot):
                    sync.wait_ge(CC.h, CC.n)
                    if mst_mm[slot] is not None:
                        sync.wait_ge(MM.h, mst_mm[slot])
                    sync.dma_start(out=mst[slot][:],
                                   in_=ag_mod_dst[r:r + 1, off:off + 512]
                                   ).then_inc(MST_S[slot].h, 16)
                blk.sync(fs)
                MST_S[slot].n += 16
                ldt = MST_S[slot].n

                def ft(tensor, slot=slot, ldt=ldt, nparts=nparts, banks=banks):
                    tensor.wait_ge(MST_S[slot].h, ldt)
                    if mst_ev[slot] is not None:
                        tensor.wait_ge(EV.h, mst_ev[slot])
                    tensor.matmul(pb[banks[slot]][0:nparts, :],
                                  ones_rs[0:1, 0:nparts],
                                  mst[slot][:], start=True, stop=True
                                  ).then_inc(MM.h, 1)
                blk.tensor(ft)
                MM.n += 1
                mst_mm[slot] = MM.n
                mmt = MM.n

                def fv(vector, slot=slot, mmt=mmt, c=c, banks=banks,
                       nparts=nparts, target=target, add_one=add_one):
                    vector.wait_ge(MM.h, mmt)
                    dst = target[:, 512 * c:512 * (c + 1)]
                    if add_one:
                        vector.tensor_scalar_add(dst, pb[banks[slot]][0:nparts, :],
                                                 1.0).then_inc(EV.h, 1)
                    else:
                        vector.tensor_copy(dst, pb[banks[slot]][0:nparts, :]
                                           ).then_inc(EV.h, 1)
                blk.vector(fv)
                EV.n += 1
                mst_ev[slot] = EV.n

        bcast_dram(0, A0_bc, False)
        bcast_dram(2048, A1_bc, True)

        row_free = [None, None]

        def bcast_row(row_ap, target):
            for c in range(4):
                slot = c % 2

                def ft(tensor, c=c, slot=slot, row_ap=row_ap):
                    tensor.wait_ge(EV.h, 1)
                    if row_free[slot] is not None:
                        tensor.wait_ge(EV.h, row_free[slot])
                    tensor.matmul(pb[4 + slot][:, :], ones_rs[0:1, :],
                                  row_ap[0:1, 512 * c:512 * (c + 1)],
                                  start=True, stop=True).then_inc(MM.h, 1)
                blk.tensor(ft)
                MM.n += 1
                mmt = MM.n

                def fv(vector, c=c, slot=slot, mmt=mmt, target=target):
                    vector.wait_ge(MM.h, mmt)
                    vector.tensor_copy(target[:, 512 * c:512 * (c + 1)],
                                       pb[4 + slot][:, :]).then_inc(EV.h, 1)
                blk.vector(fv)
                EV.n += 1
                row_free[slot] = EV.n

        bcast_row(p0_r[0], omg_bc)
        bcast_row(p0_r[1], lnw_bc)
        bcast_row(p0_r[2], lnb_bc)
        bcast_row(p0_r[3], p0_gbc)
        ROWS_DONE_EV = EV.n

        @blk.sync
        def _(sync):
            sync.wait_ge(EV.h, ROWS_DONE_EV)
            sync.wait_ge(CC.h, CC.n)
            sync.dma_start(out=p0_r[0][:],
                           in_=ag_mod_dst[1:2, 1024:3072]).then_inc(LD_A.h, 16)
        LD_A.n += 16
        GA_LD = LD_A.n

        @blk.vector
        def _(vector):
            vector.wait_ge(LD_A.h, GA_LD)
            vector.tensor_scalar_max(p0_r[1][:], p0_r[0][:], 0.25)
            vector.tensor_scalar_min(p0_r[2][:], p0_r[0][:], -0.25)
            vector.tensor_single_scalar(p0_r[3][:], p0_r[0][:], 0.0, OP.is_ge)
            vector.drain()
            vector.tensor_sub(p0_r[0][:], p0_r[1][:], p0_r[2][:])
            vector.drain()
            vector.tensor_tensor(p0_r[0][:], p0_r[3][:], p0_r[0][:], OP.mult)
            vector.drain()
            vector.tensor_add(p0_r[0][:], p0_r[2][:], p0_r[0][:]
                              ).then_inc(EV.h, 1)
        EV.n += 1
        GA_EV = EV.n

        ga_free = [None, None]
        for c in range(4):
            slot = c % 2

            def ft(tensor, c=c, slot=slot):
                tensor.wait_ge(EV.h, GA_EV)
                if ga_free[slot] is not None:
                    tensor.wait_ge(EV.h, ga_free[slot])
                tensor.matmul(pb[6 + slot][0:S, :], ones_rs[0:1, 0:S],
                              p0_r[0][0:1, 512 * c:512 * (c + 1)],
                              start=True, stop=True).then_inc(MM.h, 1)
            blk.tensor(ft)
            MM.n += 1
            mmt = MM.n

            def fv(vector, c=c, slot=slot, mmt=mmt):
                vector.wait_ge(MM.h, mmt)
                vector.tensor_tensor(WoTga[:, 512 * c:512 * (c + 1)],
                                     p0_wo[:, 512 * c:512 * (c + 1)],
                                     pb[6 + slot][0:S, :], OP.mult
                                     ).then_inc(EV.h, 1)
            blk.vector(fv)
            EV.n += 1
            ga_free[slot] = EV.n

        @blk.sync
        def _(sync):
            sync.wait_ge(EV.h, EV.n)
            sync.dma_start(out=p0_r[1][:],
                           in_=ag_mod_dst[3:4, 1024:3072]).then_inc(LD_A.h, 16)
        LD_A.n += 16
        GB_LD = LD_A.n

        @blk.vector
        def _(vector):
            vector.wait_ge(LD_A.h, GB_LD)
            vector.tensor_scalar_max(p0_r[2][:], p0_r[1][:], 0.25)
            vector.tensor_scalar_min(p0_r[3][:], p0_r[1][:], -0.25)
            vector.tensor_single_scalar(p0_r[0][:], p0_r[1][:], 0.0, OP.is_ge)
            vector.drain()
            vector.tensor_sub(p0_r[1][:], p0_r[2][:], p0_r[3][:])
            vector.drain()
            vector.tensor_tensor(p0_r[1][:], p0_r[0][:], p0_r[1][:], OP.mult)
            vector.drain()
            vector.tensor_add(p0_r[1][:], p0_r[3][:], p0_r[1][:]
                              ).then_inc(EV.h, 1)
        EV.n += 1
        GB_EV = EV.n

        @blk.tensor
        def _(tensor):
            tensor.wait_ge(EV.h, GB_EV)
            for k in range(KC):
                inst = tensor.transpose(pb[0][:, k:k + 1],
                                        p0_r[1][0:1, 128 * k:128 * (k + 1)],
                                        eyeF_s[0:1, 0:1])
            inst.then_inc(MM.h, 1)
        MM.n += 1
        GBT_MM = MM.n

        @blk.vector
        def _(vector):
            vector.wait_ge(MM.h, GBT_MM)
            vector.tensor_copy(gbT[:], pb[0][:, 0:KC]).then_inc(EV.h, 1)
        EV.n += 1

        barrier()

        p1_ev = [None, None]
        p1_st = [None, None]
        for i in range(KC):
            slot = i % 2

            def fs(sync, i=i, slot=slot):
                if p1_ev[slot] is not None:
                    sync.wait_ge(EV.h, p1_ev[slot])
                sync.dma_start(out=pre_w[slot][:],
                               in_=wtsT[128 * i:128 * (i + 1), :]
                               ).then_inc(PRE[slot].h, 16)
            blk.sync(fs)
            PRE[slot].n += 16
            ldt = PRE[slot].n

            def fv(vector, i=i, slot=slot, ldt=ldt, stw=p1_st[slot]):
                vector.wait_ge(PRE[slot].h, ldt)
                if stw is not None:
                    vector.wait_ge(SCW[slot].h, stw)
                vector.tensor_tensor(pre_dq[slot][:], pre_w[slot][:],
                                     p0_gbc[:], OP.mult).then_inc(EV.h, 1)
            blk.vector(fv)
            EV.n += 1
            p1_ev[slot] = EV.n
            evt = EV.n

            def fw(sync, i=i, slot=slot, evt=evt):
                sync.wait_ge(EV.h, evt)
                sync.dma_start(out=scr_wg[128 * i:128 * (i + 1), :],
                               in_=pre_dq[slot][:]).then_inc(SCW[slot].h, 16)
            blk.sync(fw)
            SCW[slot].n += 16
            p1_st[slot] = SCW[slot].n
        FOLD0, FOLD1 = SCW[0].n, SCW[1].n

        pp_q = [max(x for x in (p1_ev[0], p1_ev[1]) if x is not None)]
        pp_w = [p1_st[0], p1_st[1]]
        pp_g1 = [None]
        pp_g2 = [None]
        pp_pair = [0]

        def prepass_pair(pieces, pair_idx):
            lds = []
            for slot, (src_ap, dst_ap, down, r) in enumerate(pieces):
                def fs(sync, src_ap=src_ap, slot=slot, qw=pp_q[0]):
                    if qw is not None:
                        sync.wait_ge(EV.h, qw)
                    sync.dma_start(out=pre_w[slot][:], in_=src_ap
                                   ).then_inc(PRE[slot].h, 16)
                blk.sync(fs)
                PRE[slot].n += 16
                lds.append(PRE[slot].n)

            qslot = pair_idx % 2

            def fv(vector, pieces=pieces, lds=tuple(lds), qslot=qslot,
                   gw_=pp_g2[0]):
                for slot in range(len(pieces)):
                    vector.wait_ge(PRE[slot].h, lds[slot])
                for slot, (_, _, down, r) in enumerate(pieces):
                    b = 384 * slot
                    w3 = pre_w[slot][:].rearrange("p (b c) -> p b c", c=32)
                    vector.tensor_reduce(qsc[:, b:b + 64], w3, AX.X, OP.max,
                                         apply_absolute_value=True)
                vector.drain()
                if gw_ is not None:
                    vector.wait_ge(GP.h, gw_)
                for slot, (_, _, down, r) in enumerate(pieces):
                    b = 384 * slot
                    sc = b + 128 + 64 * qslot
                    vector.tensor_scalar(qsc[:, sc:sc + 64], qsc[:, b:b + 64],
                                         2.0 * C448I, 2.0 * EPS30, OP.mult, OP.max)
                vector.drain()
                for slot, (_, _, down, r) in enumerate(pieces):
                    b = 384 * slot
                    sc = b + 128 + 64 * qslot
                    vector.reciprocal(qsc[:, b + 64:b + 128], qsc[:, sc:sc + 64])
                vector.drain()
                any_down = any(p[2] for p in pieces)
                if any_down:
                    for slot, (_, _, down, r) in enumerate(pieces):
                        if down:
                            b = 384 * slot
                            sc = b + 128 + 64 * qslot
                            dc = b + 256 + 64 * qslot
                            vector.tensor_scalar_mul(qsc[:, dc:dc + 64],
                                                     qsc[:, sc:sc + 64],
                                                     gbT[:, r:r + 1])
                    vector.drain()
                for slot in range(len(pieces)):
                    b = 384 * slot
                    w3 = pre_w[slot][:].rearrange("p (b c) -> p b c", c=32)
                    qo = 4096 * qslot + 2048 * slot
                    q3 = q8buf[:, qo:qo + 2048].rearrange("p (b c) -> p b c", c=32)
                    i3 = qsc[:, b + 64:b + 128].rearrange("p (b o) -> p b o", o=1)
                    inst = vector.tensor_tensor(q3, w3,
                                                i3.broadcast_to([128, 64, 32]),
                                                OP.mult)
                inst.then_inc(EV.h, 1)
            blk.vector(fv)
            EV.n += 1
            pp_q[0] = EV.n
            evt = EV.n

            def fg(gpsimd, pieces=pieces, evt=evt, qslot=qslot,
                   ww=(pp_w[0], pp_w[1])):
                gpsimd.wait_ge(EV.h, evt)
                for slot in range(len(pieces)):
                    if ww[slot] is not None:
                        gpsimd.wait_ge(SCW[slot].h, ww[slot])
                for slot, (_, _, down, r) in enumerate(pieces):
                    b = 384 * slot
                    sc = b + 128 + 64 * qslot
                    dc = b + 256 + 64 * qslot
                    qo = 4096 * qslot + 2048 * slot
                    q3 = q8buf[:, qo:qo + 2048].rearrange("p (b c) -> p b c", c=32)
                    scol = qsc[:, dc:dc + 64] if down else qsc[:, sc:sc + 64]
                    s3 = scol.rearrange("p (b o) -> p b o", o=1)
                    d3 = pre_dq[slot][:].rearrange("p (b c) -> p b c", c=32)
                    gpsimd.tensor_tensor(d3, q3,
                                         s3.broadcast_to([128, 64, 32]),
                                         OP.mult).then_inc(GP.h, 1)
            blk.gpsimd(fg)
            gpt_s = []
            for _ in pieces:
                GP.n += 1
                gpt_s.append(GP.n)
            pp_g2[0] = pp_g1[0]
            pp_g1[0] = GP.n

            for slot, (_, dst_ap, down, r) in enumerate(pieces):
                def fw(sync, dst_ap=dst_ap, slot=slot, gpt=gpt_s[slot]):
                    sync.wait_ge(GP.h, gpt)
                    sync.dma_start(out=dst_ap, in_=pre_dq[slot][:]
                                   ).then_inc(SCW[slot].h, 16)
                blk.sync(fw)
                SCW[slot].n += 16
                pp_w[slot] = SCW[slot].n

        pp_pieces = []
        for p_i in range(64):
            pp_pieces.append((gw[128 * p_i:128 * (p_i + 1), :],
                              scr_g[128 * p_i:128 * (p_i + 1), :], False, 0))
            pp_pieces.append((uw[128 * p_i:128 * (p_i + 1), :],
                              scr_u[128 * p_i:128 * (p_i + 1), :], False, 0))
            r, cc_ = p_i // 4, p_i % 4
            pp_pieces.append((dw[128 * r:128 * (r + 1), 2048 * cc_:2048 * (cc_ + 1)],
                              scr_d[128 * r:128 * (r + 1), 2048 * cc_:2048 * (cc_ + 1)],
                              True, r))
        pp_scw = []
        pp_next = [0]

        def emit_pp(n):
            done = 0
            while done < n and pp_next[0] < len(pp_pieces):
                i = pp_next[0]
                pcs = pp_pieces[i:i + 2]
                pp_next[0] += len(pcs)
                done += len(pcs)
                prepass_pair(pcs, pp_pair[0])
                pp_pair[0] += 1
                for _ in pcs:
                    pp_scw.append((SCW[0].n, SCW[1].n))

        tcr_w = [None, None]
        tc_slot = [0]

        def tcopy_call(read_src, write_dst_ap, gate_piece_idx):
            slot = tc_slot[0] % 2
            tc_slot[0] += 1
            g0, g1 = pp_scw[gate_piece_idx]

            def fr(scalar, read_src=read_src, slot=slot, ww=tcr_w[slot],
                   g0=g0, g1=g1):
                scalar.wait_ge(SCW[0].h, g0)
                scalar.wait_ge(SCW[1].h, g1)
                if ww is not None:
                    scalar.wait_ge(TCW[slot].h, ww)
                scalar.dma_start(out=tc_buf[slot][:], in_=read_src,
                                 transpose=True).then_inc(TCR[slot].h, 16)
            blk.scalar(fr)
            TCR[slot].n += 16
            tpt = TCR[slot].n

            def fw(sync, write_dst_ap=write_dst_ap, slot=slot, tpt=tpt):
                sync.wait_ge(TCR[slot].h, tpt)
                sync.dma_start(
                    out=write_dst_ap,
                    in_=tc_buf[slot][:].rearrange("p (a c) -> p a c", c=128),
                ).then_inc(TCW[slot].h, 16)
            blk.sync(fw)
            TCW[slot].n += 16
            tcr_w[slot] = TCW[slot].n

        tc_jobs = []
        for wi, (scr, scrT) in enumerate(((scr_g, scrT_g), (scr_u, scrT_u))):
            for osp in range(4):
                ready = 3 * (16 * osp + 15) + wi

                def job(osp=osp, scr=scr, scrT=scrT, ready=ready):
                    for k in range(KC):
                        tcopy_call(
                            scr[2048 * osp:2048 * (osp + 1),
                                128 * k:128 * (k + 1)],
                            scrT[16 * osp:16 * (osp + 1),
                                 128 * k:128 * (k + 1), :
                                 ].rearrange("a p c -> p a c"),
                            ready)
                tc_jobs.append((ready, job))
        for i128_ in range(ICH):
            ready = 3 * (60 + i128_ // 16) + 2

            def djob(i128=i128_, ready=ready):
                tcopy_call(
                    scr_d[:, 128 * i128:128 * (i128 + 1)],
                    scrT_d[:, 128 * i128:128 * (i128 + 1), :
                           ].rearrange("a p c -> p a c"),
                    ready)
            tc_jobs.append((ready, djob))

        def emit_ready_tcopy():
            rem = []
            for ready, job in tc_jobs:
                if ready < pp_next[0]:
                    job()
                else:
                    rem.append((ready, job))
            tc_jobs[:] = rem

        def emit_down_tcopy():
            emit_ready_tcopy()
            assert not tc_jobs

        def ln_stats(vector, src, slot, waits=()):
            b = 32 * slot
            for w in waits:
                w(vector)
            for c_ in range(4):
                vector.bn_stats(lnstat[:, b + 6 * c_:b + 6 * c_ + 6],
                                src[:, 512 * c_:512 * (c_ + 1)])
            vector.drain()
            vector.bn_aggr(lnstat[:, b + 24:b + 26], lnstat[:, b:b + 24])
            vector.drain()
            return vector.tensor_scalar_add(lnstat[:, b + 26:b + 27],
                                            lnstat[:, b + 25:b + 26], LNEPS)

        def ln_rsig(scalar, slot):
            b = 32 * slot
            return scalar.activation(out=lnstat[:, b + 27:b + 28],
                                     in_=lnstat[:, b + 26:b + 27], func=AF.Sqrt)

        def ln_nmur(vector, slot):
            b = 32 * slot
            vector.reciprocal(lnstat[:, b + 28:b + 29], lnstat[:, b + 27:b + 28])
            vector.drain()
            return vector.scalar_tensor_tensor(lnstat[:, b + 29:b + 30],
                                               lnstat[:, b + 24:b + 25], -1.0,
                                               lnstat[:, b + 28:b + 29],
                                               OP.mult, OP.mult)

        def ln_norm(scalar, src, out, slot):
            b = 32 * slot
            return scalar.activation(out=out, in_=src, func=AF.Identity,
                                     scale=lnstat[:, b + 28:b + 29],
                                     bias=lnstat[:, b + 29:b + 30])

        def ln_chain(src, xc_out, wait_ld=None, wait_ld_sem=None, wait_ev=None,
                     wait_mm=None, slot=0, skip_evall=False, wait_ac=None,
                     fc_wait_ev=None):
            ev_all = EV.n

            def fstats(vector, ev_all=ev_all):
                if not skip_evall:
                    vector.wait_ge(EV.h, ev_all)
                if wait_ac is not None:
                    vector.wait_ge(ACS.h, wait_ac)
                if wait_ld is not None:
                    vector.wait_ge(wait_ld_sem.h, wait_ld)
                if wait_ev is not None:
                    vector.wait_ge(EV.h, wait_ev)
                ln_stats(vector, src, slot).then_inc(EV.h, 1)
            blk.vector(fstats)
            EV.n += 1
            evt = EV.n

            def fsc(scalar, evt=evt):
                scalar.wait_ge(EV.h, evt)
                ln_rsig(scalar, slot).then_inc(ACS.h, 1)
            blk.scalar(fsc)
            ACS.n += 1
            act = ACS.n

            def fnm(vector, act=act):
                vector.wait_ge(ACS.h, act)
                ln_nmur(vector, slot).then_inc(EV.h, 1)
            blk.vector(fnm)
            EV.n += 1
            evt2 = EV.n

            def fc(scalar, evt2=evt2, src=src, xc_out=xc_out, wait_mm=wait_mm):
                scalar.wait_ge(EV.h, evt2)
                if fc_wait_ev is not None:
                    scalar.wait_ge(EV.h, fc_wait_ev)
                if wait_mm is not None:
                    scalar.wait_ge(MM.h, wait_mm)
                ln_norm(scalar, src, xc_out, slot).then_inc(ACS.h, 1)
            blk.scalar(fc)
            ACS.n += 1
            return ACS.n

        def emit_transposes(src_f32, dst_bf_fn, nk, banks=(6, 7), pre_wait=None):
            ngrp = nk // 4
            last_ev = None
            for g_ in range(ngrp):
                bank = banks[g_ % 2]

                def ftr(tensor, g_=g_, bank=bank, pfree=mx_tpev[bank],
                        pre_wait=pre_wait):
                    if g_ == 0 and pre_wait is not None:
                        pre_wait(tensor)
                    if pfree is not None:
                        tensor.wait_ge(EV.h, pfree)
                    for j_ in range(4):
                        k_ = 4 * g_ + j_
                        inst = tensor.transpose(
                            pb[bank][:, 128 * j_:128 * (j_ + 1)],
                            src_f32[:, 128 * k_:128 * (k_ + 1)], eyeF_s[:])
                    inst.then_inc(MM.h, 1)
                blk.tensor(ftr)
                MM.n += 1
                mmt = MM.n

                def fev(vector, g_=g_, bank=bank, mmt=mmt):
                    vector.wait_ge(MM.h, mmt)
                    vector.tensor_copy(dst_bf_fn(g_), pb[bank][:, :]
                                       ).then_inc(EV.h, 1)
                blk.vector(fev)
                EV.n += 1
                mx_tpev[bank] = EV.n
                last_ev = EV.n
            return last_ev

        mx_scr2 = [mx_scr, vF(49408, 2048)]
        mx_free = {}
        mx_lastT = [None, None]
        mx_xsfree = [None]
        mx_xsTmm = [None, None]
        mx_kvmm = [None]
        mx_wld = [None, None]
        mx_psev = [None] * 4
        mx_tpev = {6: None, 7: None}

        for ms in range(NMST):
            emit_pp(24)
            emit_ready_tcopy()
            tiles = [2 * ms, 2 * ms + 1]

            for tloc, t in enumerate(tiles):
                def fso(sync, t=t, tloc=tloc, fr=mx_free.get(("own", tloc))):
                    if fr is not None:
                        sync.wait_ge(ACS.h, fr)
                    sync.dma_start(out=mx_xo[tloc][:],
                                   in_=x_own[128 * t:128 * (t + 1), :]
                                   ).then_inc(MXX[tloc].h, 16)
                blk.sync(fso)
                MXX[tloc].n += 16

                def fst2(sync, t=t, tloc=tloc, fr=mx_xsfree[0]):
                    if fr is not None:
                        sync.wait_ge(ACS.h, fr)
                    sync.dma_start(out=mx_xs[tloc][:],
                                   in_=x_oth[128 * t:128 * (t + 1), :]
                                   ).then_inc(MXO[tloc].h, 16)
                blk.sync(fst2)
                MXO[tloc].n += 16
            LD_OWN = [MXX[0].n, MXX[1].n]
            LD_OTH = [MXO[0].n, MXO[1].n]

            ev_all = EV.n

            def fstats4(vector, ev_all=ev_all, lo=tuple(LD_OWN), lt=tuple(LD_OTH)):
                vector.wait_ge(EV.h, ev_all)
                vector.wait_ge(MXX[0].h, lo[0])
                ln_stats(vector, mx_xo[0][:], 0)
                vector.drain()
                vector.wait_ge(MXX[1].h, lo[1])
                ln_stats(vector, mx_xo[1][:], 1)
                vector.drain()
                vector.wait_ge(MXO[0].h, lt[0])
                ln_stats(vector, mx_xs[0][:], 2)
                vector.drain()
                vector.wait_ge(MXO[1].h, lt[1])
                ln_stats(vector, mx_xs[1][:], 3).then_inc(EV.h, 1)
            blk.vector(fstats4)
            EV.n += 1
            evs = EV.n

            def fsc4(scalar, evs=evs):
                scalar.wait_ge(EV.h, evs)
                for s_ in range(4):
                    inst = ln_rsig(scalar, s_)
                    if s_ < 3:
                        scalar.drain()
                inst.then_inc(ACS.h, 1)
            blk.scalar(fsc4)
            ACS.n += 1
            acs4 = ACS.n

            def fnm4(vector, acs4=acs4):
                vector.wait_ge(ACS.h, acs4)
                for s_ in range(4):
                    inst = ln_nmur(vector, s_)
                inst.then_inc(EV.h, 1)
            blk.vector(fnm4)
            EV.n += 1
            env4 = EV.n

            def fc4(scalar, env4=env4, w0=mx_lastT[0], w1=mx_lastT[1]):
                scalar.wait_ge(EV.h, env4)
                if w0 is not None:
                    scalar.wait_ge(MM.h, max(w0, w1))
                ln_norm(scalar, mx_xo[0][:], mx_scr2[0][:], 0)
                ln_norm(scalar, mx_xo[1][:], mx_scr2[1][:], 1)
                ln_norm(scalar, mx_xs[0][:], mx_xs[0][:], 2)
                inst = ln_norm(scalar, mx_xs[1][:], mx_xs[1][:], 3)
                inst.then_inc(ACS.h, 1)
            blk.scalar(fc4)
            ACS.n += 1
            norm_ac = ACS.n
            mx_free[("own", 0)] = norm_ac
            mx_free[("own", 1)] = norm_ac

            def fvh(vector, norm_ac=norm_ac):
                vector.wait_ge(ACS.h, norm_ac)
                for tloc in range(2):
                    sc = mx_scr2[tloc]
                    vector.tensor_tensor(sc[:], sc[:], A1_bc[:], OP.mult)
                for tloc in range(2):
                    xv = mx_xs[tloc]
                    vector.tensor_tensor(xv[:], xv[:], A1_bc[:], OP.mult)
                vector.drain()
                for tloc in range(2):
                    sc = mx_scr2[tloc]
                    vector.tensor_add(sc[:], sc[:], A0_bc[:])
                for tloc in range(2):
                    xv = mx_xs[tloc]
                    vector.tensor_add(xv[:], xv[:], A0_bc[:])
                vector.drain()
                for tloc in range(2):
                    inst = vector.tensor_tensor(mx_hos[tloc][:], mx_scr2[tloc][:],
                                                omg_bc[:], OP.mult)
                inst.then_inc(EV.h, 1)
            blk.vector(fvh)
            EV.n += 1
            hmod_ev = EV.n

            for tloc in range(2):
                def prew(tensor, hmod_ev=hmod_ev):
                    tensor.wait_ge(EV.h, hmod_ev)
                emit_transposes(
                    mx_xs[tloc][:],
                    lambda g_, tloc=tloc: mxb_hsT[:, tloc, 4 * g_:4 * (g_ + 1), :
                                                  ].rearrange("p a c -> p (a c)"),
                    KC, pre_wait=prew)
                mx_xsTmm[tloc] = MM.n
            HST_EV = EV.n

            for ec in range(8):
                slot = ec % 2

                def fsw(sync, ms=ms, ec=ec, slot=slot, wld=mx_wld[slot]):
                    sync.wait_ge(SCW[0].h, FOLD0)
                    sync.wait_ge(SCW[1].h, FOLD1)
                    if wld is not None:
                        sync.wait_ge(MM.h, wld)
                    src_ = scr_wg.rearrange("(k p) (e f) -> p k e f",
                                            p=128, f=256)[:, :, ec, :]
                    sync.dma_start(out=mxb_w[slot][:, :, 0:256], in_=src_
                                   ).then_inc(WTS[slot].h, 16)
                blk.sync(fsw)
                WTS[slot].n += 16
                wldt = WTS[slot].n

                for tloc in range(2):
                    bank = 2 * (ec % 2) + tloc

                    def ft(tensor, tloc=tloc, ec=ec, slot=slot, bank=bank,
                           wldt=wldt, hst=HST_EV, psev=mx_psev[bank]):
                        tensor.wait_ge(WTS[slot].h, wldt)
                        tensor.wait_ge(EV.h, hst)
                        if psev is not None:
                            tensor.wait_ge(EV.h, psev)
                        for k in range(KC):
                            inst = tensor.matmul(pb[bank][:, 0:256],
                                                 mxb_hsT[:, tloc, k, :],
                                                 mxb_w[slot][:, k, 0:256],
                                                 start=(k == 0),
                                                 stop=(k == KC - 1))
                        inst.then_inc(MM.h, 1)
                    blk.tensor(ft)
                    MM.n += 1
                    if tloc == 1:
                        mx_wld[slot] = MM.n
                    mmt = MM.n

                    def fv(vector, tloc=tloc, ec=ec, bank=bank, mmt=mmt,
                           tmm=mx_xsTmm[tloc]):
                        vector.wait_ge(MM.h, max(mmt, tmm))
                        dst = mx_xs[tloc][:, 256 * ec:256 * (ec + 1)]
                        vector.tensor_add(dst, pb[bank][:, 0:256],
                                          mx_hos[tloc][:, 256 * ec:256 * (ec + 1)]
                                          ).then_inc(EV.h, 1)
                    blk.vector(fv)
                    EV.n += 1
                    mx_psev[bank] = EV.n
            XS_EV = EV.n

            ev_all2 = EV.n

            def fstats2(vector, ev_all2=ev_all2):
                vector.wait_ge(EV.h, ev_all2)
                ln_stats(vector, mx_xs[0][:], 0)
                vector.drain()
                ln_stats(vector, mx_xs[1][:], 1).then_inc(EV.h, 1)
            blk.vector(fstats2)
            EV.n += 1
            evs2 = EV.n

            def fsc2(scalar, evs2=evs2):
                scalar.wait_ge(EV.h, evs2)
                ln_rsig(scalar, 0)
                scalar.drain()
                ln_rsig(scalar, 1).then_inc(ACS.h, 1)
            blk.scalar(fsc2)
            ACS.n += 1
            acs2 = ACS.n

            def fnm2(vector, acs2=acs2):
                vector.wait_ge(ACS.h, acs2)
                ln_nmur(vector, 0)
                ln_nmur(vector, 1).then_inc(EV.h, 1)
            blk.vector(fnm2)
            EV.n += 1
            env2 = EV.n

            def fc2(scalar, env2=env2, w0=mx_lastT[0], w1=mx_lastT[1]):
                scalar.wait_ge(EV.h, env2)
                if w0 is not None:
                    scalar.wait_ge(MM.h, max(w0, w1))
                ln_norm(scalar, mx_xs[0][:], mx_scr2[0][:], 0)
                ln_norm(scalar, mx_xs[1][:], mx_scr2[1][:], 1).then_inc(ACS.h, 1)
            blk.scalar(fc2)
            ACS.n += 1
            xn_ac = ACS.n
            mx_xsfree[0] = xn_ac

            def fvxn(vector, xn_ac=xn_ac):
                vector.wait_ge(ACS.h, xn_ac)
                for tloc in range(2):
                    sc = mx_scr2[tloc]
                    vector.tensor_tensor(sc[:], sc[:], lnw_bc[:], OP.mult)
                vector.drain()
                for tloc in range(2):
                    sc = mx_scr2[tloc]
                    inst = vector.tensor_add(sc[:], sc[:], lnb_bc[:])
                inst.then_inc(EV.h, 1)
            blk.vector(fvxn)
            EV.n += 1
            xn_ev = EV.n

            for tloc, t in enumerate(tiles):
                def prew2(tensor, xn_ev=xn_ev):
                    tensor.wait_ge(EV.h, xn_ev)
                xnt_ev = emit_transposes(
                    mx_scr2[tloc][:],
                    lambda g_: mxb_xnT[:, 4 * g_:4 * (g_ + 1), :
                                       ].rearrange("p a c -> p (a c)"),
                    KC, pre_wait=prew2)
                mx_lastT[tloc] = MM.n

                kvb = 4 + tloc

                def ftkv(tensor, xnt_ev=xnt_ev, kvb=kvb, wk=mx_kvmm[0]):
                    tensor.wait_ge(EV.h, xnt_ev)
                    tensor.wait_ge(GLD.h, 16)
                    for k in range(KC):
                        inst = tensor.matmul(pb[kvb][:, 0:128], mxb_xnT[:, k, :],
                                             WkvTb[:, k, :],
                                             start=(k == 0), stop=(k == KC - 1))
                    inst.then_inc(MM.h, 1)
                blk.tensor(ftkv)
                MM.n += 1
                kv_mm = MM.n
                mx_kvmm[0] = kv_mm

                b = 128 + 64 * tloc
                w1 = lnstat[:, b:b + 64]

                def fsg(scalar, kv_mm=kv_mm, kvb=kvb, w1=w1):
                    scalar.wait_ge(MM.h, kv_mm)
                    scalar.activation(out=w1, in_=pb[kvb][:, 0:S],
                                      func=AF.Exp, scale=-1.0).then_inc(ACS.h, 1)
                blk.scalar(fsg)
                ACS.n += 1
                sg_ac = ACS.n

                def fu(vector, sg_ac=sg_ac, w1=w1, tloc=tloc):
                    vector.wait_ge(ACS.h, sg_ac)
                    vector.tensor_scalar_add(w1, w1, 1.0)
                    vector.drain()
                    vector.reciprocal(mx_w2[:], w1)
                    vector.drain()
                    vector.tensor_tensor(w1, mx_w2[:], etf_bc[:], OP.mult
                                         ).then_inc(EV.h, 1)
                blk.vector(fu)
                EV.n += 1
                u_ev = EV.n

                def fe(scalar, u_ev=u_ev, w1=w1):
                    scalar.wait_ge(EV.h, u_ev)
                    scalar.activation(out=mx_w2[:], in_=w1, func=AF.Exp,
                                      scale=-1.0).then_inc(ACS.h, 1)
                blk.scalar(fe)
                ACS.n += 1
                e_ac = ACS.n

                def fwkv(vector, e_ac=e_ac, w1=w1, kvb=kvb):
                    vector.wait_ge(ACS.h, e_ac)
                    vector.tensor_tensor(w1, mx_w2[:], pb[kvb][:, S:128],
                                         OP.mult).then_inc(EV.h, 1)
                blk.vector(fwkv)
                EV.n += 1
                wkv_ev = EV.n

                trb = 6 + tloc

                def ftr2(tensor, wkv_ev=wkv_ev, w1=w1, trb=trb,
                         pfree=mx_tpev[6 + tloc]):
                    tensor.wait_ge(EV.h, wkv_ev)
                    if pfree is not None:
                        tensor.wait_ge(EV.h, pfree)
                    tensor.transpose(pb[trb][0:S, 0:128], w1[:, 0:S],
                                     eyeF_s[:]).then_inc(MM.h, 1)
                blk.tensor(ftr2)
                MM.n += 1
                tr_mm = MM.n

                def fev2(vector, t=t, tr_mm=tr_mm, trb=trb):
                    vector.wait_ge(MM.h, tr_mm)
                    vector.tensor_copy(wkvT_s[:, 128 * t:128 * (t + 1)],
                                       pb[trb][0:S, 0:128]).then_inc(EV.h, 1)
                blk.vector(fev2)
                EV.n += 1
                mx_tpev[6 + tloc] = EV.n
        emit_pp(len(pp_pieces))
        emit_ready_tcopy()
        assert not tc_jobs and pp_next[0] == len(pp_pieces)

        barrier()

        @blk.vector
        def _(vector):
            vector.tensor_tensor_scan(
                scanbuf[:, :], wcol[:, 0:1].broadcast_to([S, NT]),
                wkvT_s[:, :], 0.0, op0=OP.mult, op1=OP.add).then_inc(EV.h, 1)
        EV.n += 1
        SC1 = EV.n

        @blk.sync
        def _(sync):
            sync.wait_ge(EV.h, SC1)
            sync.dma_start(out=ag_car_src, in_=scanbuf[:, NT - 1:NT]
                           ).then_inc(CAR.h, 16)
        CAR.n += 16

        @blk.gpsimd
        def _(gpsimd):
            gpsimd.wait_ge(CAR.h, CAR.n)
            gpsimd.collective_compute(
                "AllGather", OP.bypass,
                replica_groups=[[0, 1, 2, 3], [4, 5, 6, 7]],
                ins=[ag_car_src], outs=[ag_car_dst],
            ).then_inc(CC.h, 1)
        CC.n += 1

        @blk.sync
        def _(sync):
            sync.wait_ge(CC.h, CC.n)
            sync.dma_start(out=carw[:], in_=ag_car_dst).then_inc(LD_A.h, 16)
        LD_A.n += 16
        CAR_LD = LD_A.n

        @blk.vector
        def _(vector):
            vector.wait_ge(LD_A.h, CAR_LD)
            vector.tensor_tensor(qsc[0:4, 0:64], tdB_s[:], maskE_s[:], OP.mult
                                 ).then_inc(EV.h, 1)
        EV.n += 1

        @blk.scalar
        def _(scalar):
            scalar.wait_ge(EV.h, EV.n)
            scalar.activation(out=qsc[0:4, 64:128], in_=qsc[0:4, 0:64],
                              func=AF.Exp).then_inc(ACS.h, 1)
        ACS.n += 1

        @blk.vector
        def _(vector):
            vector.wait_ge(ACS.h, ACS.n)
            vector.tensor_tensor(qsc[0:4, 0:64], qsc[0:4, 64:128], maskM_s[:],
                                 OP.mult)
            vector.drain()
            vector.tensor_tensor(qsc[0:4, 64:128], qsc[0:4, 0:64], carw[:],
                                 OP.mult).then_inc(EV.h, 1)
        EV.n += 1
        CARW_EV = EV.n

        @blk.tensor
        def _(tensor):
            tensor.wait_ge(EV.h, CARW_EV)
            tensor.matmul(pb[6][0:S, 0:1], qsc[0:4, 64:128], ones_cs[0:4, 0:1],
                          start=True, stop=True).then_inc(MM.h, 1)
        MM.n += 1
        CIN_MM = MM.n

        @blk.vector
        def _(vector):
            vector.wait_ge(MM.h, CIN_MM)
            vector.tensor_copy(cin_col[:], pb[6][0:S, 0:1]).then_inc(EV.h, 1)
            vector.drain()
            vector.tensor_tensor_scan(
                statesB[:, :], wcol[:, 0:1].broadcast_to([S, NT]),
                wkvT_s[:, :], cin_col[:, 0:1], op0=OP.mult, op1=OP.add
            ).then_inc(EV.h, 1)
        EV.n += 2

        b5_mm = [None, None]
        b5_ev = [None, None]

        def bcast_bf(vec_j0, target, add_one):
            for c in range(4):
                j = vec_j0 + 512 * c
                r, off = j // JSH, j % JSH
                slot = c % 2

                def fs(sync, r=r, off=off, slot=slot, mw=b5_mm[slot]):
                    if mw is not None:
                        sync.wait_ge(MM.h, mw)
                    sync.dma_start(out=mst[slot][:],
                                   in_=ag_mod_dst[r:r + 1, off:off + 512]
                                   ).then_inc(MST_S[slot].h, 16)
                blk.sync(fs)
                MST_S[slot].n += 16
                ldt = MST_S[slot].n

                def ft(tensor, slot=slot, ldt=ldt, evw=b5_ev[slot]):
                    tensor.wait_ge(MST_S[slot].h, ldt)
                    if evw is not None:
                        tensor.wait_ge(EV.h, evw)
                    tensor.matmul(pb[slot][:, :], ones_rs[0:1, :], mst[slot][:],
                                  start=True, stop=True).then_inc(MM.h, 1)
                blk.tensor(ft)
                MM.n += 1
                b5_mm[slot] = MM.n
                mmt = MM.n

                def fv(vector, slot=slot, mmt=mmt, c=c, target=target,
                       add_one=add_one):
                    vector.wait_ge(MM.h, mmt)
                    dst = target[:, 512 * c:512 * (c + 1)]
                    if add_one:
                        vector.tensor_scalar_add(dst, pb[slot][:, :], 1.0
                                                 ).then_inc(EV.h, 1)
                    else:
                        vector.tensor_copy(dst, pb[slot][:, :]).then_inc(EV.h, 1)
                blk.vector(fv)
                EV.n += 1
                b5_ev[slot] = EV.n

        bcast_bf(8192, B1b, True)
        bcast_bf(6144, B0b, False)

        barrier()

        emit_down_tcopy()
        TCT0, TCT1 = TCW[0].n, TCW[1].n

        e_guev = [None] * 4
        e_psev = [None] * 8
        e_dnmm = [None, None]
        e_dpsev = [None, None]
        e_ptev = [None]
        e_atev = [None]
        e_xoev = [None]
        d_xofree = [None, None]
        d_woev = [None, None]
        d_h2qtp2 = [None, None]
        d_xcac = [None, None]
        d_h2sc = [None]

        for st in range(NST):
            ST_EV = EV.n
            ST_Y = YST.n
            for tt in range(TPS):
                t = TPS * st + tt
                par = tt % 2
                xoh2_p = d_xoh2 if par == 0 else d_xoh2b
                x2_p = d_x2 if par == 0 else d_x2b
                h2q_p = d_h2q if par == 0 else d_h2qb
                qb = 192 * par

                def fs(sync, t=t, xofree=d_xofree[par], tt=tt, xoh2_p=xoh2_p):
                    sync.wait_ge(EV.h, ST_EV)
                    if tt == 0:
                        sync.wait_ge(YST.h, ST_Y)
                    if xofree is not None:
                        sync.wait_ge(GP.h, xofree)
                    sync.dma_start(out=xoh2_p[:],
                                   in_=x_own[128 * t:128 * (t + 1), :]
                                   ).then_inc(DXO.h, 16)
                blk.sync(fs)
                DXO.n += 16
                xo_ld = DXO.n

                for c in range(4):
                    def ft(tensor, t=t, c=c, woev=d_woev[par], par=par):
                        if c == 0:
                            tensor.wait_ge(EV.h, ST_EV)
                            if woev is not None:
                                tensor.wait_ge(EV.h, woev)
                        tensor.matmul(pb[4 * par + c][:, :],
                                      statesB[:, 128 * t:128 * (t + 1)],
                                      WoTga[:, 512 * c:512 * (c + 1)],
                                      start=True, stop=True).then_inc(MM.h, 1)
                    blk.tensor(ft)
                    MM.n += 1
                WO_MM = MM.n

                def fv(vector, xo_ld=xo_ld, WO_MM=WO_MM, tt=tt, par=par,
                       x2_p=x2_p, xoh2_p=xoh2_p, xcg=d_xcac[par]):
                    if tt == 0:
                        vector.wait_ge(YST.h, ST_Y)
                    if xcg is not None:
                        vector.wait_ge(ACS.h, xcg)
                    vector.wait_ge(DXO.h, xo_ld)
                    vector.wait_ge(MM.h, WO_MM)
                    for c in range(4):
                        vector.tensor_add(x2_p[:, 512 * c:512 * (c + 1)],
                                          pb[4 * par + c][:, :],
                                          xoh2_p[:, 512 * c:512 * (c + 1)])
                    vector.engine_nop().then_inc(EV.h, 1)
                blk.vector(fv)
                EV.n += 1
                X2_EV = EV.n
                d_woev[par] = X2_EV

                xc_ac = ln_chain(x2_p[:], d_scr[:], wait_ev=X2_EV, slot=par,
                                 skip_evall=True, wait_ac=d_xcac[par],
                                 fc_wait_ev=d_h2sc[0])
                d_xcac[par] = xc_ac

                def fv4(vector, xc_ac=xc_ac, h2q_prev=d_h2qtp2[par],
                        xoh2_p=xoh2_p, qb=qb, par=par):
                    vector.wait_ge(ACS.h, xc_ac)
                    if h2q_prev is not None:
                        vector.wait_ge((H2QT if par == 0 else H2QT1).h,
                                       h2q_prev[0])
                        vector.wait_ge((H2QS if par == 0 else H2QS1).h,
                                       h2q_prev[1])
                    vector.tensor_tensor(d_scr[:], d_scr[:], B1b[:], OP.mult)
                    vector.drain()
                    vector.tensor_add(xoh2_p[:], d_scr[:], B0b[:])
                    vector.drain()
                    h3 = xoh2_p[:].rearrange("p (b c) -> p b c", c=32)
                    vector.tensor_reduce(qsc[:, qb:qb + 64], h3, AX.X, OP.max,
                                         apply_absolute_value=True)
                    vector.drain()
                    vector.tensor_scalar(qsc[:, qb + 128:qb + 192],
                                         qsc[:, qb:qb + 64],
                                         2.0 * C448I, 2.0 * EPS30, OP.mult, OP.max)
                    vector.drain()
                    vector.reciprocal(qsc[:, qb + 64:qb + 128],
                                      qsc[:, qb + 128:qb + 192]
                                      ).then_inc(EV.h, 1)
                blk.vector(fv4)
                EV.n += 1
                h2sc_ev = EV.n
                d_h2sc[0] = h2sc_ev

                def fgq4(gpsimd, h2sc_ev=h2sc_ev, par=par, xoh2_p=xoh2_p,
                         h2q_p=h2q_p, qb=qb):
                    gpsimd.wait_ge(EV.h, h2sc_ev)
                    h3g = xoh2_p[:].rearrange("p (b c) -> p b c", c=32)
                    q3 = q8buf[:, 2048 * par:2048 * (par + 1)
                               ].rearrange("p (b c) -> p b c", c=32)
                    i3 = qsc[:, qb + 64:qb + 128].rearrange("p (b o) -> p b o",
                                                            o=1)
                    gpsimd.tensor_tensor(q3, h3g, i3.broadcast_to([128, 64, 32]),
                                         OP.mult)
                    gpsimd.drain()
                    s3 = qsc[:, qb + 128:qb + 192].rearrange("p (b o) -> p b o",
                                                             o=1)
                    d3 = h2q_p[:].rearrange("p (b c) -> p b c", c=32)
                    gpsimd.tensor_tensor(d3, q3, s3.broadcast_to([128, 64, 32]),
                                         OP.mult).then_inc(GP.h, 1)
                blk.gpsimd(fgq4)
                GP.n += 1
                h2q_gp = GP.n
                d_xofree[par] = h2q_gp

                def ftp(eng, tt=tt, h2q_ev=h2q_gp, par2=0, h2q_p=h2q_p,
                        sem=None):
                    eng.wait_ge(GP.h, h2q_ev)
                    for k in range(par2, KC, 2):
                        eng.dma_start(out=h2qT[:, k, tt, :],
                                      in_=h2q_p[:, 128 * k:128 * (k + 1)],
                                      transpose=True).then_inc(sem, 16)
                semT = H2QT if par == 0 else H2QT1
                semS = H2QS if par == 0 else H2QS1
                blk.scalar(lambda scalar, tt=tt, h2q_ev=h2q_gp, h2q_p=h2q_p,
                           semT=semT:
                           ftp(scalar, tt, h2q_ev, 0, h2q_p, semT.h))
                blk.sync(lambda sync, tt=tt, h2q_ev=h2q_gp, h2q_p=h2q_p,
                         semS=semS:
                         ftp(sync, tt, h2q_ev, 1, h2q_p, semS.h))
                semT.n += 8 * 16
                semS.n += 8 * 16
                d_h2qtp2[par] = (semT.n, semS.n)
            H2QT_TP = (H2QT.n, H2QS.n, H2QT1.n, H2QS1.n)

            barrier()

            e_silev = [None]
            e_gpq = [None]
            e_dqev = [None]
            for o in range(ICH):
                gslot = o % 2
                uslot = 2 + o % 2

                def fsg(sync, o=o, gslot=gslot, guev=e_guev[gslot]):
                    sync.wait_ge(TCW[0].h, TCT0)
                    sync.wait_ge(TCW[1].h, TCT1)
                    if guev is not None:
                        sync.wait_ge(EV.h, guev)
                    src = scrT_g[o].rearrange("(k p) c -> p k c", p=128)
                    sync.dma_start(out=e_gu[gslot][:], in_=src
                                   ).then_inc(GU[gslot].h, 16)
                blk.sync(fsg)
                GU[gslot].n += 16
                gld = GU[gslot].n

                def fsu(sync, o=o, uslot=uslot, guev=e_guev[uslot]):
                    sync.wait_ge(TCW[0].h, TCT0)
                    sync.wait_ge(TCW[1].h, TCT1)
                    if guev is not None:
                        sync.wait_ge(EV.h, guev)
                    src = scrT_u[o].rearrange("(k p) c -> p k c", p=128)
                    sync.dma_start(out=e_gu[uslot][:], in_=src
                                   ).then_inc(GU[uslot].h, 16)
                blk.sync(fsu)
                GU[uslot].n += 16
                uld = GU[uslot].n

                bg = 4 + (o % 2)
                bu = 6 + (o % 2)

                def ft(tensor, o=o, gslot=gslot, uslot=uslot, bg=bg, bu=bu,
                       gld=gld, uld=uld, h2qt=H2QT_TP,
                       pg=e_psev[bg], pu=e_psev[bu]):
                    tensor.wait_ge(H2QT.h, h2qt[0])
                    tensor.wait_ge(H2QS.h, h2qt[1])
                    tensor.wait_ge(H2QT1.h, h2qt[2])
                    tensor.wait_ge(H2QS1.h, h2qt[3])
                    tensor.wait_ge(GU[gslot].h, gld)
                    if pg is not None:
                        tensor.wait_ge(EV.h, pg)
                    for k in range(KC):
                        inst = tensor.matmul(
                            pb[bg][:, :], e_gu[gslot][:, k, :],
                            h2qT[:, k, :, :].rearrange("p t c -> p (t c)"),
                            start=(k == 0), stop=(k == KC - 1))
                    inst.then_inc(MM.h, 1)
                    tensor.wait_ge(GU[uslot].h, uld)
                    if pu is not None:
                        tensor.wait_ge(EV.h, pu)
                    for k in range(KC):
                        inst = tensor.matmul(
                            pb[bu][:, :], e_gu[uslot][:, k, :],
                            h2qT[:, k, :, :].rearrange("p t c -> p (t c)"),
                            start=(k == 0), stop=(k == KC - 1))
                    inst.then_inc(MM.h, 1)
                blk.tensor(ft)
                MM.n += 2
                gu_mm = MM.n

                def fsil(scalar, bg=bg, gu_mm=gu_mm, sev=e_silev[0]):
                    scalar.wait_ge(MM.h, gu_mm)
                    if sev is not None:
                        scalar.wait_ge(EV.h, sev)
                    scalar.activation(out=e_sil[:], in_=pb[bg][:, :],
                                      func=AF.Silu).then_inc(ACS.h, 1)
                blk.scalar(fsil)
                ACS.n += 1
                sil_ac = ACS.n

                def fdance1(vector, o=o, bg=bg, bu=bu, sil_ac=sil_ac,
                            gpw=e_gpq[0]):
                    vector.wait_ge(ACS.h, sil_ac)
                    vector.tensor_tensor(e_m[:], e_sil[:], pb[bu][:, :], OP.mult
                                         ).then_inc(EV.h, 1)
                    vector.drain()
                    if gpw is not None:
                        vector.wait_ge(GP.h, gpw)
                    vector.transpose(e_Bv[:], e_m[:])
                    vector.drain()
                    B3 = e_Bv[:].rearrange("p (b c) -> p b c", c=32)
                    vector.tensor_reduce(qsc[:, 0:16], B3, AX.X, OP.max,
                                         apply_absolute_value=True)
                    vector.drain()
                    vector.tensor_scalar(qsc[:, 32:48], qsc[:, 0:16],
                                         2.0 * C448I, 2.0 * EPS30, OP.mult, OP.max)
                    vector.drain()
                    vector.reciprocal(qsc[:, 16:32], qsc[:, 32:48]
                                      ).then_inc(EV.h, 1)
                blk.vector(fdance1)
                EV.n += 2
                e_silev[0] = EV.n - 1
                e_psev[bg] = EV.n - 1
                e_psev[bu] = EV.n - 1
                e_guev[gslot] = EV.n - 1
                e_guev[uslot] = EV.n - 1
                sc_ev = EV.n

                def fgq(gpsimd, o=o, sc_ev=sc_ev, dqw=e_dqev[0]):
                    gpsimd.wait_ge(EV.h, sc_ev)
                    if dqw is not None:
                        gpsimd.wait_ge(EV.h, dqw)
                    B3 = e_Bv[:].rearrange("p (b c) -> p b c", c=32)
                    q3 = q8buf[:, 0:512].rearrange("p (b c) -> p b c", c=32)
                    i3 = qsc[:, 16:32].rearrange("p (b o) -> p b o", o=1)
                    gpsimd.tensor_tensor(q3, B3, i3.broadcast_to([128, 16, 32]),
                                         OP.mult)
                    gpsimd.drain()
                    s3 = qsc[:, 32:48].rearrange("p (b o) -> p b o", o=1)
                    d3 = e_dqB[:].rearrange("p (b c) -> p b c", c=32)
                    gpsimd.tensor_tensor(d3, q3, s3.broadcast_to([128, 16, 32]),
                                         OP.mult).then_inc(GP.h, 1)
                blk.gpsimd(fgq)
                GP.n += 1
                e_gpq[0] = GP.n
                gq_gp = GP.n

                def fdance2(vector, o=o, gq_gp=gq_gp):
                    vector.wait_ge(GP.h, gq_gp)
                    vector.transpose(mqfm[:, o, :], e_dqB[:]).then_inc(EV.h, 1)
                blk.vector(fdance2)
                EV.n += 1
                e_dqev[0] = EV.n

            barrier()

            for dgrp in range(4):
                for dj in range(4):
                    d128 = 4 * dgrp + dj
                    accb = dj % 2
                    for iq in range(4):
                        dslot = iq % 2

                        def fsd(sync, d128=d128, iq=iq, dslot=dslot,
                                dnm=e_dnmm[dslot]):
                            sync.wait_ge(TCW[0].h, TCT0)
                            sync.wait_ge(TCW[1].h, TCT1)
                            if dnm is not None:
                                sync.wait_ge(MM.h, dnm)
                            src = scrT_d[d128, 2048 * iq:2048 * (iq + 1), :]
                            sync.dma_start(
                                out=e_dn[dslot][:],
                                in_=src.rearrange("(k p) c -> p k c", p=128),
                            ).then_inc(DN[dslot].h, 16)
                        blk.sync(fsd)
                        DN[dslot].n += 16
                        dld = DN[dslot].n

                        def ftd(tensor, iq=iq, dslot=dslot, accb=accb, dld=dld,
                                dpe=e_dpsev[accb]):
                            tensor.wait_ge(DN[dslot].h, dld)
                            if iq == 0 and dpe is not None:
                                tensor.wait_ge(EV.h, dpe)
                            for kk in range(KC):
                                isl = 16 * iq + kk
                                inst = tensor.matmul(
                                    pb[accb][:, :], e_dn[dslot][:, kk, :],
                                    mqfm[:, isl, :],
                                    start=(iq == 0 and kk == 0),
                                    stop=(iq == 3 and kk == KC - 1))
                            inst.then_inc(MM.h, 1)
                        blk.tensor(ftd)
                        MM.n += 1
                        e_dnmm[dslot] = MM.n
                    dn_mm = MM.n

                    def fev(vector, accb=accb, dn_mm=dn_mm):
                        vector.wait_ge(MM.h, dn_mm)
                        vector.tensor_copy(e_rot[:], pb[accb][:, :]
                                           ).then_inc(EV.h, 1)
                    blk.vector(fev)
                    EV.n += 1
                    e_dpsev[accb] = EV.n
                    rot_ev = EV.n

                    def ftt(tensor, dj=dj, rot_ev=rot_ev, pte=e_ptev[0]):
                        tensor.wait_ge(EV.h, rot_ev)
                        if dj == 0 and pte is not None:
                            tensor.wait_ge(EV.h, pte)
                        for tt in range(TPS):
                            inst = tensor.transpose(
                                pb[2 + tt][:, 128 * dj:128 * (dj + 1)],
                                e_rot[:, 128 * tt:128 * (tt + 1)], eyeF_s[:])
                        inst.then_inc(MM.h, 1)
                    blk.tensor(ftt)
                    MM.n += 1
                PT_MM = MM.n

                for tt in range(TPS):
                    t = TPS * st + tt

                    def fso(sync, t=t, dgrp=dgrp, xoev=e_xoev[0]):
                        if xoev is not None:
                            sync.wait_ge(EV.h, xoev)
                        sync.dma_start(
                            out=e_sil[:],
                            in_=x_own[128 * t:128 * (t + 1),
                                      512 * dgrp:512 * (dgrp + 1)],
                        ).then_inc(XOD.h, 16)
                    blk.sync(fso)
                    XOD.n += 16
                    xod_ld = XOD.n

                    def fta(tensor, t=t, dgrp=dgrp, atev=e_atev[0]):
                        if atev is not None:
                            tensor.wait_ge(EV.h, atev)
                        tensor.matmul(pb[6][:, :],
                                      statesB[:, 128 * t:128 * (t + 1)],
                                      WoTga[:, 512 * dgrp:512 * (dgrp + 1)],
                                      start=True, stop=True).then_inc(MM.h, 1)
                    blk.tensor(fta)
                    MM.n += 1
                    at_mm = MM.n

                    def ffin(vector, tt=tt, at_mm=at_mm, xod_ld=xod_ld,
                             PT_MM=PT_MM, yprev=YST.n):
                        vector.wait_ge(MM.h, max(at_mm, PT_MM))
                        vector.wait_ge(XOD.h, xod_ld)
                        if yprev:
                            vector.wait_ge(YST.h, yprev)
                        vector.tensor_add(e_x2r[:], pb[6][:, :], e_sil[:]
                                          ).then_inc(EV.h, 1)
                        vector.drain()
                        vector.tensor_add(e_fin[:], pb[2 + tt][:, :], e_x2r[:]
                                          ).then_inc(EV.h, 1)
                    blk.vector(ffin)
                    EV.n += 2
                    e_atev[0] = EV.n - 1
                    e_xoev[0] = EV.n - 1
                    fin_ev = EV.n

                    def fst(sync, t=t, dgrp=dgrp, fin_ev=fin_ev):
                        sync.wait_ge(EV.h, fin_ev)
                        sync.dma_start(
                            out=y[128 * t:128 * (t + 1),
                                  512 * dgrp:512 * (dgrp + 1)],
                            in_=e_fin[:]).then_inc(YST.h, 16)
                    blk.sync(fst)
                    YST.n += 16

                e_ptev[0] = EV.n

            barrier()

        @blk.sync
        def _(sync):
            sync.wait_ge(YST.h, YST.n)
        barrier()

    return nc


_NC_CACHE = {}


def _get_nc():
    if "nc" not in _NC_CACHE:
        _NC_CACHE["nc"] = build_nc()
    return _NC_CACHE["nc"]


def kernel(L0_attn_x, L0_attn_Wk, L0_attn_Wv, L0_attn_Wo, L0_attn_Wts,
           L0_attn_gate, L0_attn_time_decay, L0_attn_time_first,
           L0_attn_ln_w, L0_attn_ln_b, L0_ffn_gate_w, L0_ffn_up_w,
           L0_ffn_down_w, L0_c, L0_wmod):
    f32 = np.float32
    x = np.ascontiguousarray(np.asarray(L0_attn_x, dtype=f32))
    Wk = np.asarray(L0_attn_Wk, dtype=f32)
    Wv = np.asarray(L0_attn_Wv, dtype=f32)
    Wo = np.asarray(L0_attn_Wo, dtype=f32)
    Wts = np.asarray(L0_attn_Wts, dtype=f32)
    gate = np.asarray(L0_attn_gate, dtype=f32)
    td = np.asarray(L0_attn_time_decay, dtype=f32)
    tf = np.asarray(L0_attn_time_first, dtype=f32)
    lnw = np.asarray(L0_attn_ln_w, dtype=f32)
    lnb = np.asarray(L0_attn_ln_b, dtype=f32)
    gwn = np.ascontiguousarray(np.asarray(L0_ffn_gate_w, dtype=f32))
    uwn = np.ascontiguousarray(np.asarray(L0_ffn_up_w, dtype=f32))
    dwn = np.ascontiguousarray(np.asarray(L0_ffn_down_w, dtype=f32))
    c = np.asarray(L0_c, dtype=f32)
    wmod = np.ascontiguousarray(np.asarray(L0_wmod, dtype=f32))

    wtsT = np.ascontiguousarray(Wts.T)
    wkvT = np.ascontiguousarray(np.concatenate([Wk, Wv], 0).T)
    woT = np.ascontiguousarray(Wo.T)
    eyeF = np.eye(128, dtype=f32)
    ones_r = np.ones((1, 128), dtype=f32)
    ones_c = np.ones((128, 1), dtype=f32)
    tfB = np.ascontiguousarray(np.broadcast_to(tf[None, :], (128, S)))
    tdB = np.ascontiguousarray(np.broadcast_to(td[None, :], (4, S)))
    tdc = np.ascontiguousarray(td[:, None])

    in_maps = []
    for core in range(8):
        b, q = core // 4, core % 4
        mM = np.zeros((4, S), dtype=f32)
        mE = np.zeros((4, S), dtype=f32)
        for j in range(4):
            if j < q:
                mM[j, :] = 1.0
                mE[j, :] = 2048.0 * (q - 1 - j)
        cT = np.ascontiguousarray(c[b].reshape(KC, 128).T)
        in_maps.append(dict(
            x_own=np.ascontiguousarray(x[b, 2048 * q:2048 * (q + 1), :]),
            x_oth=np.ascontiguousarray(x[b, 2048 * ((q + 2) % 4):
                                         2048 * ((q + 2) % 4 + 1), :]),
            wtsT=wtsT, wkvT_in=wkvT, woT_in=woT,
            gate_r=gate[None, :], lnw_r=lnw[None, :], lnb_r=lnb[None, :],
            tdc=tdc, tfB=tfB, tdB=tdB, maskM=mM, maskE=mE,
            cT_in=cT,
            wmod_sh=np.ascontiguousarray(wmod[:, JSH * q:JSH * (q + 1)]),
            gw=gwn, uw=uwn, dw=dwn,
            eyeF=eyeF, ones_r=ones_r, ones_c=ones_c,
        ))

    nc = _get_nc()
    res = run_bass_kernel_spmd(nc, in_maps, core_ids=list(range(8)))
    out = np.empty((B, T, D), dtype=np.float32)
    for core in range(8):
        b, q = core // 4, core % 4
        out[b, 2048 * q:2048 * (q + 1), :] = res.results[core]["y"]
    return out


# version 15
# speedup vs baseline: 1.0145x; 1.0145x over previous
import numpy as np
import ml_dtypes
import concourse.bass as bass
import concourse.mybir as mybir
from concourse.bass_utils import run_bass_kernel_spmd

F32 = mybir.dt.float32
BF16 = mybir.dt.bfloat16
FP8 = mybir.dt.float8e4
I32 = mybir.dt.int32
AF = mybir.ActivationFunctionType
OP = mybir.AluOpType
AX = mybir.AxisListType

B, T, D, S = 2, 8192, 2048, 64
NT = 2048
NTILES = 16
KC = 16
I = 8192
ICH = 64
STILE = 512
NST = 4
TPS = 4
NMST = 8

C448I = float(np.float32(1.0) / np.float32(448.0))
EPS30 = 1e-30
LNEPS = 1e-5
RD = float(np.float32(1.0 / 2048.0))
JSH = 3072


class SemT:
    __slots__ = ("h", "n")

    def __init__(self, nc, name):
        self.h = nc.alloc_semaphore(name=name)
        self.n = 0


def build_nc():
    nc = bass.Bass("TRN2", target_bir_lowering=False)

    def di(name, shape, dt=F32):
        return nc.dram_tensor(name, shape, dt, kind="ExternalInput").ap()

    x_own = di("x_own", [NT, D])
    x_oth = di("x_oth", [NT, D])
    wtsT = di("wtsT", [D, D])
    wkvT_in = di("wkvT_in", [D, 128])
    woT_in = di("woT_in", [S, D])
    gate_r = di("gate_r", [1, D])
    lnw_r = di("lnw_r", [1, D])
    lnb_r = di("lnb_r", [1, D])
    tdc = di("tdc", [S, 1])
    tfB = di("tfB", [128, S])
    tdB = di("tdB", [4, S])
    maskM = di("maskM", [4, S])
    maskE = di("maskE", [4, S])
    cT_in = di("cT_in", [128, KC])
    wmod_sh = di("wmod_sh", [D, JSH])
    gw = di("gw", [I, D])
    uw = di("uw", [I, D])
    dw = di("dw", [D, I])
    eyeF = di("eyeF", [128, 128], F32)
    ones_r = di("ones_r", [1, 128])
    ones_c = di("ones_c", [128, 1])

    y = nc.dram_tensor("y", [NT, D], F32, kind="ExternalOutput").ap()

    def dint(name, shape, dt):
        return nc.dram_tensor(name, shape, dt, kind="Internal").ap()

    scr_g = dint("scr_g", [I, D], BF16)
    scr_u = dint("scr_u", [I, D], BF16)
    scr_d = dint("scr_d", [D, I], BF16)
    scr_wg = dint("scr_wg", [D, D], BF16)
    scrT_g = dint("scrT_g", [ICH, D, 128], BF16)
    scrT_u = dint("scrT_u", [ICH, D, 128], BF16)
    scrT_d = dint("scrT_d", [KC, I, 128], BF16)
    ag_mod_src = dint("ag_mod_src", [1, JSH], F32)
    ag_mod_dst = dint("ag_mod_dst", [4, JSH], F32)
    ag_car_src = dint("ag_car_src", [S, 1], F32)
    ag_car_dst = dint("ag_car_dst", [4, S], F32)

    sb = nc.alloc_sbuf_tensor
    eyeF_s = sb("eyeF_s", [128, 128], F32)
    ones_rs = sb("ones_rs", [1, 128], F32)
    ones_cs = sb("ones_cs", [128, 1], F32)
    gbT = sb("gbT", [128, KC], F32)
    WoTga = sb("WoTga", [S, D], BF16)
    WkvTb = sb("WkvTb", [128, KC, 128], BF16)
    cT_s = sb("cT_s", [128, KC], F32)
    scT = sb("scT", [128, KC], BF16)
    etf_bc = sb("etf_bc", [128, S], F32)
    tdc_s = sb("tdc_s", [S, 1], F32)
    wcol = sb("wcol", [S, 1], F32)
    tdB_s = sb("tdB_s", [4, S], F32)
    maskM_s = sb("maskM_s", [4, S], F32)
    maskE_s = sb("maskE_s", [4, S], F32)
    carw = sb("carw", [4, S], F32)
    cin_col = sb("cin_col", [S, 1], F32)
    wkvT_s = sb("wkvT_s", [S, NT], F32)
    statesB = sb("statesB", [S, NT], BF16)
    lnstat = sb("lnstat", [128, 256], F32)
    qsc = sb("qsc", [128, 768], F32)
    q8buf = sb("q8buf", [128, 4 * D], FP8)
    mst = [sb("mst0", [1, 512], F32), sb("mst1", [1, 512], F32)]

    ARB = 84480
    arb = sb("arb", [128, ARB], BF16)

    def vB(off, n, p=128):
        assert off + n <= ARB and off % 2 == 0, (off, n)
        return arb[0:p, off:off + n]

    def vF(off, n, p=128):
        assert off + 2 * n <= ARB and off % 2 == 0, (off, n)
        return arb[0:p, off:off + 2 * n].bitcast(F32)

    pre_w = [vF(68096, 2048), vF(76288, 2048)]
    pre_dq = [vB(72192, 2048), vB(74240, 2048)]
    tc_buf = [vB(80384, 2048), vB(82432, 2048)]

    omg_bc = vF(0, 2048)
    A1_bc = vF(4096, 2048)
    A0_bc = vF(8192, 2048)
    lnw_bc = vF(12288, 2048)
    lnb_bc = vF(16384, 2048)
    mx_xo = [vF(20480, 2048), vF(24576, 2048)]
    mx_hos = [vF(28672, 2048), vF(32768, 2048)]
    mx_xs = [vF(36864, 2048), vF(40960, 2048)]
    mx_scr = vF(45056, 2048)
    mx_w1 = vF(49152, 64)
    mx_w2 = vF(49280, 64)
    mxb_hot = vB(49408, 2048)
    mxb_xn = vB(51456, 2048)
    mxb_hsT = vB(53504, 4096).rearrange("p (t k c) -> p t k c", t=2, k=KC)
    mxb_xnT = vB(57600, 2048).rearrange("p (k c) -> p k c", k=KC)
    mxb_w = [vB(59648, 4096).rearrange("p (k c) -> p k c", k=KC),
             vB(63744, 4096).rearrange("p (k c) -> p k c", k=KC)]
    p0_wm = [vB(20480, 8192).rearrange("p (k c) -> p k c", k=KC),
             vB(36864, 8192).rearrange("p (k c) -> p k c", k=KC)]
    p0_wo = vF(53504, 2048, p=64)
    scanbuf = vF(68096, NT, p=S)
    p0_r = [vF(57600, 2048, p=1), vF(61696, 2048, p=1),
            vF(45056, 2048, p=1), vF(49152, 2048, p=1)]
    p0_gbc = vF(28672, 2048)

    mqfm = vB(0, 32768).rearrange("p (i t) -> p i t", i=ICH)
    h2qT = vB(32768, 8192).rearrange("p (k t c) -> p k t c", k=KC, t=TPS)
    e_gu = [vB(40960 + i * 2048, 2048).rearrange("p (k c) -> p k c", k=KC)
            for i in range(4)]
    e_dn = [vB(49152 + i * 2048, 2048).rearrange("p (k c) -> p k c", k=KC)
            for i in range(2)]
    e_dqB = vB(53248, 512)
    d_h2q = vB(53760, 2048)
    B1b = vB(55808, 2048)
    B0b = vB(57856, 2048)
    d_xoh2 = vF(59904, 2048)
    d_x2 = vF(64000, 2048)
    e_sil = vF(59904, 512)
    e_m = vF(60928, 512)
    e_Bv = vF(61952, 512)
    e_rot = vF(62976, 512)
    e_x2r = vF(64000, 512)
    e_fin = vF(65024, 512)
    d_scr = vF(0, 2048)
    d_h2qb = vB(68096, 2048)
    d_xoh2b = vF(70144, 2048)
    d_x2b = vF(74240, 2048)

    pb = [nc.alloc_psum_tensor(f"pb{i}", [128, 512], F32) for i in range(8)]

    MM = SemT(nc, "MM")
    EV = SemT(nc, "EV")
    ACS = SemT(nc, "ACS")
    GP = SemT(nc, "GP")
    CC = SemT(nc, "CC")
    BAR = SemT(nc, "BAR")

    LD_A = SemT(nc, "LD_A")
    MST_S = [SemT(nc, "MST0"), SemT(nc, "MST1")]
    WM_S = [SemT(nc, "WM0"), SemT(nc, "WM1")]
    MODST = [SemT(nc, "MODST0"), SemT(nc, "MODST1")]
    GLD = SemT(nc, "GLD")
    PRE = [SemT(nc, "PRE0"), SemT(nc, "PRE1")]
    SCW = [SemT(nc, "SCW0"), SemT(nc, "SCW1")]
    TCR = [SemT(nc, "TCR0"), SemT(nc, "TCR1")]
    TCW = [SemT(nc, "TCW0"), SemT(nc, "TCW1")]
    MXX = [SemT(nc, "MXX0"), SemT(nc, "MXX1")]
    MXO = [SemT(nc, "MXO0"), SemT(nc, "MXO1")]
    WTS = [SemT(nc, "WTS0"), SemT(nc, "WTS1")]
    HST = SemT(nc, "HST")
    XNT = SemT(nc, "XNT")
    H2QT = SemT(nc, "H2QT")
    H2QS = SemT(nc, "H2QS")
    H2QT1 = SemT(nc, "H2QT1")
    H2QS1 = SemT(nc, "H2QS1")
    DXO = SemT(nc, "DXO")
    GU = [SemT(nc, f"GU{i}") for i in range(4)]
    DN = [SemT(nc, "DN0"), SemT(nc, "DN1")]
    XOD = SemT(nc, "XOD")
    YST = SemT(nc, "YST")
    CAR = SemT(nc, "CAR")

    with nc.Block() as blk:
        barn = [0]

        def barrier():
            barn[0] += 5
            tgt = barn[0]
            for e in ("sync", "scalar", "vector", "tensor", "gpsimd"):
                def f(eng, tgt=tgt, e=e):
                    if e in ("scalar", "vector", "gpsimd", "tensor"):
                        eng.drain()
                    eng.sem_inc(BAR.h, 1)
                    eng.wait_ge(BAR.h, tgt)
                getattr(blk, e)(f)

        @blk.sync
        def _(sync):
            for dst, src in [(eyeF_s[:], eyeF),
                             (ones_rs[:], ones_r), (ones_cs[:], ones_c),
                             (cT_s[:], cT_in), (tdc_s[:], tdc),
                             (tdB_s[:], tdB), (maskM_s[:], maskM),
                             (maskE_s[:], maskE),
                             (qsc[:, 0:S], tfB),
                             (p0_r[0][:], gate_r),
                             (p0_r[1][:], lnw_r), (p0_r[2][:], lnb_r),
                             (p0_wo[:], woT_in)]:
                sync.dma_start(out=dst, in_=src).then_inc(LD_A.h, 16)
        LD_A.n += 13 * 16
        LD0 = LD_A.n

        @blk.gpsimd
        def _(gpsimd):
            gpsimd.dma_start(
                out=WkvTb[:],
                in_=wkvT_in.rearrange("(k p) c -> p k c", p=128),
            ).then_inc(GLD.h, 16)
        GLD.n += 16

        @blk.scalar
        def _(scalar):
            scalar.wait_ge(LD_A.h, LD0)
            scalar.activation(out=scT[:], in_=cT_s[:], func=AF.Silu).then_inc(ACS.h, 1)
            scalar.activation(out=p0_r[3][:], in_=p0_r[0][:], func=AF.Sigmoid)
            scalar.activation(out=etf_bc[:], in_=qsc[:, 0:S], func=AF.Exp)
            scalar.activation(out=wcol[:], in_=tdc_s[:], func=AF.Exp).then_inc(ACS.h, 1)
        ACS.n += 2

        @blk.vector
        def _(vector):
            vector.wait_ge(ACS.h, ACS.n)
            vector.tensor_scalar(out=p0_r[0][:], in0=p0_r[3][:],
                                 scalar1=-1.0, scalar2=1.0,
                                 op0=OP.mult, op1=OP.add).then_inc(EV.h, 1)
        EV.n += 1

        MOD_MM0 = MM.n
        mod_ev = [None, None]
        for n in range(6):
            slot = n % 2

            def fg(gpsimd, n=n, slot=slot):
                if n >= 2:
                    gpsimd.wait_ge(MM.h, MOD_MM0 + n - 1)
                gpsimd.dma_start(
                    out=p0_wm[slot][:],
                    in_=wmod_sh.rearrange("(k p) (n f) -> p k n f",
                                          p=128, f=512)[:, :, n, :],
                ).then_inc(WM_S[slot].h, 16)
            blk.gpsimd(fg)
            WM_S[slot].n += 16
            wmt = WM_S[slot].n

            def ft(tensor, n=n, slot=slot, wmt=wmt, evw=mod_ev[slot]):
                if n == 0:
                    tensor.wait_ge(ACS.h, 1)
                if evw is not None:
                    tensor.wait_ge(EV.h, evw)
                tensor.wait_ge(WM_S[slot].h, wmt)
                for k in range(KC):
                    inst = tensor.matmul(pb[slot][0:1, :], scT[:, k:k + 1],
                                         p0_wm[slot][:, k, :],
                                         start=(k == 0), stop=(k == KC - 1))
                inst.then_inc(MM.h, 1)
            blk.tensor(ft)
            MM.n += 1
            mmt = MM.n

            def fv(vector, n=n, slot=slot, mmt=mmt, stw=MODST[slot].n):
                vector.wait_ge(MM.h, mmt)
                if n >= 2:
                    vector.wait_ge(MODST[slot].h, stw)
                vector.tensor_copy(mst[slot][:], pb[slot][0:1, :]).then_inc(EV.h, 1)
            blk.vector(fv)
            EV.n += 1
            evt = EV.n

            def fst(sync, n=n, slot=slot, evt=evt):
                sync.wait_ge(EV.h, evt)
                sync.dma_start(out=ag_mod_src[0:1, 512 * n:512 * (n + 1)],
                               in_=mst[slot][:]).then_inc(MODST[slot].h, 16)
            blk.sync(fst)
            MODST[slot].n += 16
            mod_ev[slot] = EV.n

        @blk.gpsimd
        def _(gpsimd):
            gpsimd.wait_ge(MODST[0].h, MODST[0].n)
            gpsimd.wait_ge(MODST[1].h, MODST[1].n)
            gpsimd.collective_compute(
                "AllGather", OP.bypass,
                replica_groups=[[0, 1, 2, 3], [4, 5, 6, 7]],
                ins=[ag_mod_src], outs=[ag_mod_dst],
            ).then_inc(CC.h, 1)
        CC.n += 1

        mst_mm = [None, None]
        mst_ev = [None, None]

        def bcast_dram(vec_j0, target, add_one, nparts=128, banks=(2, 3)):
            for c in range(4):
                j = vec_j0 + 512 * c
                r, off = j // JSH, j % JSH
                slot = c % 2

                def fs(sync, r=r, off=off, slot=slot):
                    sync.wait_ge(CC.h, CC.n)
                    if mst_mm[slot] is not None:
                        sync.wait_ge(MM.h, mst_mm[slot])
                    sync.dma_start(out=mst[slot][:],
                                   in_=ag_mod_dst[r:r + 1, off:off + 512]
                                   ).then_inc(MST_S[slot].h, 16)
                blk.sync(fs)
                MST_S[slot].n += 16
                ldt = MST_S[slot].n

                def ft(tensor, slot=slot, ldt=ldt, nparts=nparts, banks=banks):
                    tensor.wait_ge(MST_S[slot].h, ldt)
                    if mst_ev[slot] is not None:
                        tensor.wait_ge(EV.h, mst_ev[slot])
                    tensor.matmul(pb[banks[slot]][0:nparts, :],
                                  ones_rs[0:1, 0:nparts],
                                  mst[slot][:], start=True, stop=True
                                  ).then_inc(MM.h, 1)
                blk.tensor(ft)
                MM.n += 1
                mst_mm[slot] = MM.n
                mmt = MM.n

                def fv(vector, slot=slot, mmt=mmt, c=c, banks=banks,
                       nparts=nparts, target=target, add_one=add_one):
                    vector.wait_ge(MM.h, mmt)
                    dst = target[:, 512 * c:512 * (c + 1)]
                    if add_one:
                        vector.tensor_scalar_add(dst, pb[banks[slot]][0:nparts, :],
                                                 1.0).then_inc(EV.h, 1)
                    else:
                        vector.tensor_copy(dst, pb[banks[slot]][0:nparts, :]
                                           ).then_inc(EV.h, 1)
                blk.vector(fv)
                EV.n += 1
                mst_ev[slot] = EV.n

        bcast_dram(0, A0_bc, False)
        bcast_dram(2048, A1_bc, True)

        row_free = [None, None]

        def bcast_row(row_ap, target):
            for c in range(4):
                slot = c % 2

                def ft(tensor, c=c, slot=slot, row_ap=row_ap):
                    tensor.wait_ge(EV.h, 1)
                    if row_free[slot] is not None:
                        tensor.wait_ge(EV.h, row_free[slot])
                    tensor.matmul(pb[4 + slot][:, :], ones_rs[0:1, :],
                                  row_ap[0:1, 512 * c:512 * (c + 1)],
                                  start=True, stop=True).then_inc(MM.h, 1)
                blk.tensor(ft)
                MM.n += 1
                mmt = MM.n

                def fv(vector, c=c, slot=slot, mmt=mmt, target=target):
                    vector.wait_ge(MM.h, mmt)
                    vector.tensor_copy(target[:, 512 * c:512 * (c + 1)],
                                       pb[4 + slot][:, :]).then_inc(EV.h, 1)
                blk.vector(fv)
                EV.n += 1
                row_free[slot] = EV.n

        bcast_row(p0_r[0], omg_bc)
        bcast_row(p0_r[1], lnw_bc)
        bcast_row(p0_r[2], lnb_bc)
        bcast_row(p0_r[3], p0_gbc)
        ROWS_DONE_EV = EV.n

        @blk.sync
        def _(sync):
            sync.wait_ge(EV.h, ROWS_DONE_EV)
            sync.wait_ge(CC.h, CC.n)
            sync.dma_start(out=p0_r[0][:],
                           in_=ag_mod_dst[1:2, 1024:3072]).then_inc(LD_A.h, 16)
        LD_A.n += 16
        GA_LD = LD_A.n

        @blk.vector
        def _(vector):
            vector.wait_ge(LD_A.h, GA_LD)
            vector.tensor_scalar_max(p0_r[1][:], p0_r[0][:], 0.25)
            vector.tensor_scalar_min(p0_r[2][:], p0_r[0][:], -0.25)
            vector.tensor_single_scalar(p0_r[3][:], p0_r[0][:], 0.0, OP.is_ge)
            vector.drain()
            vector.tensor_sub(p0_r[0][:], p0_r[1][:], p0_r[2][:])
            vector.drain()
            vector.tensor_tensor(p0_r[0][:], p0_r[3][:], p0_r[0][:], OP.mult)
            vector.drain()
            vector.tensor_add(p0_r[0][:], p0_r[2][:], p0_r[0][:]
                              ).then_inc(EV.h, 1)
        EV.n += 1
        GA_EV = EV.n

        ga_free = [None, None]
        for c in range(4):
            slot = c % 2

            def ft(tensor, c=c, slot=slot):
                tensor.wait_ge(EV.h, GA_EV)
                if ga_free[slot] is not None:
                    tensor.wait_ge(EV.h, ga_free[slot])
                tensor.matmul(pb[6 + slot][0:S, :], ones_rs[0:1, 0:S],
                              p0_r[0][0:1, 512 * c:512 * (c + 1)],
                              start=True, stop=True).then_inc(MM.h, 1)
            blk.tensor(ft)
            MM.n += 1
            mmt = MM.n

            def fv(vector, c=c, slot=slot, mmt=mmt):
                vector.wait_ge(MM.h, mmt)
                vector.tensor_tensor(WoTga[:, 512 * c:512 * (c + 1)],
                                     p0_wo[:, 512 * c:512 * (c + 1)],
                                     pb[6 + slot][0:S, :], OP.mult
                                     ).then_inc(EV.h, 1)
            blk.vector(fv)
            EV.n += 1
            ga_free[slot] = EV.n

        @blk.sync
        def _(sync):
            sync.wait_ge(EV.h, EV.n)
            sync.dma_start(out=p0_r[1][:],
                           in_=ag_mod_dst[3:4, 1024:3072]).then_inc(LD_A.h, 16)
        LD_A.n += 16
        GB_LD = LD_A.n

        @blk.vector
        def _(vector):
            vector.wait_ge(LD_A.h, GB_LD)
            vector.tensor_scalar_max(p0_r[2][:], p0_r[1][:], 0.25)
            vector.tensor_scalar_min(p0_r[3][:], p0_r[1][:], -0.25)
            vector.tensor_single_scalar(p0_r[0][:], p0_r[1][:], 0.0, OP.is_ge)
            vector.drain()
            vector.tensor_sub(p0_r[1][:], p0_r[2][:], p0_r[3][:])
            vector.drain()
            vector.tensor_tensor(p0_r[1][:], p0_r[0][:], p0_r[1][:], OP.mult)
            vector.drain()
            vector.tensor_add(p0_r[1][:], p0_r[3][:], p0_r[1][:]
                              ).then_inc(EV.h, 1)
        EV.n += 1
        GB_EV = EV.n

        @blk.tensor
        def _(tensor):
            tensor.wait_ge(EV.h, GB_EV)
            for k in range(KC):
                inst = tensor.transpose(pb[0][:, k:k + 1],
                                        p0_r[1][0:1, 128 * k:128 * (k + 1)],
                                        eyeF_s[0:1, 0:1])
            inst.then_inc(MM.h, 1)
        MM.n += 1
        GBT_MM = MM.n

        @blk.vector
        def _(vector):
            vector.wait_ge(MM.h, GBT_MM)
            vector.tensor_copy(gbT[:], pb[0][:, 0:KC]).then_inc(EV.h, 1)
        EV.n += 1

        barrier()

        p1_ev = [None, None]
        p1_st = [None, None]
        for i in range(KC):
            slot = i % 2

            def fs(sync, i=i, slot=slot):
                if p1_ev[slot] is not None:
                    sync.wait_ge(EV.h, p1_ev[slot])
                sync.dma_start(out=pre_w[slot][:],
                               in_=wtsT[128 * i:128 * (i + 1), :]
                               ).then_inc(PRE[slot].h, 16)
            blk.sync(fs)
            PRE[slot].n += 16
            ldt = PRE[slot].n

            def fv(vector, i=i, slot=slot, ldt=ldt, stw=p1_st[slot]):
                vector.wait_ge(PRE[slot].h, ldt)
                if stw is not None:
                    vector.wait_ge(SCW[slot].h, stw)
                vector.tensor_tensor(pre_dq[slot][:], pre_w[slot][:],
                                     p0_gbc[:], OP.mult).then_inc(EV.h, 1)
            blk.vector(fv)
            EV.n += 1
            p1_ev[slot] = EV.n
            evt = EV.n

            def fw(sync, i=i, slot=slot, evt=evt):
                sync.wait_ge(EV.h, evt)
                sync.dma_start(out=scr_wg[128 * i:128 * (i + 1), :],
                               in_=pre_dq[slot][:]).then_inc(SCW[slot].h, 16)
            blk.sync(fw)
            SCW[slot].n += 16
            p1_st[slot] = SCW[slot].n
        FOLD0, FOLD1 = SCW[0].n, SCW[1].n

        pp_q = [max(x for x in (p1_ev[0], p1_ev[1]) if x is not None)]
        pp_w = [p1_st[0], p1_st[1]]
        pp_g1 = [None]
        pp_g2 = [None]
        pp_pair = [0]

        def prepass_pair(pieces, pair_idx):
            lds = []
            for slot, (src_ap, dst_ap, down, r) in enumerate(pieces):
                def fs(sync, src_ap=src_ap, slot=slot, qw=pp_q[0]):
                    if qw is not None:
                        sync.wait_ge(EV.h, qw)
                    sync.dma_start(out=pre_w[slot][:], in_=src_ap
                                   ).then_inc(PRE[slot].h, 16)
                blk.sync(fs)
                PRE[slot].n += 16
                lds.append(PRE[slot].n)

            qslot = pair_idx % 2

            def fv(vector, pieces=pieces, lds=tuple(lds), qslot=qslot,
                   gw_=pp_g2[0]):
                for slot in range(len(pieces)):
                    vector.wait_ge(PRE[slot].h, lds[slot])
                for slot, (_, _, down, r) in enumerate(pieces):
                    b = 384 * slot
                    w3 = pre_w[slot][:].rearrange("p (b c) -> p b c", c=32)
                    vector.tensor_reduce(qsc[:, b:b + 64], w3, AX.X, OP.max,
                                         apply_absolute_value=True)
                vector.drain()
                if gw_ is not None:
                    vector.wait_ge(GP.h, gw_)
                for slot, (_, _, down, r) in enumerate(pieces):
                    b = 384 * slot
                    sc = b + 128 + 64 * qslot
                    vector.tensor_scalar(qsc[:, sc:sc + 64], qsc[:, b:b + 64],
                                         2.0 * C448I, 2.0 * EPS30, OP.mult, OP.max)
                vector.drain()
                for slot, (_, _, down, r) in enumerate(pieces):
                    b = 384 * slot
                    sc = b + 128 + 64 * qslot
                    vector.reciprocal(qsc[:, b + 64:b + 128], qsc[:, sc:sc + 64])
                vector.drain()
                any_down = any(p[2] for p in pieces)
                if any_down:
                    for slot, (_, _, down, r) in enumerate(pieces):
                        if down:
                            b = 384 * slot
                            sc = b + 128 + 64 * qslot
                            dc = b + 256 + 64 * qslot
                            vector.tensor_scalar_mul(qsc[:, dc:dc + 64],
                                                     qsc[:, sc:sc + 64],
                                                     gbT[:, r:r + 1])
                    vector.drain()
                for slot in range(len(pieces)):
                    b = 384 * slot
                    w3 = pre_w[slot][:].rearrange("p (b c) -> p b c", c=32)
                    qo = 4096 * qslot + 2048 * slot
                    q3 = q8buf[:, qo:qo + 2048].rearrange("p (b c) -> p b c", c=32)
                    i3 = qsc[:, b + 64:b + 128].rearrange("p (b o) -> p b o", o=1)
                    inst = vector.tensor_tensor(q3, w3,
                                                i3.broadcast_to([128, 64, 32]),
                                                OP.mult)
                inst.then_inc(EV.h, 1)
            blk.vector(fv)
            EV.n += 1
            pp_q[0] = EV.n
            evt = EV.n

            def fg(gpsimd, pieces=pieces, evt=evt, qslot=qslot,
                   ww=(pp_w[0], pp_w[1])):
                gpsimd.wait_ge(EV.h, evt)
                for slot in range(len(pieces)):
                    if ww[slot] is not None:
                        gpsimd.wait_ge(SCW[slot].h, ww[slot])
                for slot, (_, _, down, r) in enumerate(pieces):
                    b = 384 * slot
                    sc = b + 128 + 64 * qslot
                    dc = b + 256 + 64 * qslot
                    qo = 4096 * qslot + 2048 * slot
                    q3 = q8buf[:, qo:qo + 2048].rearrange("p (b c) -> p b c", c=32)
                    scol = qsc[:, dc:dc + 64] if down else qsc[:, sc:sc + 64]
                    s3 = scol.rearrange("p (b o) -> p b o", o=1)
                    d3 = pre_dq[slot][:].rearrange("p (b c) -> p b c", c=32)
                    gpsimd.tensor_tensor(d3, q3,
                                         s3.broadcast_to([128, 64, 32]),
                                         OP.mult).then_inc(GP.h, 1)
            blk.gpsimd(fg)
            gpt_s = []
            for _ in pieces:
                GP.n += 1
                gpt_s.append(GP.n)
            pp_g2[0] = pp_g1[0]
            pp_g1[0] = GP.n

            for slot, (_, dst_ap, down, r) in enumerate(pieces):
                def fw(sync, dst_ap=dst_ap, slot=slot, gpt=gpt_s[slot]):
                    sync.wait_ge(GP.h, gpt)
                    sync.dma_start(out=dst_ap, in_=pre_dq[slot][:]
                                   ).then_inc(SCW[slot].h, 16)
                blk.sync(fw)
                SCW[slot].n += 16
                pp_w[slot] = SCW[slot].n

        pp_pieces = []
        for p_i in range(64):
            pp_pieces.append((gw[128 * p_i:128 * (p_i + 1), :],
                              scr_g[128 * p_i:128 * (p_i + 1), :], False, 0))
            pp_pieces.append((uw[128 * p_i:128 * (p_i + 1), :],
                              scr_u[128 * p_i:128 * (p_i + 1), :], False, 0))
            r, cc_ = p_i // 4, p_i % 4
            pp_pieces.append((dw[128 * r:128 * (r + 1), 2048 * cc_:2048 * (cc_ + 1)],
                              scr_d[128 * r:128 * (r + 1), 2048 * cc_:2048 * (cc_ + 1)],
                              True, r))
        pp_scw = []
        pp_next = [0]

        def emit_pp(n):
            done = 0
            while done < n and pp_next[0] < len(pp_pieces):
                i = pp_next[0]
                pcs = pp_pieces[i:i + 2]
                pp_next[0] += len(pcs)
                done += len(pcs)
                prepass_pair(pcs, pp_pair[0])
                pp_pair[0] += 1
                for _ in pcs:
                    pp_scw.append((SCW[0].n, SCW[1].n))

        tcr_w = [None, None]
        tc_slot = [0]

        def tcopy_call(read_src, write_dst_ap, gate_piece_idx):
            slot = tc_slot[0] % 2
            tc_slot[0] += 1
            g0, g1 = pp_scw[gate_piece_idx]

            def fr(scalar, read_src=read_src, slot=slot, ww=tcr_w[slot],
                   g0=g0, g1=g1):
                scalar.wait_ge(SCW[0].h, g0)
                scalar.wait_ge(SCW[1].h, g1)
                if ww is not None:
                    scalar.wait_ge(TCW[slot].h, ww)
                scalar.dma_start(out=tc_buf[slot][:], in_=read_src,
                                 transpose=True).then_inc(TCR[slot].h, 16)
            blk.scalar(fr)
            TCR[slot].n += 16
            tpt = TCR[slot].n

            def fw(sync, write_dst_ap=write_dst_ap, slot=slot, tpt=tpt):
                sync.wait_ge(TCR[slot].h, tpt)
                sync.dma_start(
                    out=write_dst_ap,
                    in_=tc_buf[slot][:].rearrange("p (a c) -> p a c", c=128),
                ).then_inc(TCW[slot].h, 16)
            blk.sync(fw)
            TCW[slot].n += 16
            tcr_w[slot] = TCW[slot].n

        tc_jobs = []
        for wi, (scr, scrT) in enumerate(((scr_g, scrT_g), (scr_u, scrT_u))):
            for osp in range(4):
                ready = 3 * (16 * osp + 15) + wi

                def job(osp=osp, scr=scr, scrT=scrT, ready=ready):
                    for k in range(KC):
                        tcopy_call(
                            scr[2048 * osp:2048 * (osp + 1),
                                128 * k:128 * (k + 1)],
                            scrT[16 * osp:16 * (osp + 1),
                                 128 * k:128 * (k + 1), :
                                 ].rearrange("a p c -> p a c"),
                            ready)
                tc_jobs.append((ready, job))
        for i128_ in range(ICH):
            ready = 3 * (60 + i128_ // 16) + 2

            def djob(i128=i128_, ready=ready):
                tcopy_call(
                    scr_d[:, 128 * i128:128 * (i128 + 1)],
                    scrT_d[:, 128 * i128:128 * (i128 + 1), :
                           ].rearrange("a p c -> p a c"),
                    ready)
            tc_jobs.append((ready, djob))

        def emit_ready_tcopy():
            rem = []
            for ready, job in tc_jobs:
                if ready < pp_next[0]:
                    job()
                else:
                    rem.append((ready, job))
            tc_jobs[:] = rem

        def emit_down_tcopy():
            emit_ready_tcopy()
            assert not tc_jobs

        def ln_stats(vector, src, slot, waits=()):
            b = 32 * slot
            for w in waits:
                w(vector)
            for c_ in range(4):
                vector.bn_stats(lnstat[:, b + 6 * c_:b + 6 * c_ + 6],
                                src[:, 512 * c_:512 * (c_ + 1)])
            vector.drain()
            vector.bn_aggr(lnstat[:, b + 24:b + 26], lnstat[:, b:b + 24])
            vector.drain()
            return vector.tensor_scalar_add(lnstat[:, b + 26:b + 27],
                                            lnstat[:, b + 25:b + 26], LNEPS)

        def ln_rsig(scalar, slot):
            b = 32 * slot
            return scalar.activation(out=lnstat[:, b + 27:b + 28],
                                     in_=lnstat[:, b + 26:b + 27], func=AF.Sqrt)

        def ln_nmur(vector, slot):
            b = 32 * slot
            vector.reciprocal(lnstat[:, b + 28:b + 29], lnstat[:, b + 27:b + 28])
            vector.drain()
            return vector.scalar_tensor_tensor(lnstat[:, b + 29:b + 30],
                                               lnstat[:, b + 24:b + 25], -1.0,
                                               lnstat[:, b + 28:b + 29],
                                               OP.mult, OP.mult)

        def ln_norm(scalar, src, out, slot):
            b = 32 * slot
            return scalar.activation(out=out, in_=src, func=AF.Identity,
                                     scale=lnstat[:, b + 28:b + 29],
                                     bias=lnstat[:, b + 29:b + 30])

        def ln_chain(src, xc_out, wait_ld=None, wait_ld_sem=None, wait_ev=None,
                     wait_mm=None, slot=0, skip_evall=False, wait_ac=None,
                     fc_wait_ev=None):
            ev_all = EV.n

            def fstats(vector, ev_all=ev_all):
                if not skip_evall:
                    vector.wait_ge(EV.h, ev_all)
                if wait_ac is not None:
                    vector.wait_ge(ACS.h, wait_ac)
                if wait_ld is not None:
                    vector.wait_ge(wait_ld_sem.h, wait_ld)
                if wait_ev is not None:
                    vector.wait_ge(EV.h, wait_ev)
                ln_stats(vector, src, slot).then_inc(EV.h, 1)
            blk.vector(fstats)
            EV.n += 1
            evt = EV.n

            def fsc(scalar, evt=evt):
                scalar.wait_ge(EV.h, evt)
                ln_rsig(scalar, slot).then_inc(ACS.h, 1)
            blk.scalar(fsc)
            ACS.n += 1
            act = ACS.n

            def fnm(vector, act=act):
                vector.wait_ge(ACS.h, act)
                ln_nmur(vector, slot).then_inc(EV.h, 1)
            blk.vector(fnm)
            EV.n += 1
            evt2 = EV.n

            def fc(scalar, evt2=evt2, src=src, xc_out=xc_out, wait_mm=wait_mm):
                scalar.wait_ge(EV.h, evt2)
                if fc_wait_ev is not None:
                    scalar.wait_ge(EV.h, fc_wait_ev)
                if wait_mm is not None:
                    scalar.wait_ge(MM.h, wait_mm)
                ln_norm(scalar, src, xc_out, slot).then_inc(ACS.h, 1)
            blk.scalar(fc)
            ACS.n += 1
            return ACS.n

        def emit_transposes(src_f32, dst_bf_fn, nk, banks=(6, 7), pre_wait=None):
            ngrp = nk // 4
            last_ev = None
            for g_ in range(ngrp):
                bank = banks[g_ % 2]

                def ftr(tensor, g_=g_, bank=bank, pfree=mx_tpev[bank],
                        pre_wait=pre_wait):
                    if g_ == 0 and pre_wait is not None:
                        pre_wait(tensor)
                    if pfree is not None:
                        tensor.wait_ge(EV.h, pfree)
                    for j_ in range(4):
                        k_ = 4 * g_ + j_
                        inst = tensor.transpose(
                            pb[bank][:, 128 * j_:128 * (j_ + 1)],
                            src_f32[:, 128 * k_:128 * (k_ + 1)], eyeF_s[:])
                    inst.then_inc(MM.h, 1)
                blk.tensor(ftr)
                MM.n += 1
                mmt = MM.n

                def fev(vector, g_=g_, bank=bank, mmt=mmt):
                    vector.wait_ge(MM.h, mmt)
                    vector.tensor_copy(dst_bf_fn(g_), pb[bank][:, :]
                                       ).then_inc(EV.h, 1)
                blk.vector(fev)
                EV.n += 1
                mx_tpev[bank] = EV.n
                last_ev = EV.n
            return last_ev

        mx_scr2 = [mx_scr, vF(49408, 2048)]
        mx_free = {}
        mx_lastT = [None, None]
        mx_xsfree = [None]
        mx_xsTmm = [None, None]
        mx_kvmm = [None]
        mx_wld = [None, None]
        mx_psev = [None] * 4
        mx_tpev = {6: None, 7: None}

        for ms in range(NMST):
            tiles = [2 * ms, 2 * ms + 1]

            for tloc, t in enumerate(tiles):
                def fso(sync, t=t, tloc=tloc, fr=mx_free.get(("own", tloc))):
                    if fr is not None:
                        sync.wait_ge(ACS.h, fr)
                    sync.dma_start(out=mx_xo[tloc][:],
                                   in_=x_own[128 * t:128 * (t + 1), :]
                                   ).then_inc(MXX[tloc].h, 16)
                blk.sync(fso)
                MXX[tloc].n += 16

                def fst2(sync, t=t, tloc=tloc, fr=mx_xsfree[0]):
                    if fr is not None:
                        sync.wait_ge(ACS.h, fr)
                    sync.dma_start(out=mx_xs[tloc][:],
                                   in_=x_oth[128 * t:128 * (t + 1), :]
                                   ).then_inc(MXO[tloc].h, 16)
                blk.sync(fst2)
                MXO[tloc].n += 16
            LD_OWN = [MXX[0].n, MXX[1].n]
            LD_OTH = [MXO[0].n, MXO[1].n]

            emit_pp(24)
            emit_ready_tcopy()

            ev_all = EV.n

            def fstats4(vector, ev_all=ev_all, lo=tuple(LD_OWN), lt=tuple(LD_OTH)):
                vector.wait_ge(EV.h, ev_all)
                vector.wait_ge(MXX[0].h, lo[0])
                ln_stats(vector, mx_xo[0][:], 0)
                vector.drain()
                vector.wait_ge(MXX[1].h, lo[1])
                ln_stats(vector, mx_xo[1][:], 1)
                vector.drain()
                vector.wait_ge(MXO[0].h, lt[0])
                ln_stats(vector, mx_xs[0][:], 2)
                vector.drain()
                vector.wait_ge(MXO[1].h, lt[1])
                ln_stats(vector, mx_xs[1][:], 3).then_inc(EV.h, 1)
            blk.vector(fstats4)
            EV.n += 1
            evs = EV.n

            def fsc4(scalar, evs=evs):
                scalar.wait_ge(EV.h, evs)
                for s_ in range(4):
                    inst = ln_rsig(scalar, s_)
                    if s_ < 3:
                        scalar.drain()
                inst.then_inc(ACS.h, 1)
            blk.scalar(fsc4)
            ACS.n += 1
            acs4 = ACS.n

            def fnm4(vector, acs4=acs4):
                vector.wait_ge(ACS.h, acs4)
                for s_ in range(4):
                    inst = ln_nmur(vector, s_)
                inst.then_inc(EV.h, 1)
            blk.vector(fnm4)
            EV.n += 1
            env4 = EV.n

            def fc4(scalar, env4=env4, w0=mx_lastT[0], w1=mx_lastT[1]):
                scalar.wait_ge(EV.h, env4)
                if w0 is not None:
                    scalar.wait_ge(MM.h, max(w0, w1))
                ln_norm(scalar, mx_xo[0][:], mx_scr2[0][:], 0)
                ln_norm(scalar, mx_xo[1][:], mx_scr2[1][:], 1)
                ln_norm(scalar, mx_xs[0][:], mx_xs[0][:], 2)
                inst = ln_norm(scalar, mx_xs[1][:], mx_xs[1][:], 3)
                inst.then_inc(ACS.h, 1)
            blk.scalar(fc4)
            ACS.n += 1
            norm_ac = ACS.n
            mx_free[("own", 0)] = norm_ac
            mx_free[("own", 1)] = norm_ac

            def fvh(vector, norm_ac=norm_ac):
                vector.wait_ge(ACS.h, norm_ac)
                for tloc in range(2):
                    sc = mx_scr2[tloc]
                    vector.tensor_tensor(sc[:], sc[:], A1_bc[:], OP.mult)
                for tloc in range(2):
                    xv = mx_xs[tloc]
                    vector.tensor_tensor(xv[:], xv[:], A1_bc[:], OP.mult)
                vector.drain()
                for tloc in range(2):
                    sc = mx_scr2[tloc]
                    vector.tensor_add(sc[:], sc[:], A0_bc[:])
                for tloc in range(2):
                    xv = mx_xs[tloc]
                    vector.tensor_add(xv[:], xv[:], A0_bc[:])
                vector.drain()
                for tloc in range(2):
                    inst = vector.tensor_tensor(mx_hos[tloc][:], mx_scr2[tloc][:],
                                                omg_bc[:], OP.mult)
                inst.then_inc(EV.h, 1)
            blk.vector(fvh)
            EV.n += 1
            hmod_ev = EV.n

            for tloc in range(2):
                def prew(tensor, hmod_ev=hmod_ev):
                    tensor.wait_ge(EV.h, hmod_ev)
                emit_transposes(
                    mx_xs[tloc][:],
                    lambda g_, tloc=tloc: mxb_hsT[:, tloc, 4 * g_:4 * (g_ + 1), :
                                                  ].rearrange("p a c -> p (a c)"),
                    KC, pre_wait=prew)
                mx_xsTmm[tloc] = MM.n
            HST_EV = EV.n

            for ec in range(8):
                slot = ec % 2

                def fsw(sync, ms=ms, ec=ec, slot=slot, wld=mx_wld[slot]):
                    sync.wait_ge(SCW[0].h, FOLD0)
                    sync.wait_ge(SCW[1].h, FOLD1)
                    if wld is not None:
                        sync.wait_ge(MM.h, wld)
                    src_ = scr_wg.rearrange("(k p) (e f) -> p k e f",
                                            p=128, f=256)[:, :, ec, :]
                    sync.dma_start(out=mxb_w[slot][:, :, 0:256], in_=src_
                                   ).then_inc(WTS[slot].h, 16)
                blk.sync(fsw)
                WTS[slot].n += 16
                wldt = WTS[slot].n

                for tloc in range(2):
                    bank = 2 * (ec % 2) + tloc

                    def ft(tensor, tloc=tloc, ec=ec, slot=slot, bank=bank,
                           wldt=wldt, hst=HST_EV, psev=mx_psev[bank]):
                        tensor.wait_ge(WTS[slot].h, wldt)
                        tensor.wait_ge(EV.h, hst)
                        if psev is not None:
                            tensor.wait_ge(EV.h, psev)
                        for k in range(KC):
                            inst = tensor.matmul(pb[bank][:, 0:256],
                                                 mxb_hsT[:, tloc, k, :],
                                                 mxb_w[slot][:, k, 0:256],
                                                 start=(k == 0),
                                                 stop=(k == KC - 1))
                        inst.then_inc(MM.h, 1)
                    blk.tensor(ft)
                    MM.n += 1
                    if tloc == 1:
                        mx_wld[slot] = MM.n
                    mmt = MM.n

                    def fv(vector, tloc=tloc, ec=ec, bank=bank, mmt=mmt,
                           tmm=mx_xsTmm[tloc]):
                        vector.wait_ge(MM.h, max(mmt, tmm))
                        dst = mx_xs[tloc][:, 256 * ec:256 * (ec + 1)]
                        vector.tensor_add(dst, pb[bank][:, 0:256],
                                          mx_hos[tloc][:, 256 * ec:256 * (ec + 1)]
                                          ).then_inc(EV.h, 1)
                    blk.vector(fv)
                    EV.n += 1
                    mx_psev[bank] = EV.n
            XS_EV = EV.n

            ev_all2 = EV.n

            def fstats2(vector, ev_all2=ev_all2):
                vector.wait_ge(EV.h, ev_all2)
                ln_stats(vector, mx_xs[0][:], 0)
                vector.drain()
                ln_stats(vector, mx_xs[1][:], 1).then_inc(EV.h, 1)
            blk.vector(fstats2)
            EV.n += 1
            evs2 = EV.n

            def fsc2(scalar, evs2=evs2):
                scalar.wait_ge(EV.h, evs2)
                ln_rsig(scalar, 0)
                scalar.drain()
                ln_rsig(scalar, 1).then_inc(ACS.h, 1)
            blk.scalar(fsc2)
            ACS.n += 1
            acs2 = ACS.n

            def fnm2(vector, acs2=acs2):
                vector.wait_ge(ACS.h, acs2)
                ln_nmur(vector, 0)
                ln_nmur(vector, 1).then_inc(EV.h, 1)
            blk.vector(fnm2)
            EV.n += 1
            env2 = EV.n

            def fc2(scalar, env2=env2, w0=mx_lastT[0], w1=mx_lastT[1]):
                scalar.wait_ge(EV.h, env2)
                if w0 is not None:
                    scalar.wait_ge(MM.h, max(w0, w1))
                ln_norm(scalar, mx_xs[0][:], mx_scr2[0][:], 0)
                ln_norm(scalar, mx_xs[1][:], mx_scr2[1][:], 1).then_inc(ACS.h, 1)
            blk.scalar(fc2)
            ACS.n += 1
            xn_ac = ACS.n
            mx_xsfree[0] = xn_ac

            def fvxn(vector, xn_ac=xn_ac):
                vector.wait_ge(ACS.h, xn_ac)
                for tloc in range(2):
                    sc = mx_scr2[tloc]
                    vector.tensor_tensor(sc[:], sc[:], lnw_bc[:], OP.mult)
                vector.drain()
                for tloc in range(2):
                    sc = mx_scr2[tloc]
                    inst = vector.tensor_add(sc[:], sc[:], lnb_bc[:])
                inst.then_inc(EV.h, 1)
            blk.vector(fvxn)
            EV.n += 1
            xn_ev = EV.n

            for tloc, t in enumerate(tiles):
                def prew2(tensor, xn_ev=xn_ev):
                    tensor.wait_ge(EV.h, xn_ev)
                xnt_ev = emit_transposes(
                    mx_scr2[tloc][:],
                    lambda g_: mxb_xnT[:, 4 * g_:4 * (g_ + 1), :
                                       ].rearrange("p a c -> p (a c)"),
                    KC, pre_wait=prew2)
                mx_lastT[tloc] = MM.n

                kvb = 4 + tloc

                def ftkv(tensor, xnt_ev=xnt_ev, kvb=kvb, wk=mx_kvmm[0]):
                    tensor.wait_ge(EV.h, xnt_ev)
                    tensor.wait_ge(GLD.h, 16)
                    for k in range(KC):
                        inst = tensor.matmul(pb[kvb][:, 0:128], mxb_xnT[:, k, :],
                                             WkvTb[:, k, :],
                                             start=(k == 0), stop=(k == KC - 1))
                    inst.then_inc(MM.h, 1)
                blk.tensor(ftkv)
                MM.n += 1
                kv_mm = MM.n
                mx_kvmm[0] = kv_mm

                b = 128 + 64 * tloc
                w1 = lnstat[:, b:b + 64]

                def fsg(scalar, kv_mm=kv_mm, kvb=kvb, w1=w1):
                    scalar.wait_ge(MM.h, kv_mm)
                    scalar.activation(out=w1, in_=pb[kvb][:, 0:S],
                                      func=AF.Exp, scale=-1.0).then_inc(ACS.h, 1)
                blk.scalar(fsg)
                ACS.n += 1
                sg_ac = ACS.n

                def fu(vector, sg_ac=sg_ac, w1=w1, tloc=tloc):
                    vector.wait_ge(ACS.h, sg_ac)
                    vector.tensor_scalar_add(w1, w1, 1.0)
                    vector.drain()
                    vector.reciprocal(mx_w2[:], w1)
                    vector.drain()
                    vector.tensor_tensor(w1, mx_w2[:], etf_bc[:], OP.mult
                                         ).then_inc(EV.h, 1)
                blk.vector(fu)
                EV.n += 1
                u_ev = EV.n

                def fe(scalar, u_ev=u_ev, w1=w1):
                    scalar.wait_ge(EV.h, u_ev)
                    scalar.activation(out=mx_w2[:], in_=w1, func=AF.Exp,
                                      scale=-1.0).then_inc(ACS.h, 1)
                blk.scalar(fe)
                ACS.n += 1
                e_ac = ACS.n

                def fwkv(vector, e_ac=e_ac, w1=w1, kvb=kvb):
                    vector.wait_ge(ACS.h, e_ac)
                    vector.tensor_tensor(w1, mx_w2[:], pb[kvb][:, S:128],
                                         OP.mult).then_inc(EV.h, 1)
                blk.vector(fwkv)
                EV.n += 1
                wkv_ev = EV.n

                trb = 6 + tloc

                def ftr2(tensor, wkv_ev=wkv_ev, w1=w1, trb=trb,
                         pfree=mx_tpev[6 + tloc]):
                    tensor.wait_ge(EV.h, wkv_ev)
                    if pfree is not None:
                        tensor.wait_ge(EV.h, pfree)
                    tensor.transpose(pb[trb][0:S, 0:128], w1[:, 0:S],
                                     eyeF_s[:]).then_inc(MM.h, 1)
                blk.tensor(ftr2)
                MM.n += 1
                tr_mm = MM.n

                def fev2(vector, t=t, tr_mm=tr_mm, trb=trb):
                    vector.wait_ge(MM.h, tr_mm)
                    vector.tensor_copy(wkvT_s[:, 128 * t:128 * (t + 1)],
                                       pb[trb][0:S, 0:128]).then_inc(EV.h, 1)
                blk.vector(fev2)
                EV.n += 1
                mx_tpev[6 + tloc] = EV.n
        emit_pp(len(pp_pieces))
        emit_ready_tcopy()
        assert not tc_jobs and pp_next[0] == len(pp_pieces)

        barrier()

        @blk.vector
        def _(vector):
            vector.tensor_tensor_scan(
                scanbuf[:, :], wcol[:, 0:1].broadcast_to([S, NT]),
                wkvT_s[:, :], 0.0, op0=OP.mult, op1=OP.add).then_inc(EV.h, 1)
        EV.n += 1
        SC1 = EV.n

        @blk.sync
        def _(sync):
            sync.wait_ge(EV.h, SC1)
            sync.dma_start(out=ag_car_src, in_=scanbuf[:, NT - 1:NT]
                           ).then_inc(CAR.h, 16)
        CAR.n += 16

        @blk.gpsimd
        def _(gpsimd):
            gpsimd.wait_ge(CAR.h, CAR.n)
            gpsimd.collective_compute(
                "AllGather", OP.bypass,
                replica_groups=[[0, 1, 2, 3], [4, 5, 6, 7]],
                ins=[ag_car_src], outs=[ag_car_dst],
            ).then_inc(CC.h, 1)
        CC.n += 1

        @blk.sync
        def _(sync):
            sync.wait_ge(CC.h, CC.n)
            sync.dma_start(out=carw[:], in_=ag_car_dst).then_inc(LD_A.h, 16)
        LD_A.n += 16
        CAR_LD = LD_A.n

        @blk.vector
        def _(vector):
            vector.wait_ge(LD_A.h, CAR_LD)
            vector.tensor_tensor(qsc[0:4, 0:64], tdB_s[:], maskE_s[:], OP.mult
                                 ).then_inc(EV.h, 1)
        EV.n += 1

        @blk.scalar
        def _(scalar):
            scalar.wait_ge(EV.h, EV.n)
            scalar.activation(out=qsc[0:4, 64:128], in_=qsc[0:4, 0:64],
                              func=AF.Exp).then_inc(ACS.h, 1)
        ACS.n += 1

        @blk.vector
        def _(vector):
            vector.wait_ge(ACS.h, ACS.n)
            vector.tensor_tensor(qsc[0:4, 0:64], qsc[0:4, 64:128], maskM_s[:],
                                 OP.mult)
            vector.drain()
            vector.tensor_tensor(qsc[0:4, 64:128], qsc[0:4, 0:64], carw[:],
                                 OP.mult).then_inc(EV.h, 1)
        EV.n += 1
        CARW_EV = EV.n

        @blk.tensor
        def _(tensor):
            tensor.wait_ge(EV.h, CARW_EV)
            tensor.matmul(pb[6][0:S, 0:1], qsc[0:4, 64:128], ones_cs[0:4, 0:1],
                          start=True, stop=True).then_inc(MM.h, 1)
        MM.n += 1
        CIN_MM = MM.n

        @blk.vector
        def _(vector):
            vector.wait_ge(MM.h, CIN_MM)
            vector.tensor_copy(cin_col[:], pb[6][0:S, 0:1]).then_inc(EV.h, 1)
            vector.drain()
            vector.tensor_tensor_scan(
                statesB[:, :], wcol[:, 0:1].broadcast_to([S, NT]),
                wkvT_s[:, :], cin_col[:, 0:1], op0=OP.mult, op1=OP.add
            ).then_inc(EV.h, 1)
        EV.n += 2

        b5_mm = [None, None]
        b5_ev = [None, None]

        def bcast_bf(vec_j0, target, add_one):
            for c in range(4):
                j = vec_j0 + 512 * c
                r, off = j // JSH, j % JSH
                slot = c % 2

                def fs(sync, r=r, off=off, slot=slot, mw=b5_mm[slot]):
                    if mw is not None:
                        sync.wait_ge(MM.h, mw)
                    sync.dma_start(out=mst[slot][:],
                                   in_=ag_mod_dst[r:r + 1, off:off + 512]
                                   ).then_inc(MST_S[slot].h, 16)
                blk.sync(fs)
                MST_S[slot].n += 16
                ldt = MST_S[slot].n

                def ft(tensor, slot=slot, ldt=ldt, evw=b5_ev[slot]):
                    tensor.wait_ge(MST_S[slot].h, ldt)
                    if evw is not None:
                        tensor.wait_ge(EV.h, evw)
                    tensor.matmul(pb[slot][:, :], ones_rs[0:1, :], mst[slot][:],
                                  start=True, stop=True).then_inc(MM.h, 1)
                blk.tensor(ft)
                MM.n += 1
                b5_mm[slot] = MM.n
                mmt = MM.n

                def fv(vector, slot=slot, mmt=mmt, c=c, target=target,
                       add_one=add_one):
                    vector.wait_ge(MM.h, mmt)
                    dst = target[:, 512 * c:512 * (c + 1)]
                    if add_one:
                        vector.tensor_scalar_add(dst, pb[slot][:, :], 1.0
                                                 ).then_inc(EV.h, 1)
                    else:
                        vector.tensor_copy(dst, pb[slot][:, :]).then_inc(EV.h, 1)
                blk.vector(fv)
                EV.n += 1
                b5_ev[slot] = EV.n

        bcast_bf(8192, B1b, True)
        bcast_bf(6144, B0b, False)

        barrier()

        emit_down_tcopy()
        TCT0, TCT1 = TCW[0].n, TCW[1].n

        e_guev = [None] * 4
        e_psev = [None] * 8
        e_dnmm = [None, None]
        e_dpsev = [None, None]
        e_ptev = [None]
        e_atev = [None]
        e_xoev = [None]
        d_xofree = [None, None]
        d_woev = [None, None]
        d_h2qtp2 = [None, None]
        d_xcac = [None, None]
        d_h2sc = [None]

        for st in range(NST):
            ST_EV = EV.n
            ST_Y = YST.n
            for tt in range(TPS):
                t = TPS * st + tt
                par = tt % 2
                xoh2_p = d_xoh2 if par == 0 else d_xoh2b
                x2_p = d_x2 if par == 0 else d_x2b
                h2q_p = d_h2q if par == 0 else d_h2qb
                qb = 192 * par

                def fs(sync, t=t, xofree=d_xofree[par], tt=tt, xoh2_p=xoh2_p):
                    sync.wait_ge(EV.h, ST_EV)
                    if tt == 0:
                        sync.wait_ge(YST.h, ST_Y)
                    if xofree is not None:
                        sync.wait_ge(GP.h, xofree)
                    sync.dma_start(out=xoh2_p[:],
                                   in_=x_own[128 * t:128 * (t + 1), :]
                                   ).then_inc(DXO.h, 16)
                blk.sync(fs)
                DXO.n += 16
                xo_ld = DXO.n

                for c in range(4):
                    def ft(tensor, t=t, c=c, woev=d_woev[par], par=par):
                        if c == 0:
                            tensor.wait_ge(EV.h, ST_EV)
                            if woev is not None:
                                tensor.wait_ge(EV.h, woev)
                        tensor.matmul(pb[4 * par + c][:, :],
                                      statesB[:, 128 * t:128 * (t + 1)],
                                      WoTga[:, 512 * c:512 * (c + 1)],
                                      start=True, stop=True).then_inc(MM.h, 1)
                    blk.tensor(ft)
                    MM.n += 1
                WO_MM = MM.n

                def fv(vector, xo_ld=xo_ld, WO_MM=WO_MM, tt=tt, par=par,
                       x2_p=x2_p, xoh2_p=xoh2_p, xcg=d_xcac[par]):
                    if tt == 0:
                        vector.wait_ge(YST.h, ST_Y)
                    if xcg is not None:
                        vector.wait_ge(ACS.h, xcg)
                    vector.wait_ge(DXO.h, xo_ld)
                    vector.wait_ge(MM.h, WO_MM)
                    for c in range(4):
                        vector.tensor_add(x2_p[:, 512 * c:512 * (c + 1)],
                                          pb[4 * par + c][:, :],
                                          xoh2_p[:, 512 * c:512 * (c + 1)])
                    vector.engine_nop().then_inc(EV.h, 1)
                blk.vector(fv)
                EV.n += 1
                X2_EV = EV.n
                d_woev[par] = X2_EV

                xc_ac = ln_chain(x2_p[:], d_scr[:], wait_ev=X2_EV, slot=par,
                                 skip_evall=True, wait_ac=d_xcac[par],
                                 fc_wait_ev=d_h2sc[0])
                d_xcac[par] = xc_ac

                def fv4(vector, xc_ac=xc_ac, h2q_prev=d_h2qtp2[par],
                        xoh2_p=xoh2_p, qb=qb, par=par):
                    vector.wait_ge(ACS.h, xc_ac)
                    if h2q_prev is not None:
                        vector.wait_ge((H2QT if par == 0 else H2QT1).h,
                                       h2q_prev[0])
                        vector.wait_ge((H2QS if par == 0 else H2QS1).h,
                                       h2q_prev[1])
                    vector.tensor_tensor(d_scr[:], d_scr[:], B1b[:], OP.mult)
                    vector.drain()
                    vector.tensor_add(xoh2_p[:], d_scr[:], B0b[:])
                    vector.drain()
                    h3 = xoh2_p[:].rearrange("p (b c) -> p b c", c=32)
                    vector.tensor_reduce(qsc[:, qb:qb + 64], h3, AX.X, OP.max,
                                         apply_absolute_value=True)
                    vector.drain()
                    vector.tensor_scalar(qsc[:, qb + 128:qb + 192],
                                         qsc[:, qb:qb + 64],
                                         2.0 * C448I, 2.0 * EPS30, OP.mult, OP.max)
                    vector.drain()
                    vector.reciprocal(qsc[:, qb + 64:qb + 128],
                                      qsc[:, qb + 128:qb + 192]
                                      ).then_inc(EV.h, 1)
                blk.vector(fv4)
                EV.n += 1
                h2sc_ev = EV.n
                d_h2sc[0] = h2sc_ev

                def fgq4(gpsimd, h2sc_ev=h2sc_ev, par=par, xoh2_p=xoh2_p,
                         h2q_p=h2q_p, qb=qb):
                    gpsimd.wait_ge(EV.h, h2sc_ev)
                    h3g = xoh2_p[:].rearrange("p (b c) -> p b c", c=32)
                    q3 = q8buf[:, 2048 * par:2048 * (par + 1)
                               ].rearrange("p (b c) -> p b c", c=32)
                    i3 = qsc[:, qb + 64:qb + 128].rearrange("p (b o) -> p b o",
                                                            o=1)
                    gpsimd.tensor_tensor(q3, h3g, i3.broadcast_to([128, 64, 32]),
                                         OP.mult)
                    gpsimd.drain()
                    s3 = qsc[:, qb + 128:qb + 192].rearrange("p (b o) -> p b o",
                                                             o=1)
                    d3 = h2q_p[:].rearrange("p (b c) -> p b c", c=32)
                    gpsimd.tensor_tensor(d3, q3, s3.broadcast_to([128, 64, 32]),
                                         OP.mult).then_inc(GP.h, 1)
                blk.gpsimd(fgq4)
                GP.n += 1
                h2q_gp = GP.n
                d_xofree[par] = h2q_gp

                def ftp(eng, tt=tt, h2q_ev=h2q_gp, par2=0, h2q_p=h2q_p,
                        sem=None):
                    eng.wait_ge(GP.h, h2q_ev)
                    for k in range(par2, KC, 2):
                        eng.dma_start(out=h2qT[:, k, tt, :],
                                      in_=h2q_p[:, 128 * k:128 * (k + 1)],
                                      transpose=True).then_inc(sem, 16)
                semT = H2QT if par == 0 else H2QT1
                semS = H2QS if par == 0 else H2QS1
                blk.scalar(lambda scalar, tt=tt, h2q_ev=h2q_gp, h2q_p=h2q_p,
                           semT=semT:
                           ftp(scalar, tt, h2q_ev, 0, h2q_p, semT.h))
                blk.sync(lambda sync, tt=tt, h2q_ev=h2q_gp, h2q_p=h2q_p,
                         semS=semS:
                         ftp(sync, tt, h2q_ev, 1, h2q_p, semS.h))
                semT.n += 8 * 16
                semS.n += 8 * 16
                d_h2qtp2[par] = (semT.n, semS.n)
            H2QT_TP = (H2QT.n, H2QS.n, H2QT1.n, H2QS1.n)

            barrier()

            e_silev = [None]
            e_gpq = [None]
            e_dqev = [None]
            for o in range(ICH):
                gslot = o % 2
                uslot = 2 + o % 2

                def fsg(sync, o=o, gslot=gslot, guev=e_guev[gslot]):
                    sync.wait_ge(TCW[0].h, TCT0)
                    sync.wait_ge(TCW[1].h, TCT1)
                    if guev is not None:
                        sync.wait_ge(EV.h, guev)
                    src = scrT_g[o].rearrange("(k p) c -> p k c", p=128)
                    sync.dma_start(out=e_gu[gslot][:], in_=src
                                   ).then_inc(GU[gslot].h, 16)
                blk.sync(fsg)
                GU[gslot].n += 16
                gld = GU[gslot].n

                def fsu(sync, o=o, uslot=uslot, guev=e_guev[uslot]):
                    sync.wait_ge(TCW[0].h, TCT0)
                    sync.wait_ge(TCW[1].h, TCT1)
                    if guev is not None:
                        sync.wait_ge(EV.h, guev)
                    src = scrT_u[o].rearrange("(k p) c -> p k c", p=128)
                    sync.dma_start(out=e_gu[uslot][:], in_=src
                                   ).then_inc(GU[uslot].h, 16)
                blk.sync(fsu)
                GU[uslot].n += 16
                uld = GU[uslot].n

                bg = 4 + (o % 2)
                bu = 6 + (o % 2)

                def ft(tensor, o=o, gslot=gslot, uslot=uslot, bg=bg, bu=bu,
                       gld=gld, uld=uld, h2qt=H2QT_TP,
                       pg=e_psev[bg], pu=e_psev[bu]):
                    tensor.wait_ge(H2QT.h, h2qt[0])
                    tensor.wait_ge(H2QS.h, h2qt[1])
                    tensor.wait_ge(H2QT1.h, h2qt[2])
                    tensor.wait_ge(H2QS1.h, h2qt[3])
                    tensor.wait_ge(GU[gslot].h, gld)
                    if pg is not None:
                        tensor.wait_ge(EV.h, pg)
                    for k in range(KC):
                        inst = tensor.matmul(
                            pb[bg][:, :], e_gu[gslot][:, k, :],
                            h2qT[:, k, :, :].rearrange("p t c -> p (t c)"),
                            start=(k == 0), stop=(k == KC - 1))
                    inst.then_inc(MM.h, 1)
                    tensor.wait_ge(GU[uslot].h, uld)
                    if pu is not None:
                        tensor.wait_ge(EV.h, pu)
                    for k in range(KC):
                        inst = tensor.matmul(
                            pb[bu][:, :], e_gu[uslot][:, k, :],
                            h2qT[:, k, :, :].rearrange("p t c -> p (t c)"),
                            start=(k == 0), stop=(k == KC - 1))
                    inst.then_inc(MM.h, 1)
                blk.tensor(ft)
                MM.n += 2
                gu_mm = MM.n

                def fsil(scalar, bg=bg, gu_mm=gu_mm, sev=e_silev[0]):
                    scalar.wait_ge(MM.h, gu_mm)
                    if sev is not None:
                        scalar.wait_ge(EV.h, sev)
                    scalar.activation(out=e_sil[:], in_=pb[bg][:, :],
                                      func=AF.Silu).then_inc(ACS.h, 1)
                blk.scalar(fsil)
                ACS.n += 1
                sil_ac = ACS.n

                def fdance1(vector, o=o, bg=bg, bu=bu, sil_ac=sil_ac,
                            gpw=e_gpq[0]):
                    vector.wait_ge(ACS.h, sil_ac)
                    vector.tensor_tensor(e_m[:], e_sil[:], pb[bu][:, :], OP.mult
                                         ).then_inc(EV.h, 1)
                    vector.drain()
                    if gpw is not None:
                        vector.wait_ge(GP.h, gpw)
                    vector.transpose(e_Bv[:], e_m[:])
                    vector.drain()
                    B3 = e_Bv[:].rearrange("p (b c) -> p b c", c=32)
                    vector.tensor_reduce(qsc[:, 0:16], B3, AX.X, OP.max,
                                         apply_absolute_value=True)
                    vector.drain()
                    vector.tensor_scalar(qsc[:, 32:48], qsc[:, 0:16],
                                         2.0 * C448I, 2.0 * EPS30, OP.mult, OP.max)
                    vector.drain()
                    vector.reciprocal(qsc[:, 16:32], qsc[:, 32:48]
                                      ).then_inc(EV.h, 1)
                blk.vector(fdance1)
                EV.n += 2
                e_silev[0] = EV.n - 1
                e_psev[bg] = EV.n - 1
                e_psev[bu] = EV.n - 1
                e_guev[gslot] = EV.n - 1
                e_guev[uslot] = EV.n - 1
                sc_ev = EV.n

                def fgq(gpsimd, o=o, sc_ev=sc_ev, dqw=e_dqev[0]):
                    gpsimd.wait_ge(EV.h, sc_ev)
                    if dqw is not None:
                        gpsimd.wait_ge(EV.h, dqw)
                    B3 = e_Bv[:].rearrange("p (b c) -> p b c", c=32)
                    q3 = q8buf[:, 0:512].rearrange("p (b c) -> p b c", c=32)
                    i3 = qsc[:, 16:32].rearrange("p (b o) -> p b o", o=1)
                    gpsimd.tensor_tensor(q3, B3, i3.broadcast_to([128, 16, 32]),
                                         OP.mult)
                    gpsimd.drain()
                    s3 = qsc[:, 32:48].rearrange("p (b o) -> p b o", o=1)
                    d3 = e_dqB[:].rearrange("p (b c) -> p b c", c=32)
                    gpsimd.tensor_tensor(d3, q3, s3.broadcast_to([128, 16, 32]),
                                         OP.mult).then_inc(GP.h, 1)
                blk.gpsimd(fgq)
                GP.n += 1
                e_gpq[0] = GP.n
                gq_gp = GP.n

                def fdance2(vector, o=o, gq_gp=gq_gp):
                    vector.wait_ge(GP.h, gq_gp)
                    vector.transpose(mqfm[:, o, :], e_dqB[:]).then_inc(EV.h, 1)
                blk.vector(fdance2)
                EV.n += 1
                e_dqev[0] = EV.n

            barrier()

            for dgrp in range(4):
                for dj in range(4):
                    d128 = 4 * dgrp + dj
                    accb = dj % 2
                    for iq in range(4):
                        dslot = iq % 2

                        def fsd(sync, d128=d128, iq=iq, dslot=dslot,
                                dnm=e_dnmm[dslot]):
                            sync.wait_ge(TCW[0].h, TCT0)
                            sync.wait_ge(TCW[1].h, TCT1)
                            if dnm is not None:
                                sync.wait_ge(MM.h, dnm)
                            src = scrT_d[d128, 2048 * iq:2048 * (iq + 1), :]
                            sync.dma_start(
                                out=e_dn[dslot][:],
                                in_=src.rearrange("(k p) c -> p k c", p=128),
                            ).then_inc(DN[dslot].h, 16)
                        blk.sync(fsd)
                        DN[dslot].n += 16
                        dld = DN[dslot].n

                        def ftd(tensor, iq=iq, dslot=dslot, accb=accb, dld=dld,
                                dpe=e_dpsev[accb]):
                            tensor.wait_ge(DN[dslot].h, dld)
                            if iq == 0 and dpe is not None:
                                tensor.wait_ge(EV.h, dpe)
                            for kk in range(KC):
                                isl = 16 * iq + kk
                                inst = tensor.matmul(
                                    pb[accb][:, :], e_dn[dslot][:, kk, :],
                                    mqfm[:, isl, :],
                                    start=(iq == 0 and kk == 0),
                                    stop=(iq == 3 and kk == KC - 1))
                            inst.then_inc(MM.h, 1)
                        blk.tensor(ftd)
                        MM.n += 1
                        e_dnmm[dslot] = MM.n
                    dn_mm = MM.n

                    def fev(vector, accb=accb, dn_mm=dn_mm):
                        vector.wait_ge(MM.h, dn_mm)
                        vector.tensor_copy(e_rot[:], pb[accb][:, :]
                                           ).then_inc(EV.h, 1)
                    blk.vector(fev)
                    EV.n += 1
                    e_dpsev[accb] = EV.n
                    rot_ev = EV.n

                    def ftt(tensor, dj=dj, rot_ev=rot_ev, pte=e_ptev[0]):
                        tensor.wait_ge(EV.h, rot_ev)
                        if dj == 0 and pte is not None:
                            tensor.wait_ge(EV.h, pte)
                        for tt in range(TPS):
                            inst = tensor.transpose(
                                pb[2 + tt][:, 128 * dj:128 * (dj + 1)],
                                e_rot[:, 128 * tt:128 * (tt + 1)], eyeF_s[:])
                        inst.then_inc(MM.h, 1)
                    blk.tensor(ftt)
                    MM.n += 1
                PT_MM = MM.n

                for tt in range(TPS):
                    t = TPS * st + tt

                    def fso(sync, t=t, dgrp=dgrp, xoev=e_xoev[0]):
                        if xoev is not None:
                            sync.wait_ge(EV.h, xoev)
                        sync.dma_start(
                            out=e_sil[:],
                            in_=x_own[128 * t:128 * (t + 1),
                                      512 * dgrp:512 * (dgrp + 1)],
                        ).then_inc(XOD.h, 16)
                    blk.sync(fso)
                    XOD.n += 16
                    xod_ld = XOD.n

                    def fta(tensor, t=t, dgrp=dgrp, atev=e_atev[0]):
                        if atev is not None:
                            tensor.wait_ge(EV.h, atev)
                        tensor.matmul(pb[6][:, :],
                                      statesB[:, 128 * t:128 * (t + 1)],
                                      WoTga[:, 512 * dgrp:512 * (dgrp + 1)],
                                      start=True, stop=True).then_inc(MM.h, 1)
                    blk.tensor(fta)
                    MM.n += 1
                    at_mm = MM.n

                    def ffin(vector, tt=tt, at_mm=at_mm, xod_ld=xod_ld,
                             PT_MM=PT_MM, yprev=YST.n):
                        vector.wait_ge(MM.h, max(at_mm, PT_MM))
                        vector.wait_ge(XOD.h, xod_ld)
                        if yprev:
                            vector.wait_ge(YST.h, yprev)
                        vector.tensor_add(e_x2r[:], pb[6][:, :], e_sil[:]
                                          ).then_inc(EV.h, 1)
                        vector.drain()
                        vector.tensor_add(e_fin[:], pb[2 + tt][:, :], e_x2r[:]
                                          ).then_inc(EV.h, 1)
                    blk.vector(ffin)
                    EV.n += 2
                    e_atev[0] = EV.n - 1
                    e_xoev[0] = EV.n - 1
                    fin_ev = EV.n

                    def fst(sync, t=t, dgrp=dgrp, fin_ev=fin_ev):
                        sync.wait_ge(EV.h, fin_ev)
                        sync.dma_start(
                            out=y[128 * t:128 * (t + 1),
                                  512 * dgrp:512 * (dgrp + 1)],
                            in_=e_fin[:]).then_inc(YST.h, 16)
                    blk.sync(fst)
                    YST.n += 16

                e_ptev[0] = EV.n

            barrier()

        @blk.sync
        def _(sync):
            sync.wait_ge(YST.h, YST.n)
        barrier()

    return nc


_NC_CACHE = {}


def _get_nc():
    if "nc" not in _NC_CACHE:
        _NC_CACHE["nc"] = build_nc()
    return _NC_CACHE["nc"]


def kernel(L0_attn_x, L0_attn_Wk, L0_attn_Wv, L0_attn_Wo, L0_attn_Wts,
           L0_attn_gate, L0_attn_time_decay, L0_attn_time_first,
           L0_attn_ln_w, L0_attn_ln_b, L0_ffn_gate_w, L0_ffn_up_w,
           L0_ffn_down_w, L0_c, L0_wmod):
    f32 = np.float32
    x = np.ascontiguousarray(np.asarray(L0_attn_x, dtype=f32))
    Wk = np.asarray(L0_attn_Wk, dtype=f32)
    Wv = np.asarray(L0_attn_Wv, dtype=f32)
    Wo = np.asarray(L0_attn_Wo, dtype=f32)
    Wts = np.asarray(L0_attn_Wts, dtype=f32)
    gate = np.asarray(L0_attn_gate, dtype=f32)
    td = np.asarray(L0_attn_time_decay, dtype=f32)
    tf = np.asarray(L0_attn_time_first, dtype=f32)
    lnw = np.asarray(L0_attn_ln_w, dtype=f32)
    lnb = np.asarray(L0_attn_ln_b, dtype=f32)
    gwn = np.ascontiguousarray(np.asarray(L0_ffn_gate_w, dtype=f32))
    uwn = np.ascontiguousarray(np.asarray(L0_ffn_up_w, dtype=f32))
    dwn = np.ascontiguousarray(np.asarray(L0_ffn_down_w, dtype=f32))
    c = np.asarray(L0_c, dtype=f32)
    wmod = np.ascontiguousarray(np.asarray(L0_wmod, dtype=f32))

    wtsT = np.ascontiguousarray(Wts.T)
    wkvT = np.ascontiguousarray(np.concatenate([Wk, Wv], 0).T)
    woT = np.ascontiguousarray(Wo.T)
    eyeF = np.eye(128, dtype=f32)
    ones_r = np.ones((1, 128), dtype=f32)
    ones_c = np.ones((128, 1), dtype=f32)
    tfB = np.ascontiguousarray(np.broadcast_to(tf[None, :], (128, S)))
    tdB = np.ascontiguousarray(np.broadcast_to(td[None, :], (4, S)))
    tdc = np.ascontiguousarray(td[:, None])

    in_maps = []
    for core in range(8):
        b, q = core // 4, core % 4
        mM = np.zeros((4, S), dtype=f32)
        mE = np.zeros((4, S), dtype=f32)
        for j in range(4):
            if j < q:
                mM[j, :] = 1.0
                mE[j, :] = 2048.0 * (q - 1 - j)
        cT = np.ascontiguousarray(c[b].reshape(KC, 128).T)
        in_maps.append(dict(
            x_own=np.ascontiguousarray(x[b, 2048 * q:2048 * (q + 1), :]),
            x_oth=np.ascontiguousarray(x[b, 2048 * ((q + 2) % 4):
                                         2048 * ((q + 2) % 4 + 1), :]),
            wtsT=wtsT, wkvT_in=wkvT, woT_in=woT,
            gate_r=gate[None, :], lnw_r=lnw[None, :], lnb_r=lnb[None, :],
            tdc=tdc, tfB=tfB, tdB=tdB, maskM=mM, maskE=mE,
            cT_in=cT,
            wmod_sh=np.ascontiguousarray(wmod[:, JSH * q:JSH * (q + 1)]),
            gw=gwn, uw=uwn, dw=dwn,
            eyeF=eyeF, ones_r=ones_r, ones_c=ones_c,
        ))

    nc = _get_nc()
    res = run_bass_kernel_spmd(nc, in_maps, core_ids=list(range(8)))
    out = np.empty((B, T, D), dtype=np.float32)
    for core in range(8):
        b, q = core // 4, core % 4
        out[b, 2048 * q:2048 * (q + 1), :] = res.results[core]["y"]
    return out


# version 16
# speedup vs baseline: 1.0996x; 1.0839x over previous
import numpy as np
import ml_dtypes
import concourse.bass as bass
import concourse.mybir as mybir
from concourse.bass_utils import run_bass_kernel_spmd

F32 = mybir.dt.float32
BF16 = mybir.dt.bfloat16
FP8 = mybir.dt.float8e4
I32 = mybir.dt.int32
AF = mybir.ActivationFunctionType
OP = mybir.AluOpType
AX = mybir.AxisListType

B, T, D, S = 2, 8192, 2048, 64
NT = 2048
NTILES = 16
KC = 16
I = 8192
ICH = 64
STILE = 512
NST = 4
TPS = 4
NMST = 8

C448I = float(np.float32(1.0) / np.float32(448.0))
EPS30 = 1e-30
LNEPS = 1e-5
RD = float(np.float32(1.0 / 2048.0))
JSH = 3072


class SemT:
    __slots__ = ("h", "n")

    def __init__(self, nc, name):
        self.h = nc.alloc_semaphore(name=name)
        self.n = 0


def build_nc():
    nc = bass.Bass("TRN2", target_bir_lowering=False)

    def di(name, shape, dt=F32):
        return nc.dram_tensor(name, shape, dt, kind="ExternalInput").ap()

    x_own = di("x_own", [NT, D])
    x_oth = di("x_oth", [NT, D])
    wtsT = di("wtsT", [D, D])
    wkvT_in = di("wkvT_in", [D, 128])
    woT_in = di("woT_in", [S, D])
    gate_r = di("gate_r", [1, D])
    lnw_r = di("lnw_r", [1, D])
    lnb_r = di("lnb_r", [1, D])
    tdc = di("tdc", [S, 1])
    tfB = di("tfB", [128, S])
    tdB = di("tdB", [4, S])
    maskM = di("maskM", [4, S])
    maskE = di("maskE", [4, S])
    cT_in = di("cT_in", [128, KC])
    wmod_sh = di("wmod_sh", [D, JSH])
    gw = di("gw", [I, D])
    uw = di("uw", [I, D])
    dw = di("dw", [D, I])
    eyeF = di("eyeF", [128, 128], F32)
    ones_r = di("ones_r", [1, 128])
    ones_c = di("ones_c", [128, 1])

    y = nc.dram_tensor("y", [NT, D], F32, kind="ExternalOutput").ap()

    def dint(name, shape, dt):
        return nc.dram_tensor(name, shape, dt, kind="Internal").ap()

    scr_g = dint("scr_g", [I, D], BF16)
    scr_u = dint("scr_u", [I, D], BF16)
    scr_d = dint("scr_d", [D, I], BF16)
    scr_wg = dint("scr_wg", [D, D], BF16)
    scrT_g = dint("scrT_g", [ICH, D, 128], BF16)
    scrT_u = dint("scrT_u", [ICH, D, 128], BF16)
    scrT_d = dint("scrT_d", [KC, I, 128], BF16)
    ag_mod_src = dint("ag_mod_src", [1, JSH], F32)
    ag_mod_dst = dint("ag_mod_dst", [4, JSH], F32)
    ag_car_src = dint("ag_car_src", [S, 1], F32)
    ag_car_dst = dint("ag_car_dst", [4, S], F32)

    sb = nc.alloc_sbuf_tensor
    eyeF_s = sb("eyeF_s", [128, 128], F32)
    ones_rs = sb("ones_rs", [1, 128], F32)
    ones_cs = sb("ones_cs", [128, 1], F32)
    gbT = sb("gbT", [128, KC], F32)
    WoTga = sb("WoTga", [S, D], BF16)
    WkvTb = sb("WkvTb", [128, KC, 128], BF16)
    cT_s = sb("cT_s", [128, KC], F32)
    scT = sb("scT", [128, KC], BF16)
    etf_bc = sb("etf_bc", [128, S], F32)
    tdc_s = sb("tdc_s", [S, 1], F32)
    wcol = sb("wcol", [S, 1], F32)
    tdB_s = sb("tdB_s", [4, S], F32)
    maskM_s = sb("maskM_s", [4, S], F32)
    maskE_s = sb("maskE_s", [4, S], F32)
    carw = sb("carw", [4, S], F32)
    cin_col = sb("cin_col", [S, 1], F32)
    wkvT_s = sb("wkvT_s", [S, NT], F32)
    statesB = sb("statesB", [S, NT], BF16)
    lnstat = sb("lnstat", [128, 256], F32)
    qsc = sb("qsc", [128, 768], F32)
    q8buf = sb("q8buf", [128, 4 * D], FP8)
    mst = [sb("mst0", [1, 512], F32), sb("mst1", [1, 512], F32)]

    ARB = 84480
    arb = sb("arb", [128, ARB], BF16)

    def vB(off, n, p=128):
        assert off + n <= ARB and off % 2 == 0, (off, n)
        return arb[0:p, off:off + n]

    def vF(off, n, p=128):
        assert off + 2 * n <= ARB and off % 2 == 0, (off, n)
        return arb[0:p, off:off + 2 * n].bitcast(F32)

    pre_w = [vF(68096, 2048), vF(76288, 2048)]
    pre_dq = [vB(72192, 2048), vB(74240, 2048)]
    tc_buf = [vB(80384, 2048), vB(82432, 2048)]

    omg_bc = vF(0, 2048)
    A1_bc = vF(4096, 2048)
    A0_bc = vF(8192, 2048)
    lnw_bc = vF(12288, 2048)
    lnb_bc = vF(16384, 2048)
    mx_xo = [vF(20480, 2048), vF(24576, 2048)]
    mx_hos = [vF(28672, 2048), vF(32768, 2048)]
    mx_xs = [vF(36864, 2048), vF(40960, 2048)]
    mx_scr = vF(45056, 2048)
    mx_w1 = vF(49152, 64)
    mx_w2 = vF(49280, 64)
    mxb_hot = vB(49408, 2048)
    mxb_xn = vB(51456, 2048)
    mxb_hsT = vB(53504, 4096).rearrange("p (t k c) -> p t k c", t=2, k=KC)
    mxb_xnT = vB(57600, 2048).rearrange("p (k c) -> p k c", k=KC)
    mxb_w = [vB(59648, 4096).rearrange("p (k c) -> p k c", k=KC),
             vB(63744, 4096).rearrange("p (k c) -> p k c", k=KC)]
    p0_wm = [vB(20480, 8192).rearrange("p (k c) -> p k c", k=KC),
             vB(36864, 8192).rearrange("p (k c) -> p k c", k=KC)]
    p0_wo = vF(53504, 2048, p=64)
    scanbuf = vF(68096, NT, p=S)
    p0_r = [vF(57600, 2048, p=1), vF(61696, 2048, p=1),
            vF(45056, 2048, p=1), vF(49152, 2048, p=1)]
    p0_gbc = vF(28672, 2048)

    mqfm = vB(0, 32768).rearrange("p (i t) -> p i t", i=ICH)
    h2qT = vB(32768, 8192).rearrange("p (k t c) -> p k t c", k=KC, t=TPS)
    e_gu = [vB(40960 + i * 2048, 2048).rearrange("p (k c) -> p k c", k=KC)
            for i in range(4)]
    e_dn = [vB(49152 + i * 2048, 2048).rearrange("p (k c) -> p k c", k=KC)
            for i in range(2)]
    e_dqB = vB(53248, 512)
    d_h2q = vB(53760, 2048)
    B1b = vB(55808, 2048)
    B0b = vB(57856, 2048)
    d_xoh2 = vF(59904, 2048)
    d_x2 = vF(64000, 2048)
    e_sil = vF(59904, 512)
    e_m = vF(60928, 512)
    e_Bv = vF(61952, 512)
    e_rot = vF(62976, 512)
    e_x2r = vF(64000, 512)
    e_fin = vF(65024, 512)
    d_scr = vF(0, 2048)
    d_h2qb = vB(68096, 2048)
    d_xoh2b = vF(70144, 2048)
    d_x2b = vF(74240, 2048)

    pb = [nc.alloc_psum_tensor(f"pb{i}", [128, 512], F32) for i in range(8)]

    MM = SemT(nc, "MM")
    EV = SemT(nc, "EV")
    ACS = SemT(nc, "ACS")
    GP = SemT(nc, "GP")
    CC = SemT(nc, "CC")
    BAR = SemT(nc, "BAR")

    LD_A = SemT(nc, "LD_A")
    MST_S = [SemT(nc, "MST0"), SemT(nc, "MST1")]
    WM_S = [SemT(nc, "WM0"), SemT(nc, "WM1")]
    MODST = [SemT(nc, "MODST0"), SemT(nc, "MODST1")]
    GLD = SemT(nc, "GLD")
    PRE = [SemT(nc, "PRE0"), SemT(nc, "PRE1")]
    SCW = [SemT(nc, "SCW0"), SemT(nc, "SCW1")]
    SCWG = [SemT(nc, "SCWG0"), SemT(nc, "SCWG1")]
    TCR = [SemT(nc, "TCR0"), SemT(nc, "TCR1")]
    TCW = [SemT(nc, "TCW0"), SemT(nc, "TCW1")]
    MXX = [SemT(nc, "MXX0"), SemT(nc, "MXX1")]
    MXO = [SemT(nc, "MXO0"), SemT(nc, "MXO1")]
    WTS = [SemT(nc, "WTS0"), SemT(nc, "WTS1")]
    HST = SemT(nc, "HST")
    XNT = SemT(nc, "XNT")
    H2QT = SemT(nc, "H2QT")
    H2QS = SemT(nc, "H2QS")
    H2QT1 = SemT(nc, "H2QT1")
    H2QS1 = SemT(nc, "H2QS1")
    DXO = SemT(nc, "DXO")
    GU = [SemT(nc, f"GU{i}") for i in range(4)]
    DN = [SemT(nc, "DN0"), SemT(nc, "DN1")]
    XOD = SemT(nc, "XOD")
    YST = SemT(nc, "YST")
    CAR = SemT(nc, "CAR")

    with nc.Block() as blk:
        barn = [0]

        def barrier():
            barn[0] += 5
            tgt = barn[0]
            for e in ("sync", "scalar", "vector", "tensor", "gpsimd"):
                def f(eng, tgt=tgt, e=e):
                    if e in ("scalar", "vector", "gpsimd", "tensor"):
                        eng.drain()
                    eng.sem_inc(BAR.h, 1)
                    eng.wait_ge(BAR.h, tgt)
                getattr(blk, e)(f)

        @blk.sync
        def _(sync):
            for dst, src in [(eyeF_s[:], eyeF),
                             (ones_rs[:], ones_r), (ones_cs[:], ones_c),
                             (cT_s[:], cT_in), (tdc_s[:], tdc),
                             (tdB_s[:], tdB), (maskM_s[:], maskM),
                             (maskE_s[:], maskE),
                             (qsc[:, 0:S], tfB),
                             (p0_r[0][:], gate_r),
                             (p0_r[1][:], lnw_r), (p0_r[2][:], lnb_r),
                             (p0_wo[:], woT_in)]:
                sync.dma_start(out=dst, in_=src).then_inc(LD_A.h, 16)
        LD_A.n += 13 * 16
        LD0 = LD_A.n

        @blk.gpsimd
        def _(gpsimd):
            gpsimd.dma_start(
                out=WkvTb[:],
                in_=wkvT_in.rearrange("(k p) c -> p k c", p=128),
            ).then_inc(GLD.h, 16)
        GLD.n += 16

        @blk.scalar
        def _(scalar):
            scalar.wait_ge(LD_A.h, LD0)
            scalar.activation(out=scT[:], in_=cT_s[:], func=AF.Silu).then_inc(ACS.h, 1)
            scalar.activation(out=p0_r[3][:], in_=p0_r[0][:], func=AF.Sigmoid)
            scalar.activation(out=etf_bc[:], in_=qsc[:, 0:S], func=AF.Exp)
            scalar.activation(out=wcol[:], in_=tdc_s[:], func=AF.Exp).then_inc(ACS.h, 1)
        ACS.n += 2

        @blk.vector
        def _(vector):
            vector.wait_ge(ACS.h, ACS.n)
            vector.tensor_scalar(out=p0_r[0][:], in0=p0_r[3][:],
                                 scalar1=-1.0, scalar2=1.0,
                                 op0=OP.mult, op1=OP.add).then_inc(EV.h, 1)
        EV.n += 1

        MOD_MM0 = MM.n
        mod_ev = [None, None]
        for n in range(6):
            slot = n % 2

            def fg(gpsimd, n=n, slot=slot):
                if n >= 2:
                    gpsimd.wait_ge(MM.h, MOD_MM0 + n - 1)
                gpsimd.dma_start(
                    out=p0_wm[slot][:],
                    in_=wmod_sh.rearrange("(k p) (n f) -> p k n f",
                                          p=128, f=512)[:, :, n, :],
                ).then_inc(WM_S[slot].h, 16)
            blk.gpsimd(fg)
            WM_S[slot].n += 16
            wmt = WM_S[slot].n

            def ft(tensor, n=n, slot=slot, wmt=wmt, evw=mod_ev[slot]):
                if n == 0:
                    tensor.wait_ge(ACS.h, 1)
                if evw is not None:
                    tensor.wait_ge(EV.h, evw)
                tensor.wait_ge(WM_S[slot].h, wmt)
                for k in range(KC):
                    inst = tensor.matmul(pb[slot][0:1, :], scT[:, k:k + 1],
                                         p0_wm[slot][:, k, :],
                                         start=(k == 0), stop=(k == KC - 1))
                inst.then_inc(MM.h, 1)
            blk.tensor(ft)
            MM.n += 1
            mmt = MM.n

            def fv(vector, n=n, slot=slot, mmt=mmt, stw=MODST[slot].n):
                vector.wait_ge(MM.h, mmt)
                if n >= 2:
                    vector.wait_ge(MODST[slot].h, stw)
                vector.tensor_copy(mst[slot][:], pb[slot][0:1, :]).then_inc(EV.h, 1)
            blk.vector(fv)
            EV.n += 1
            evt = EV.n

            def fst(sync, n=n, slot=slot, evt=evt):
                sync.wait_ge(EV.h, evt)
                sync.dma_start(out=ag_mod_src[0:1, 512 * n:512 * (n + 1)],
                               in_=mst[slot][:]).then_inc(MODST[slot].h, 16)
            blk.sync(fst)
            MODST[slot].n += 16
            mod_ev[slot] = EV.n

        @blk.gpsimd
        def _(gpsimd):
            gpsimd.wait_ge(MODST[0].h, MODST[0].n)
            gpsimd.wait_ge(MODST[1].h, MODST[1].n)
            gpsimd.collective_compute(
                "AllGather", OP.bypass,
                replica_groups=[[0, 1, 2, 3], [4, 5, 6, 7]],
                ins=[ag_mod_src], outs=[ag_mod_dst],
            ).then_inc(CC.h, 1)
        CC.n += 1

        mst_mm = [None, None]
        mst_ev = [None, None]

        def bcast_dram(vec_j0, target, add_one, nparts=128, banks=(2, 3)):
            for c in range(4):
                j = vec_j0 + 512 * c
                r, off = j // JSH, j % JSH
                slot = c % 2

                def fs(sync, r=r, off=off, slot=slot):
                    sync.wait_ge(CC.h, CC.n)
                    if mst_mm[slot] is not None:
                        sync.wait_ge(MM.h, mst_mm[slot])
                    sync.dma_start(out=mst[slot][:],
                                   in_=ag_mod_dst[r:r + 1, off:off + 512]
                                   ).then_inc(MST_S[slot].h, 16)
                blk.sync(fs)
                MST_S[slot].n += 16
                ldt = MST_S[slot].n

                def ft(tensor, slot=slot, ldt=ldt, nparts=nparts, banks=banks):
                    tensor.wait_ge(MST_S[slot].h, ldt)
                    if mst_ev[slot] is not None:
                        tensor.wait_ge(EV.h, mst_ev[slot])
                    tensor.matmul(pb[banks[slot]][0:nparts, :],
                                  ones_rs[0:1, 0:nparts],
                                  mst[slot][:], start=True, stop=True
                                  ).then_inc(MM.h, 1)
                blk.tensor(ft)
                MM.n += 1
                mst_mm[slot] = MM.n
                mmt = MM.n

                def fv(vector, slot=slot, mmt=mmt, c=c, banks=banks,
                       nparts=nparts, target=target, add_one=add_one):
                    vector.wait_ge(MM.h, mmt)
                    dst = target[:, 512 * c:512 * (c + 1)]
                    if add_one:
                        vector.tensor_scalar_add(dst, pb[banks[slot]][0:nparts, :],
                                                 1.0).then_inc(EV.h, 1)
                    else:
                        vector.tensor_copy(dst, pb[banks[slot]][0:nparts, :]
                                           ).then_inc(EV.h, 1)
                blk.vector(fv)
                EV.n += 1
                mst_ev[slot] = EV.n

        bcast_dram(0, A0_bc, False)
        bcast_dram(2048, A1_bc, True)

        row_free = [None, None]

        def bcast_row(row_ap, target):
            for c in range(4):
                slot = c % 2

                def ft(tensor, c=c, slot=slot, row_ap=row_ap):
                    tensor.wait_ge(EV.h, 1)
                    if row_free[slot] is not None:
                        tensor.wait_ge(EV.h, row_free[slot])
                    tensor.matmul(pb[4 + slot][:, :], ones_rs[0:1, :],
                                  row_ap[0:1, 512 * c:512 * (c + 1)],
                                  start=True, stop=True).then_inc(MM.h, 1)
                blk.tensor(ft)
                MM.n += 1
                mmt = MM.n

                def fv(vector, c=c, slot=slot, mmt=mmt, target=target):
                    vector.wait_ge(MM.h, mmt)
                    vector.tensor_copy(target[:, 512 * c:512 * (c + 1)],
                                       pb[4 + slot][:, :]).then_inc(EV.h, 1)
                blk.vector(fv)
                EV.n += 1
                row_free[slot] = EV.n

        bcast_row(p0_r[0], omg_bc)
        bcast_row(p0_r[1], lnw_bc)
        bcast_row(p0_r[2], lnb_bc)
        bcast_row(p0_r[3], p0_gbc)
        ROWS_DONE_EV = EV.n

        @blk.sync
        def _(sync):
            sync.wait_ge(EV.h, ROWS_DONE_EV)
            sync.wait_ge(CC.h, CC.n)
            sync.dma_start(out=p0_r[0][:],
                           in_=ag_mod_dst[1:2, 1024:3072]).then_inc(LD_A.h, 16)
        LD_A.n += 16
        GA_LD = LD_A.n

        @blk.vector
        def _(vector):
            vector.wait_ge(LD_A.h, GA_LD)
            vector.tensor_scalar_max(p0_r[1][:], p0_r[0][:], 0.25)
            vector.tensor_scalar_min(p0_r[2][:], p0_r[0][:], -0.25)
            vector.tensor_single_scalar(p0_r[3][:], p0_r[0][:], 0.0, OP.is_ge)
            vector.drain()
            vector.tensor_sub(p0_r[0][:], p0_r[1][:], p0_r[2][:])
            vector.drain()
            vector.tensor_tensor(p0_r[0][:], p0_r[3][:], p0_r[0][:], OP.mult)
            vector.drain()
            vector.tensor_add(p0_r[0][:], p0_r[2][:], p0_r[0][:]
                              ).then_inc(EV.h, 1)
        EV.n += 1
        GA_EV = EV.n

        ga_free = [None, None]
        for c in range(4):
            slot = c % 2

            def ft(tensor, c=c, slot=slot):
                tensor.wait_ge(EV.h, GA_EV)
                if ga_free[slot] is not None:
                    tensor.wait_ge(EV.h, ga_free[slot])
                tensor.matmul(pb[6 + slot][0:S, :], ones_rs[0:1, 0:S],
                              p0_r[0][0:1, 512 * c:512 * (c + 1)],
                              start=True, stop=True).then_inc(MM.h, 1)
            blk.tensor(ft)
            MM.n += 1
            mmt = MM.n

            def fv(vector, c=c, slot=slot, mmt=mmt):
                vector.wait_ge(MM.h, mmt)
                vector.tensor_tensor(WoTga[:, 512 * c:512 * (c + 1)],
                                     p0_wo[:, 512 * c:512 * (c + 1)],
                                     pb[6 + slot][0:S, :], OP.mult
                                     ).then_inc(EV.h, 1)
            blk.vector(fv)
            EV.n += 1
            ga_free[slot] = EV.n

        @blk.sync
        def _(sync):
            sync.wait_ge(EV.h, EV.n)
            sync.dma_start(out=p0_r[1][:],
                           in_=ag_mod_dst[3:4, 1024:3072]).then_inc(LD_A.h, 16)
        LD_A.n += 16
        GB_LD = LD_A.n

        @blk.vector
        def _(vector):
            vector.wait_ge(LD_A.h, GB_LD)
            vector.tensor_scalar_max(p0_r[2][:], p0_r[1][:], 0.25)
            vector.tensor_scalar_min(p0_r[3][:], p0_r[1][:], -0.25)
            vector.tensor_single_scalar(p0_r[0][:], p0_r[1][:], 0.0, OP.is_ge)
            vector.drain()
            vector.tensor_sub(p0_r[1][:], p0_r[2][:], p0_r[3][:])
            vector.drain()
            vector.tensor_tensor(p0_r[1][:], p0_r[0][:], p0_r[1][:], OP.mult)
            vector.drain()
            vector.tensor_add(p0_r[1][:], p0_r[3][:], p0_r[1][:]
                              ).then_inc(EV.h, 1)
        EV.n += 1
        GB_EV = EV.n

        @blk.tensor
        def _(tensor):
            tensor.wait_ge(EV.h, GB_EV)
            for k in range(KC):
                inst = tensor.transpose(pb[0][:, k:k + 1],
                                        p0_r[1][0:1, 128 * k:128 * (k + 1)],
                                        eyeF_s[0:1, 0:1])
            inst.then_inc(MM.h, 1)
        MM.n += 1
        GBT_MM = MM.n

        @blk.vector
        def _(vector):
            vector.wait_ge(MM.h, GBT_MM)
            vector.tensor_copy(gbT[:], pb[0][:, 0:KC]).then_inc(EV.h, 1)
        EV.n += 1

        barrier()

        p1_ev = [None, None]
        p1_st = [None, None]
        for i in range(KC):
            slot = i % 2

            def fs(sync, i=i, slot=slot):
                if p1_ev[slot] is not None:
                    sync.wait_ge(EV.h, p1_ev[slot])
                sync.dma_start(out=pre_w[slot][:],
                               in_=wtsT[128 * i:128 * (i + 1), :]
                               ).then_inc(PRE[slot].h, 16)
            blk.sync(fs)
            PRE[slot].n += 16
            ldt = PRE[slot].n

            def fv(vector, i=i, slot=slot, ldt=ldt, stw=p1_st[slot]):
                vector.wait_ge(PRE[slot].h, ldt)
                if stw is not None:
                    vector.wait_ge(SCW[slot].h, stw)
                vector.tensor_tensor(pre_dq[slot][:], pre_w[slot][:],
                                     p0_gbc[:], OP.mult).then_inc(EV.h, 1)
            blk.vector(fv)
            EV.n += 1
            p1_ev[slot] = EV.n
            evt = EV.n

            def fw(sync, i=i, slot=slot, evt=evt):
                sync.wait_ge(EV.h, evt)
                sync.dma_start(out=scr_wg[128 * i:128 * (i + 1), :],
                               in_=pre_dq[slot][:]).then_inc(SCW[slot].h, 16)
            blk.sync(fw)
            SCW[slot].n += 16
            p1_st[slot] = SCW[slot].n
        FOLD0, FOLD1 = SCW[0].n, SCW[1].n

        pp_q = [max(x for x in (p1_ev[0], p1_ev[1]) if x is not None)]
        pp_w = [(SCW[0].h, p1_st[0]), (SCW[1].h, p1_st[1])]
        pp_g1 = [None]
        pp_g2 = [None]
        pp_pair = [0]

        def prepass_pair(pieces, pair_idx):
            lds = []
            for slot, (src_ap, dst_ap, down, r) in enumerate(pieces):
                def fs(sync, src_ap=src_ap, slot=slot, qw=pp_q[0]):
                    if qw is not None:
                        sync.wait_ge(EV.h, qw)
                    sync.dma_start(out=pre_w[slot][:], in_=src_ap
                                   ).then_inc(PRE[slot].h, 16)
                blk.sync(fs)
                PRE[slot].n += 16
                lds.append(PRE[slot].n)

            qslot = pair_idx % 2

            def fv(vector, pieces=pieces, lds=tuple(lds), qslot=qslot,
                   gw_=pp_g2[0]):
                for slot in range(len(pieces)):
                    vector.wait_ge(PRE[slot].h, lds[slot])
                for slot, (_, _, down, r) in enumerate(pieces):
                    b = 384 * slot
                    w3 = pre_w[slot][:].rearrange("p (b c) -> p b c", c=32)
                    vector.tensor_reduce(qsc[:, b:b + 64], w3, AX.X, OP.max,
                                         apply_absolute_value=True)
                vector.drain()
                if gw_ is not None:
                    vector.wait_ge(GP.h, gw_)
                for slot, (_, _, down, r) in enumerate(pieces):
                    b = 384 * slot
                    sc = b + 128 + 64 * qslot
                    vector.tensor_scalar(qsc[:, sc:sc + 64], qsc[:, b:b + 64],
                                         2.0 * C448I, 2.0 * EPS30, OP.mult, OP.max)
                vector.drain()
                for slot, (_, _, down, r) in enumerate(pieces):
                    b = 384 * slot
                    sc = b + 128 + 64 * qslot
                    vector.reciprocal(qsc[:, b + 64:b + 128], qsc[:, sc:sc + 64])
                vector.drain()
                any_down = any(p[2] for p in pieces)
                if any_down:
                    for slot, (_, _, down, r) in enumerate(pieces):
                        if down:
                            b = 384 * slot
                            sc = b + 128 + 64 * qslot
                            dc = b + 256 + 64 * qslot
                            vector.tensor_scalar_mul(qsc[:, dc:dc + 64],
                                                     qsc[:, sc:sc + 64],
                                                     gbT[:, r:r + 1])
                    vector.drain()
                for slot in range(len(pieces)):
                    b = 384 * slot
                    w3 = pre_w[slot][:].rearrange("p (b c) -> p b c", c=32)
                    qo = 4096 * qslot + 2048 * slot
                    q3 = q8buf[:, qo:qo + 2048].rearrange("p (b c) -> p b c", c=32)
                    i3 = qsc[:, b + 64:b + 128].rearrange("p (b o) -> p b o", o=1)
                    inst = vector.tensor_tensor(q3, w3,
                                                i3.broadcast_to([128, 64, 32]),
                                                OP.mult)
                inst.then_inc(EV.h, 1)
            blk.vector(fv)
            EV.n += 1
            pp_q[0] = EV.n
            evt = EV.n

            def fg(gpsimd, pieces=pieces, evt=evt, qslot=qslot,
                   ww=(pp_w[0], pp_w[1])):
                gpsimd.wait_ge(EV.h, evt)
                for slot in range(len(pieces)):
                    if ww[slot] is not None:
                        gpsimd.wait_ge(ww[slot][0], ww[slot][1])
                for slot, (_, _, down, r) in enumerate(pieces):
                    b = 384 * slot
                    sc = b + 128 + 64 * qslot
                    dc = b + 256 + 64 * qslot
                    qo = 4096 * qslot + 2048 * slot
                    q3 = q8buf[:, qo:qo + 2048].rearrange("p (b c) -> p b c", c=32)
                    scol = qsc[:, dc:dc + 64] if down else qsc[:, sc:sc + 64]
                    s3 = scol.rearrange("p (b o) -> p b o", o=1)
                    d3 = pre_dq[slot][:].rearrange("p (b c) -> p b c", c=32)
                    gpsimd.tensor_tensor(d3, q3,
                                         s3.broadcast_to([128, 64, 32]),
                                         OP.mult).then_inc(GP.h, 1)
            blk.gpsimd(fg)
            gpt_s = []
            for _ in pieces:
                GP.n += 1
                gpt_s.append(GP.n)
            pp_g2[0] = pp_g1[0]
            pp_g1[0] = GP.n

            for slot, (_, dst_ap, down, r) in enumerate(pieces):
                def fw(gpsimd, dst_ap=dst_ap, slot=slot, gpt=gpt_s[slot]):
                    gpsimd.wait_ge(GP.h, gpt)
                    gpsimd.dma_start(out=dst_ap, in_=pre_dq[slot][:]
                                     ).then_inc(SCWG[slot].h, 16)
                blk.gpsimd(fw)
                SCWG[slot].n += 16
                pp_w[slot] = (SCWG[slot].h, SCWG[slot].n)

        pp_pieces = []
        for p_i in range(64):
            pp_pieces.append((gw[128 * p_i:128 * (p_i + 1), :],
                              scr_g[128 * p_i:128 * (p_i + 1), :], False, 0))
            pp_pieces.append((uw[128 * p_i:128 * (p_i + 1), :],
                              scr_u[128 * p_i:128 * (p_i + 1), :], False, 0))
            r, cc_ = p_i // 4, p_i % 4
            pp_pieces.append((dw[128 * r:128 * (r + 1), 2048 * cc_:2048 * (cc_ + 1)],
                              scr_d[128 * r:128 * (r + 1), 2048 * cc_:2048 * (cc_ + 1)],
                              True, r))
        pp_scw = []
        pp_next = [0]

        def emit_pp(n):
            done = 0
            while done < n and pp_next[0] < len(pp_pieces):
                i = pp_next[0]
                pcs = pp_pieces[i:i + 2]
                pp_next[0] += len(pcs)
                done += len(pcs)
                prepass_pair(pcs, pp_pair[0])
                pp_pair[0] += 1
                for _ in pcs:
                    pp_scw.append((SCWG[0].n, SCWG[1].n))

        tcr_w = [None, None]
        tc_slot = [0]

        def tcopy_call(read_src, write_dst_ap, gate_piece_idx):
            slot = tc_slot[0] % 2
            tc_slot[0] += 1
            g0, g1 = pp_scw[gate_piece_idx]

            def fr(scalar, read_src=read_src, slot=slot, ww=tcr_w[slot],
                   g0=g0, g1=g1):
                scalar.wait_ge(SCWG[0].h, g0)
                scalar.wait_ge(SCWG[1].h, g1)
                if ww is not None:
                    scalar.wait_ge(TCW[slot].h, ww)
                scalar.dma_start(out=tc_buf[slot][:], in_=read_src,
                                 transpose=True).then_inc(TCR[slot].h, 16)
            blk.scalar(fr)
            TCR[slot].n += 16
            tpt = TCR[slot].n

            def fw(sync, write_dst_ap=write_dst_ap, slot=slot, tpt=tpt):
                sync.wait_ge(TCR[slot].h, tpt)
                sync.dma_start(
                    out=write_dst_ap,
                    in_=tc_buf[slot][:].rearrange("p (a c) -> p a c", c=128),
                ).then_inc(TCW[slot].h, 16)
            blk.sync(fw)
            TCW[slot].n += 16
            tcr_w[slot] = TCW[slot].n

        tc_jobs = []
        for wi, (scr, scrT) in enumerate(((scr_g, scrT_g), (scr_u, scrT_u))):
            for osp in range(4):
                ready = 3 * (16 * osp + 15) + wi

                def job(osp=osp, scr=scr, scrT=scrT, ready=ready):
                    for k in range(KC):
                        tcopy_call(
                            scr[2048 * osp:2048 * (osp + 1),
                                128 * k:128 * (k + 1)],
                            scrT[16 * osp:16 * (osp + 1),
                                 128 * k:128 * (k + 1), :
                                 ].rearrange("a p c -> p a c"),
                            ready)
                tc_jobs.append((ready, job))
        for i128_ in range(ICH):
            ready = 3 * (60 + i128_ // 16) + 2

            def djob(i128=i128_, ready=ready):
                tcopy_call(
                    scr_d[:, 128 * i128:128 * (i128 + 1)],
                    scrT_d[:, 128 * i128:128 * (i128 + 1), :
                           ].rearrange("a p c -> p a c"),
                    ready)
            tc_jobs.append((ready, djob))

        def emit_ready_tcopy():
            rem = []
            for ready, job in tc_jobs:
                if ready < pp_next[0]:
                    job()
                else:
                    rem.append((ready, job))
            tc_jobs[:] = rem

        def emit_down_tcopy():
            emit_ready_tcopy()
            assert not tc_jobs

        def ln_stats(vector, src, slot, waits=()):
            b = 32 * slot
            for w in waits:
                w(vector)
            for c_ in range(4):
                vector.bn_stats(lnstat[:, b + 6 * c_:b + 6 * c_ + 6],
                                src[:, 512 * c_:512 * (c_ + 1)])
            vector.drain()
            vector.bn_aggr(lnstat[:, b + 24:b + 26], lnstat[:, b:b + 24])
            vector.drain()
            return vector.tensor_scalar_add(lnstat[:, b + 26:b + 27],
                                            lnstat[:, b + 25:b + 26], LNEPS)

        def ln_rsig(scalar, slot):
            b = 32 * slot
            return scalar.activation(out=lnstat[:, b + 27:b + 28],
                                     in_=lnstat[:, b + 26:b + 27], func=AF.Sqrt)

        def ln_nmur(vector, slot):
            b = 32 * slot
            vector.reciprocal(lnstat[:, b + 28:b + 29], lnstat[:, b + 27:b + 28])
            vector.drain()
            return vector.scalar_tensor_tensor(lnstat[:, b + 29:b + 30],
                                               lnstat[:, b + 24:b + 25], -1.0,
                                               lnstat[:, b + 28:b + 29],
                                               OP.mult, OP.mult)

        def ln_norm(scalar, src, out, slot):
            b = 32 * slot
            return scalar.activation(out=out, in_=src, func=AF.Identity,
                                     scale=lnstat[:, b + 28:b + 29],
                                     bias=lnstat[:, b + 29:b + 30])

        def ln_chain(src, xc_out, wait_ld=None, wait_ld_sem=None, wait_ev=None,
                     wait_mm=None, slot=0, skip_evall=False, wait_ac=None,
                     fc_wait_ev=None):
            ev_all = EV.n

            def fstats(vector, ev_all=ev_all):
                if not skip_evall:
                    vector.wait_ge(EV.h, ev_all)
                if wait_ac is not None:
                    vector.wait_ge(ACS.h, wait_ac)
                if wait_ld is not None:
                    vector.wait_ge(wait_ld_sem.h, wait_ld)
                if wait_ev is not None:
                    vector.wait_ge(EV.h, wait_ev)
                ln_stats(vector, src, slot).then_inc(EV.h, 1)
            blk.vector(fstats)
            EV.n += 1
            evt = EV.n

            def fsc(scalar, evt=evt):
                scalar.wait_ge(EV.h, evt)
                ln_rsig(scalar, slot).then_inc(ACS.h, 1)
            blk.scalar(fsc)
            ACS.n += 1
            act = ACS.n

            def fnm(vector, act=act):
                vector.wait_ge(ACS.h, act)
                ln_nmur(vector, slot).then_inc(EV.h, 1)
            blk.vector(fnm)
            EV.n += 1
            evt2 = EV.n

            def fc(scalar, evt2=evt2, src=src, xc_out=xc_out, wait_mm=wait_mm):
                scalar.wait_ge(EV.h, evt2)
                if fc_wait_ev is not None:
                    scalar.wait_ge(EV.h, fc_wait_ev)
                if wait_mm is not None:
                    scalar.wait_ge(MM.h, wait_mm)
                ln_norm(scalar, src, xc_out, slot).then_inc(ACS.h, 1)
            blk.scalar(fc)
            ACS.n += 1
            return ACS.n

        def emit_transposes(src_f32, dst_bf_fn, nk, banks=(6, 7), pre_wait=None):
            ngrp = nk // 4
            last_ev = None
            for g_ in range(ngrp):
                bank = banks[g_ % 2]

                def ftr(tensor, g_=g_, bank=bank, pfree=mx_tpev[bank],
                        pre_wait=pre_wait):
                    if g_ == 0 and pre_wait is not None:
                        pre_wait(tensor)
                    if pfree is not None:
                        tensor.wait_ge(EV.h, pfree)
                    for j_ in range(4):
                        k_ = 4 * g_ + j_
                        inst = tensor.transpose(
                            pb[bank][:, 128 * j_:128 * (j_ + 1)],
                            src_f32[:, 128 * k_:128 * (k_ + 1)], eyeF_s[:])
                    inst.then_inc(MM.h, 1)
                blk.tensor(ftr)
                MM.n += 1
                mmt = MM.n

                def fev(vector, g_=g_, bank=bank, mmt=mmt):
                    vector.wait_ge(MM.h, mmt)
                    vector.tensor_copy(dst_bf_fn(g_), pb[bank][:, :]
                                       ).then_inc(EV.h, 1)
                blk.vector(fev)
                EV.n += 1
                mx_tpev[bank] = EV.n
                last_ev = EV.n
            return last_ev

        mx_scr2 = [mx_scr, vF(49408, 2048)]
        mx_free = {}
        mx_lastT = [None, None]
        mx_xsfree = [None]
        mx_xsTmm = [None, None]
        mx_kvmm = [None]
        mx_wld = [None, None]
        mx_psev = [None] * 4
        mx_tpev = {6: None, 7: None}

        for ms in range(NMST):
            tiles = [2 * ms, 2 * ms + 1]

            for tloc, t in enumerate(tiles):
                def fso(sync, t=t, tloc=tloc, fr=mx_free.get(("own", tloc))):
                    if fr is not None:
                        sync.wait_ge(ACS.h, fr)
                    sync.dma_start(out=mx_xo[tloc][:],
                                   in_=x_own[128 * t:128 * (t + 1), :]
                                   ).then_inc(MXX[tloc].h, 16)
                blk.sync(fso)
                MXX[tloc].n += 16

                def fst2(sync, t=t, tloc=tloc, fr=mx_xsfree[0]):
                    if fr is not None:
                        sync.wait_ge(ACS.h, fr)
                    sync.dma_start(out=mx_xs[tloc][:],
                                   in_=x_oth[128 * t:128 * (t + 1), :]
                                   ).then_inc(MXO[tloc].h, 16)
                blk.sync(fst2)
                MXO[tloc].n += 16
            LD_OWN = [MXX[0].n, MXX[1].n]
            LD_OTH = [MXO[0].n, MXO[1].n]

            emit_pp(24)
            emit_ready_tcopy()

            ev_all = EV.n

            def fstats4(vector, ev_all=ev_all, lo=tuple(LD_OWN), lt=tuple(LD_OTH)):
                vector.wait_ge(EV.h, ev_all)
                vector.wait_ge(MXX[0].h, lo[0])
                ln_stats(vector, mx_xo[0][:], 0)
                vector.drain()
                vector.wait_ge(MXX[1].h, lo[1])
                ln_stats(vector, mx_xo[1][:], 1)
                vector.drain()
                vector.wait_ge(MXO[0].h, lt[0])
                ln_stats(vector, mx_xs[0][:], 2)
                vector.drain()
                vector.wait_ge(MXO[1].h, lt[1])
                ln_stats(vector, mx_xs[1][:], 3).then_inc(EV.h, 1)
            blk.vector(fstats4)
            EV.n += 1
            evs = EV.n

            def fsc4(scalar, evs=evs):
                scalar.wait_ge(EV.h, evs)
                for s_ in range(4):
                    inst = ln_rsig(scalar, s_)
                    if s_ < 3:
                        scalar.drain()
                inst.then_inc(ACS.h, 1)
            blk.scalar(fsc4)
            ACS.n += 1
            acs4 = ACS.n

            def fnm4(vector, acs4=acs4):
                vector.wait_ge(ACS.h, acs4)
                for s_ in range(4):
                    inst = ln_nmur(vector, s_)
                inst.then_inc(EV.h, 1)
            blk.vector(fnm4)
            EV.n += 1
            env4 = EV.n

            def fc4(scalar, env4=env4, w0=mx_lastT[0], w1=mx_lastT[1]):
                scalar.wait_ge(EV.h, env4)
                if w0 is not None:
                    scalar.wait_ge(MM.h, max(w0, w1))
                ln_norm(scalar, mx_xo[0][:], mx_scr2[0][:], 0)
                ln_norm(scalar, mx_xo[1][:], mx_scr2[1][:], 1)
                ln_norm(scalar, mx_xs[0][:], mx_xs[0][:], 2)
                inst = ln_norm(scalar, mx_xs[1][:], mx_xs[1][:], 3)
                inst.then_inc(ACS.h, 1)
            blk.scalar(fc4)
            ACS.n += 1
            norm_ac = ACS.n
            mx_free[("own", 0)] = norm_ac
            mx_free[("own", 1)] = norm_ac

            def fvh(vector, norm_ac=norm_ac):
                vector.wait_ge(ACS.h, norm_ac)
                for tloc in range(2):
                    sc = mx_scr2[tloc]
                    vector.tensor_tensor(sc[:], sc[:], A1_bc[:], OP.mult)
                for tloc in range(2):
                    xv = mx_xs[tloc]
                    vector.tensor_tensor(xv[:], xv[:], A1_bc[:], OP.mult)
                vector.drain()
                for tloc in range(2):
                    sc = mx_scr2[tloc]
                    vector.tensor_add(sc[:], sc[:], A0_bc[:])
                for tloc in range(2):
                    xv = mx_xs[tloc]
                    vector.tensor_add(xv[:], xv[:], A0_bc[:])
                vector.drain()
                for tloc in range(2):
                    inst = vector.tensor_tensor(mx_hos[tloc][:], mx_scr2[tloc][:],
                                                omg_bc[:], OP.mult)
                inst.then_inc(EV.h, 1)
            blk.vector(fvh)
            EV.n += 1
            hmod_ev = EV.n

            for tloc in range(2):
                def prew(tensor, hmod_ev=hmod_ev):
                    tensor.wait_ge(EV.h, hmod_ev)
                emit_transposes(
                    mx_xs[tloc][:],
                    lambda g_, tloc=tloc: mxb_hsT[:, tloc, 4 * g_:4 * (g_ + 1), :
                                                  ].rearrange("p a c -> p (a c)"),
                    KC, pre_wait=prew)
                mx_xsTmm[tloc] = MM.n
            HST_EV = EV.n

            for ec in range(8):
                slot = ec % 2

                def fsw(sync, ms=ms, ec=ec, slot=slot, wld=mx_wld[slot]):
                    sync.wait_ge(SCW[0].h, FOLD0)
                    sync.wait_ge(SCW[1].h, FOLD1)
                    if wld is not None:
                        sync.wait_ge(MM.h, wld)
                    src_ = scr_wg.rearrange("(k p) (e f) -> p k e f",
                                            p=128, f=256)[:, :, ec, :]
                    sync.dma_start(out=mxb_w[slot][:, :, 0:256], in_=src_
                                   ).then_inc(WTS[slot].h, 16)
                blk.sync(fsw)
                WTS[slot].n += 16
                wldt = WTS[slot].n

                for tloc in range(2):
                    bank = 2 * (ec % 2) + tloc

                    def ft(tensor, tloc=tloc, ec=ec, slot=slot, bank=bank,
                           wldt=wldt, hst=HST_EV, psev=mx_psev[bank]):
                        tensor.wait_ge(WTS[slot].h, wldt)
                        tensor.wait_ge(EV.h, hst)
                        if psev is not None:
                            tensor.wait_ge(EV.h, psev)
                        for k in range(KC):
                            inst = tensor.matmul(pb[bank][:, 0:256],
                                                 mxb_hsT[:, tloc, k, :],
                                                 mxb_w[slot][:, k, 0:256],
                                                 start=(k == 0),
                                                 stop=(k == KC - 1))
                        inst.then_inc(MM.h, 1)
                    blk.tensor(ft)
                    MM.n += 1
                    if tloc == 1:
                        mx_wld[slot] = MM.n
                    mmt = MM.n

                    def fv(vector, tloc=tloc, ec=ec, bank=bank, mmt=mmt,
                           tmm=mx_xsTmm[tloc]):
                        vector.wait_ge(MM.h, max(mmt, tmm))
                        dst = mx_xs[tloc][:, 256 * ec:256 * (ec + 1)]
                        vector.tensor_add(dst, pb[bank][:, 0:256],
                                          mx_hos[tloc][:, 256 * ec:256 * (ec + 1)]
                                          ).then_inc(EV.h, 1)
                    blk.vector(fv)
                    EV.n += 1
                    mx_psev[bank] = EV.n
            XS_EV = EV.n

            ev_all2 = EV.n

            def fstats2(vector, ev_all2=ev_all2):
                vector.wait_ge(EV.h, ev_all2)
                ln_stats(vector, mx_xs[0][:], 0)
                vector.drain()
                ln_stats(vector, mx_xs[1][:], 1).then_inc(EV.h, 1)
            blk.vector(fstats2)
            EV.n += 1
            evs2 = EV.n

            def fsc2(scalar, evs2=evs2):
                scalar.wait_ge(EV.h, evs2)
                ln_rsig(scalar, 0)
                scalar.drain()
                ln_rsig(scalar, 1).then_inc(ACS.h, 1)
            blk.scalar(fsc2)
            ACS.n += 1
            acs2 = ACS.n

            def fnm2(vector, acs2=acs2):
                vector.wait_ge(ACS.h, acs2)
                ln_nmur(vector, 0)
                ln_nmur(vector, 1).then_inc(EV.h, 1)
            blk.vector(fnm2)
            EV.n += 1
            env2 = EV.n

            def fc2(scalar, env2=env2, w0=mx_lastT[0], w1=mx_lastT[1]):
                scalar.wait_ge(EV.h, env2)
                if w0 is not None:
                    scalar.wait_ge(MM.h, max(w0, w1))
                ln_norm(scalar, mx_xs[0][:], mx_scr2[0][:], 0)
                ln_norm(scalar, mx_xs[1][:], mx_scr2[1][:], 1).then_inc(ACS.h, 1)
            blk.scalar(fc2)
            ACS.n += 1
            xn_ac = ACS.n
            mx_xsfree[0] = xn_ac

            def fvxn(vector, xn_ac=xn_ac):
                vector.wait_ge(ACS.h, xn_ac)
                for tloc in range(2):
                    sc = mx_scr2[tloc]
                    vector.tensor_tensor(sc[:], sc[:], lnw_bc[:], OP.mult)
                vector.drain()
                for tloc in range(2):
                    sc = mx_scr2[tloc]
                    inst = vector.tensor_add(sc[:], sc[:], lnb_bc[:])
                inst.then_inc(EV.h, 1)
            blk.vector(fvxn)
            EV.n += 1
            xn_ev = EV.n

            for tloc, t in enumerate(tiles):
                def prew2(tensor, xn_ev=xn_ev):
                    tensor.wait_ge(EV.h, xn_ev)
                xnt_ev = emit_transposes(
                    mx_scr2[tloc][:],
                    lambda g_: mxb_xnT[:, 4 * g_:4 * (g_ + 1), :
                                       ].rearrange("p a c -> p (a c)"),
                    KC, pre_wait=prew2)
                mx_lastT[tloc] = MM.n

                kvb = 4 + tloc

                def ftkv(tensor, xnt_ev=xnt_ev, kvb=kvb, wk=mx_kvmm[0]):
                    tensor.wait_ge(EV.h, xnt_ev)
                    tensor.wait_ge(GLD.h, 16)
                    for k in range(KC):
                        inst = tensor.matmul(pb[kvb][:, 0:128], mxb_xnT[:, k, :],
                                             WkvTb[:, k, :],
                                             start=(k == 0), stop=(k == KC - 1))
                    inst.then_inc(MM.h, 1)
                blk.tensor(ftkv)
                MM.n += 1
                kv_mm = MM.n
                mx_kvmm[0] = kv_mm

                b = 128 + 64 * tloc
                w1 = lnstat[:, b:b + 64]

                def fsg(scalar, kv_mm=kv_mm, kvb=kvb, w1=w1):
                    scalar.wait_ge(MM.h, kv_mm)
                    scalar.activation(out=w1, in_=pb[kvb][:, 0:S],
                                      func=AF.Exp, scale=-1.0).then_inc(ACS.h, 1)
                blk.scalar(fsg)
                ACS.n += 1
                sg_ac = ACS.n

                def fu(vector, sg_ac=sg_ac, w1=w1, tloc=tloc):
                    vector.wait_ge(ACS.h, sg_ac)
                    vector.tensor_scalar_add(w1, w1, 1.0)
                    vector.drain()
                    vector.reciprocal(mx_w2[:], w1)
                    vector.drain()
                    vector.tensor_tensor(w1, mx_w2[:], etf_bc[:], OP.mult
                                         ).then_inc(EV.h, 1)
                blk.vector(fu)
                EV.n += 1
                u_ev = EV.n

                def fe(scalar, u_ev=u_ev, w1=w1):
                    scalar.wait_ge(EV.h, u_ev)
                    scalar.activation(out=mx_w2[:], in_=w1, func=AF.Exp,
                                      scale=-1.0).then_inc(ACS.h, 1)
                blk.scalar(fe)
                ACS.n += 1
                e_ac = ACS.n

                def fwkv(vector, e_ac=e_ac, w1=w1, kvb=kvb):
                    vector.wait_ge(ACS.h, e_ac)
                    vector.tensor_tensor(w1, mx_w2[:], pb[kvb][:, S:128],
                                         OP.mult).then_inc(EV.h, 1)
                blk.vector(fwkv)
                EV.n += 1
                wkv_ev = EV.n

                trb = 6 + tloc

                def ftr2(tensor, wkv_ev=wkv_ev, w1=w1, trb=trb,
                         pfree=mx_tpev[6 + tloc]):
                    tensor.wait_ge(EV.h, wkv_ev)
                    if pfree is not None:
                        tensor.wait_ge(EV.h, pfree)
                    tensor.transpose(pb[trb][0:S, 0:128], w1[:, 0:S],
                                     eyeF_s[:]).then_inc(MM.h, 1)
                blk.tensor(ftr2)
                MM.n += 1
                tr_mm = MM.n

                def fev2(vector, t=t, tr_mm=tr_mm, trb=trb):
                    vector.wait_ge(MM.h, tr_mm)
                    vector.tensor_copy(wkvT_s[:, 128 * t:128 * (t + 1)],
                                       pb[trb][0:S, 0:128]).then_inc(EV.h, 1)
                blk.vector(fev2)
                EV.n += 1
                mx_tpev[6 + tloc] = EV.n
        emit_pp(len(pp_pieces))
        emit_ready_tcopy()
        assert not tc_jobs and pp_next[0] == len(pp_pieces)

        barrier()

        @blk.vector
        def _(vector):
            vector.tensor_tensor_scan(
                scanbuf[:, :], wcol[:, 0:1].broadcast_to([S, NT]),
                wkvT_s[:, :], 0.0, op0=OP.mult, op1=OP.add).then_inc(EV.h, 1)
        EV.n += 1
        SC1 = EV.n

        @blk.sync
        def _(sync):
            sync.wait_ge(EV.h, SC1)
            sync.dma_start(out=ag_car_src, in_=scanbuf[:, NT - 1:NT]
                           ).then_inc(CAR.h, 16)
        CAR.n += 16

        @blk.gpsimd
        def _(gpsimd):
            gpsimd.wait_ge(CAR.h, CAR.n)
            gpsimd.collective_compute(
                "AllGather", OP.bypass,
                replica_groups=[[0, 1, 2, 3], [4, 5, 6, 7]],
                ins=[ag_car_src], outs=[ag_car_dst],
            ).then_inc(CC.h, 1)
        CC.n += 1

        @blk.sync
        def _(sync):
            sync.wait_ge(CC.h, CC.n)
            sync.dma_start(out=carw[:], in_=ag_car_dst).then_inc(LD_A.h, 16)
        LD_A.n += 16
        CAR_LD = LD_A.n

        @blk.vector
        def _(vector):
            vector.wait_ge(LD_A.h, CAR_LD)
            vector.tensor_tensor(qsc[0:4, 0:64], tdB_s[:], maskE_s[:], OP.mult
                                 ).then_inc(EV.h, 1)
        EV.n += 1

        @blk.scalar
        def _(scalar):
            scalar.wait_ge(EV.h, EV.n)
            scalar.activation(out=qsc[0:4, 64:128], in_=qsc[0:4, 0:64],
                              func=AF.Exp).then_inc(ACS.h, 1)
        ACS.n += 1

        @blk.vector
        def _(vector):
            vector.wait_ge(ACS.h, ACS.n)
            vector.tensor_tensor(qsc[0:4, 0:64], qsc[0:4, 64:128], maskM_s[:],
                                 OP.mult)
            vector.drain()
            vector.tensor_tensor(qsc[0:4, 64:128], qsc[0:4, 0:64], carw[:],
                                 OP.mult).then_inc(EV.h, 1)
        EV.n += 1
        CARW_EV = EV.n

        @blk.tensor
        def _(tensor):
            tensor.wait_ge(EV.h, CARW_EV)
            tensor.matmul(pb[6][0:S, 0:1], qsc[0:4, 64:128], ones_cs[0:4, 0:1],
                          start=True, stop=True).then_inc(MM.h, 1)
        MM.n += 1
        CIN_MM = MM.n

        @blk.vector
        def _(vector):
            vector.wait_ge(MM.h, CIN_MM)
            vector.tensor_copy(cin_col[:], pb[6][0:S, 0:1]).then_inc(EV.h, 1)
            vector.drain()
            vector.tensor_tensor_scan(
                statesB[:, :], wcol[:, 0:1].broadcast_to([S, NT]),
                wkvT_s[:, :], cin_col[:, 0:1], op0=OP.mult, op1=OP.add
            ).then_inc(EV.h, 1)
        EV.n += 2

        b5_mm = [None, None]
        b5_ev = [None, None]

        def bcast_bf(vec_j0, target, add_one):
            for c in range(4):
                j = vec_j0 + 512 * c
                r, off = j // JSH, j % JSH
                slot = c % 2

                def fs(sync, r=r, off=off, slot=slot, mw=b5_mm[slot]):
                    if mw is not None:
                        sync.wait_ge(MM.h, mw)
                    sync.dma_start(out=mst[slot][:],
                                   in_=ag_mod_dst[r:r + 1, off:off + 512]
                                   ).then_inc(MST_S[slot].h, 16)
                blk.sync(fs)
                MST_S[slot].n += 16
                ldt = MST_S[slot].n

                def ft(tensor, slot=slot, ldt=ldt, evw=b5_ev[slot]):
                    tensor.wait_ge(MST_S[slot].h, ldt)
                    if evw is not None:
                        tensor.wait_ge(EV.h, evw)
                    tensor.matmul(pb[slot][:, :], ones_rs[0:1, :], mst[slot][:],
                                  start=True, stop=True).then_inc(MM.h, 1)
                blk.tensor(ft)
                MM.n += 1
                b5_mm[slot] = MM.n
                mmt = MM.n

                def fv(vector, slot=slot, mmt=mmt, c=c, target=target,
                       add_one=add_one):
                    vector.wait_ge(MM.h, mmt)
                    dst = target[:, 512 * c:512 * (c + 1)]
                    if add_one:
                        vector.tensor_scalar_add(dst, pb[slot][:, :], 1.0
                                                 ).then_inc(EV.h, 1)
                    else:
                        vector.tensor_copy(dst, pb[slot][:, :]).then_inc(EV.h, 1)
                blk.vector(fv)
                EV.n += 1
                b5_ev[slot] = EV.n

        bcast_bf(8192, B1b, True)
        bcast_bf(6144, B0b, False)

        barrier()

        emit_down_tcopy()
        TCT0, TCT1 = TCW[0].n, TCW[1].n

        e_guev = [None] * 4
        e_psev = [None] * 8
        e_dnmm = [None, None]
        e_dpsev = [None, None]
        e_ptev = [None]
        e_atev = [None]
        e_xoev = [None]
        d_xofree = [None, None]
        d_woev = [None, None]
        d_h2qtp2 = [None, None]
        d_xcac = [None, None]
        d_h2sc = [None]

        for st in range(NST):
            ST_EV = EV.n
            ST_Y = YST.n
            for tt in range(TPS):
                t = TPS * st + tt
                par = tt % 2
                xoh2_p = d_xoh2 if par == 0 else d_xoh2b
                x2_p = d_x2 if par == 0 else d_x2b
                h2q_p = d_h2q if par == 0 else d_h2qb
                qb = 192 * par

                def fs(sync, t=t, xofree=d_xofree[par], tt=tt, xoh2_p=xoh2_p):
                    sync.wait_ge(EV.h, ST_EV)
                    if tt == 0:
                        sync.wait_ge(YST.h, ST_Y)
                    if xofree is not None:
                        sync.wait_ge(GP.h, xofree)
                    sync.dma_start(out=xoh2_p[:],
                                   in_=x_own[128 * t:128 * (t + 1), :]
                                   ).then_inc(DXO.h, 16)
                blk.sync(fs)
                DXO.n += 16
                xo_ld = DXO.n

                for c in range(4):
                    def ft(tensor, t=t, c=c, woev=d_woev[par], par=par):
                        if c == 0:
                            tensor.wait_ge(EV.h, ST_EV)
                            if woev is not None:
                                tensor.wait_ge(EV.h, woev)
                        tensor.matmul(pb[4 * par + c][:, :],
                                      statesB[:, 128 * t:128 * (t + 1)],
                                      WoTga[:, 512 * c:512 * (c + 1)],
                                      start=True, stop=True).then_inc(MM.h, 1)
                    blk.tensor(ft)
                    MM.n += 1
                WO_MM = MM.n

                def fv(vector, xo_ld=xo_ld, WO_MM=WO_MM, tt=tt, par=par,
                       x2_p=x2_p, xoh2_p=xoh2_p, xcg=d_xcac[par]):
                    if tt == 0:
                        vector.wait_ge(YST.h, ST_Y)
                    if xcg is not None:
                        vector.wait_ge(ACS.h, xcg)
                    vector.wait_ge(DXO.h, xo_ld)
                    vector.wait_ge(MM.h, WO_MM)
                    for c in range(4):
                        vector.tensor_add(x2_p[:, 512 * c:512 * (c + 1)],
                                          pb[4 * par + c][:, :],
                                          xoh2_p[:, 512 * c:512 * (c + 1)])
                    vector.engine_nop().then_inc(EV.h, 1)
                blk.vector(fv)
                EV.n += 1
                X2_EV = EV.n
                d_woev[par] = X2_EV

                xc_ac = ln_chain(x2_p[:], d_scr[:], wait_ev=X2_EV, slot=par,
                                 skip_evall=True, wait_ac=d_xcac[par],
                                 fc_wait_ev=d_h2sc[0])
                d_xcac[par] = xc_ac

                def fv4(vector, xc_ac=xc_ac, h2q_prev=d_h2qtp2[par],
                        xoh2_p=xoh2_p, qb=qb, par=par):
                    vector.wait_ge(ACS.h, xc_ac)
                    if h2q_prev is not None:
                        vector.wait_ge((H2QT if par == 0 else H2QT1).h,
                                       h2q_prev[0])
                        vector.wait_ge((H2QS if par == 0 else H2QS1).h,
                                       h2q_prev[1])
                    vector.tensor_tensor(d_scr[:], d_scr[:], B1b[:], OP.mult)
                    vector.drain()
                    vector.tensor_add(xoh2_p[:], d_scr[:], B0b[:])
                    vector.drain()
                    h3 = xoh2_p[:].rearrange("p (b c) -> p b c", c=32)
                    vector.tensor_reduce(qsc[:, qb:qb + 64], h3, AX.X, OP.max,
                                         apply_absolute_value=True)
                    vector.drain()
                    vector.tensor_scalar(qsc[:, qb + 128:qb + 192],
                                         qsc[:, qb:qb + 64],
                                         2.0 * C448I, 2.0 * EPS30, OP.mult, OP.max)
                    vector.drain()
                    vector.reciprocal(qsc[:, qb + 64:qb + 128],
                                      qsc[:, qb + 128:qb + 192]
                                      ).then_inc(EV.h, 1)
                blk.vector(fv4)
                EV.n += 1
                h2sc_ev = EV.n
                d_h2sc[0] = h2sc_ev

                def fgq4(gpsimd, h2sc_ev=h2sc_ev, par=par, xoh2_p=xoh2_p,
                         h2q_p=h2q_p, qb=qb):
                    gpsimd.wait_ge(EV.h, h2sc_ev)
                    h3g = xoh2_p[:].rearrange("p (b c) -> p b c", c=32)
                    q3 = q8buf[:, 2048 * par:2048 * (par + 1)
                               ].rearrange("p (b c) -> p b c", c=32)
                    i3 = qsc[:, qb + 64:qb + 128].rearrange("p (b o) -> p b o",
                                                            o=1)
                    gpsimd.tensor_tensor(q3, h3g, i3.broadcast_to([128, 64, 32]),
                                         OP.mult)
                    gpsimd.drain()
                    s3 = qsc[:, qb + 128:qb + 192].rearrange("p (b o) -> p b o",
                                                             o=1)
                    d3 = h2q_p[:].rearrange("p (b c) -> p b c", c=32)
                    gpsimd.tensor_tensor(d3, q3, s3.broadcast_to([128, 64, 32]),
                                         OP.mult).then_inc(GP.h, 1)
                blk.gpsimd(fgq4)
                GP.n += 1
                h2q_gp = GP.n
                d_xofree[par] = h2q_gp

                def ftp(eng, tt=tt, h2q_ev=h2q_gp, par2=0, h2q_p=h2q_p,
                        sem=None):
                    eng.wait_ge(GP.h, h2q_ev)
                    for k in range(par2, KC, 2):
                        eng.dma_start(out=h2qT[:, k, tt, :],
                                      in_=h2q_p[:, 128 * k:128 * (k + 1)],
                                      transpose=True).then_inc(sem, 16)
                semT = H2QT if par == 0 else H2QT1
                semS = H2QS if par == 0 else H2QS1
                blk.scalar(lambda scalar, tt=tt, h2q_ev=h2q_gp, h2q_p=h2q_p,
                           semT=semT:
                           ftp(scalar, tt, h2q_ev, 0, h2q_p, semT.h))
                blk.sync(lambda sync, tt=tt, h2q_ev=h2q_gp, h2q_p=h2q_p,
                         semS=semS:
                         ftp(sync, tt, h2q_ev, 1, h2q_p, semS.h))
                semT.n += 8 * 16
                semS.n += 8 * 16
                d_h2qtp2[par] = (semT.n, semS.n)
            H2QT_TP = (H2QT.n, H2QS.n, H2QT1.n, H2QS1.n)

            barrier()

            e_silev = [None]
            e_gpq = [None]
            e_dqev = [None]
            for o in range(ICH):
                gslot = o % 2
                uslot = 2 + o % 2

                def fsg(sync, o=o, gslot=gslot, guev=e_guev[gslot]):
                    sync.wait_ge(TCW[0].h, TCT0)
                    sync.wait_ge(TCW[1].h, TCT1)
                    if guev is not None:
                        sync.wait_ge(EV.h, guev)
                    src = scrT_g[o].rearrange("(k p) c -> p k c", p=128)
                    sync.dma_start(out=e_gu[gslot][:], in_=src
                                   ).then_inc(GU[gslot].h, 16)
                blk.sync(fsg)
                GU[gslot].n += 16
                gld = GU[gslot].n

                def fsu(sync, o=o, uslot=uslot, guev=e_guev[uslot]):
                    sync.wait_ge(TCW[0].h, TCT0)
                    sync.wait_ge(TCW[1].h, TCT1)
                    if guev is not None:
                        sync.wait_ge(EV.h, guev)
                    src = scrT_u[o].rearrange("(k p) c -> p k c", p=128)
                    sync.dma_start(out=e_gu[uslot][:], in_=src
                                   ).then_inc(GU[uslot].h, 16)
                blk.sync(fsu)
                GU[uslot].n += 16
                uld = GU[uslot].n

                bg = 4 + (o % 2)
                bu = 6 + (o % 2)

                def ft(tensor, o=o, gslot=gslot, uslot=uslot, bg=bg, bu=bu,
                       gld=gld, uld=uld, h2qt=H2QT_TP,
                       pg=e_psev[bg], pu=e_psev[bu]):
                    tensor.wait_ge(H2QT.h, h2qt[0])
                    tensor.wait_ge(H2QS.h, h2qt[1])
                    tensor.wait_ge(H2QT1.h, h2qt[2])
                    tensor.wait_ge(H2QS1.h, h2qt[3])
                    tensor.wait_ge(GU[gslot].h, gld)
                    if pg is not None:
                        tensor.wait_ge(EV.h, pg)
                    for k in range(KC):
                        inst = tensor.matmul(
                            pb[bg][:, :], e_gu[gslot][:, k, :],
                            h2qT[:, k, :, :].rearrange("p t c -> p (t c)"),
                            start=(k == 0), stop=(k == KC - 1))
                    inst.then_inc(MM.h, 1)
                    tensor.wait_ge(GU[uslot].h, uld)
                    if pu is not None:
                        tensor.wait_ge(EV.h, pu)
                    for k in range(KC):
                        inst = tensor.matmul(
                            pb[bu][:, :], e_gu[uslot][:, k, :],
                            h2qT[:, k, :, :].rearrange("p t c -> p (t c)"),
                            start=(k == 0), stop=(k == KC - 1))
                    inst.then_inc(MM.h, 1)
                blk.tensor(ft)
                MM.n += 2
                gu_mm = MM.n

                def fsil(scalar, bg=bg, gu_mm=gu_mm, sev=e_silev[0]):
                    scalar.wait_ge(MM.h, gu_mm)
                    if sev is not None:
                        scalar.wait_ge(EV.h, sev)
                    scalar.activation(out=e_sil[:], in_=pb[bg][:, :],
                                      func=AF.Silu).then_inc(ACS.h, 1)
                blk.scalar(fsil)
                ACS.n += 1
                sil_ac = ACS.n

                def fdance1(vector, o=o, bg=bg, bu=bu, sil_ac=sil_ac,
                            gpw=e_gpq[0]):
                    vector.wait_ge(ACS.h, sil_ac)
                    vector.tensor_tensor(e_m[:], e_sil[:], pb[bu][:, :], OP.mult
                                         ).then_inc(EV.h, 1)
                    vector.drain()
                    if gpw is not None:
                        vector.wait_ge(GP.h, gpw)
                    vector.transpose(e_Bv[:], e_m[:])
                    vector.drain()
                    B3 = e_Bv[:].rearrange("p (b c) -> p b c", c=32)
                    vector.tensor_reduce(qsc[:, 0:16], B3, AX.X, OP.max,
                                         apply_absolute_value=True)
                    vector.drain()
                    vector.tensor_scalar(qsc[:, 32:48], qsc[:, 0:16],
                                         2.0 * C448I, 2.0 * EPS30, OP.mult, OP.max)
                    vector.drain()
                    vector.reciprocal(qsc[:, 16:32], qsc[:, 32:48]
                                      ).then_inc(EV.h, 1)
                blk.vector(fdance1)
                EV.n += 2
                e_silev[0] = EV.n - 1
                e_psev[bg] = EV.n - 1
                e_psev[bu] = EV.n - 1
                e_guev[gslot] = EV.n - 1
                e_guev[uslot] = EV.n - 1
                sc_ev = EV.n

                def fgq(gpsimd, o=o, sc_ev=sc_ev, dqw=e_dqev[0]):
                    gpsimd.wait_ge(EV.h, sc_ev)
                    if dqw is not None:
                        gpsimd.wait_ge(EV.h, dqw)
                    B3 = e_Bv[:].rearrange("p (b c) -> p b c", c=32)
                    q3 = q8buf[:, 0:512].rearrange("p (b c) -> p b c", c=32)
                    i3 = qsc[:, 16:32].rearrange("p (b o) -> p b o", o=1)
                    gpsimd.tensor_tensor(q3, B3, i3.broadcast_to([128, 16, 32]),
                                         OP.mult)
                    gpsimd.drain()
                    s3 = qsc[:, 32:48].rearrange("p (b o) -> p b o", o=1)
                    d3 = e_dqB[:].rearrange("p (b c) -> p b c", c=32)
                    gpsimd.tensor_tensor(d3, q3, s3.broadcast_to([128, 16, 32]),
                                         OP.mult).then_inc(GP.h, 1)
                blk.gpsimd(fgq)
                GP.n += 1
                e_gpq[0] = GP.n
                gq_gp = GP.n

                def fdance2(vector, o=o, gq_gp=gq_gp):
                    vector.wait_ge(GP.h, gq_gp)
                    vector.transpose(mqfm[:, o, :], e_dqB[:]).then_inc(EV.h, 1)
                blk.vector(fdance2)
                EV.n += 1
                e_dqev[0] = EV.n

            barrier()

            for dgrp in range(4):
                for dj in range(4):
                    d128 = 4 * dgrp + dj
                    accb = dj % 2
                    for iq in range(4):
                        dslot = iq % 2

                        def fsd(sync, d128=d128, iq=iq, dslot=dslot,
                                dnm=e_dnmm[dslot]):
                            sync.wait_ge(TCW[0].h, TCT0)
                            sync.wait_ge(TCW[1].h, TCT1)
                            if dnm is not None:
                                sync.wait_ge(MM.h, dnm)
                            src = scrT_d[d128, 2048 * iq:2048 * (iq + 1), :]
                            sync.dma_start(
                                out=e_dn[dslot][:],
                                in_=src.rearrange("(k p) c -> p k c", p=128),
                            ).then_inc(DN[dslot].h, 16)
                        blk.sync(fsd)
                        DN[dslot].n += 16
                        dld = DN[dslot].n

                        def ftd(tensor, iq=iq, dslot=dslot, accb=accb, dld=dld,
                                dpe=e_dpsev[accb]):
                            tensor.wait_ge(DN[dslot].h, dld)
                            if iq == 0 and dpe is not None:
                                tensor.wait_ge(EV.h, dpe)
                            for kk in range(KC):
                                isl = 16 * iq + kk
                                inst = tensor.matmul(
                                    pb[accb][:, :], e_dn[dslot][:, kk, :],
                                    mqfm[:, isl, :],
                                    start=(iq == 0 and kk == 0),
                                    stop=(iq == 3 and kk == KC - 1))
                            inst.then_inc(MM.h, 1)
                        blk.tensor(ftd)
                        MM.n += 1
                        e_dnmm[dslot] = MM.n
                    dn_mm = MM.n

                    def fev(vector, accb=accb, dn_mm=dn_mm):
                        vector.wait_ge(MM.h, dn_mm)
                        vector.tensor_copy(e_rot[:], pb[accb][:, :]
                                           ).then_inc(EV.h, 1)
                    blk.vector(fev)
                    EV.n += 1
                    e_dpsev[accb] = EV.n
                    rot_ev = EV.n

                    def ftt(tensor, dj=dj, rot_ev=rot_ev, pte=e_ptev[0]):
                        tensor.wait_ge(EV.h, rot_ev)
                        if dj == 0 and pte is not None:
                            tensor.wait_ge(EV.h, pte)
                        for tt in range(TPS):
                            inst = tensor.transpose(
                                pb[2 + tt][:, 128 * dj:128 * (dj + 1)],
                                e_rot[:, 128 * tt:128 * (tt + 1)], eyeF_s[:])
                        inst.then_inc(MM.h, 1)
                    blk.tensor(ftt)
                    MM.n += 1
                PT_MM = MM.n

                for tt in range(TPS):
                    t = TPS * st + tt

                    def fso(sync, t=t, dgrp=dgrp, xoev=e_xoev[0]):
                        if xoev is not None:
                            sync.wait_ge(EV.h, xoev)
                        sync.dma_start(
                            out=e_sil[:],
                            in_=x_own[128 * t:128 * (t + 1),
                                      512 * dgrp:512 * (dgrp + 1)],
                        ).then_inc(XOD.h, 16)
                    blk.sync(fso)
                    XOD.n += 16
                    xod_ld = XOD.n

                    def fta(tensor, t=t, dgrp=dgrp, atev=e_atev[0]):
                        if atev is not None:
                            tensor.wait_ge(EV.h, atev)
                        tensor.matmul(pb[6][:, :],
                                      statesB[:, 128 * t:128 * (t + 1)],
                                      WoTga[:, 512 * dgrp:512 * (dgrp + 1)],
                                      start=True, stop=True).then_inc(MM.h, 1)
                    blk.tensor(fta)
                    MM.n += 1
                    at_mm = MM.n

                    def ffin(vector, tt=tt, at_mm=at_mm, xod_ld=xod_ld,
                             PT_MM=PT_MM, yprev=YST.n):
                        vector.wait_ge(MM.h, max(at_mm, PT_MM))
                        vector.wait_ge(XOD.h, xod_ld)
                        if yprev:
                            vector.wait_ge(YST.h, yprev)
                        vector.tensor_add(e_x2r[:], pb[6][:, :], e_sil[:]
                                          ).then_inc(EV.h, 1)
                        vector.drain()
                        vector.tensor_add(e_fin[:], pb[2 + tt][:, :], e_x2r[:]
                                          ).then_inc(EV.h, 1)
                    blk.vector(ffin)
                    EV.n += 2
                    e_atev[0] = EV.n - 1
                    e_xoev[0] = EV.n - 1
                    fin_ev = EV.n

                    def fst(sync, t=t, dgrp=dgrp, fin_ev=fin_ev):
                        sync.wait_ge(EV.h, fin_ev)
                        sync.dma_start(
                            out=y[128 * t:128 * (t + 1),
                                  512 * dgrp:512 * (dgrp + 1)],
                            in_=e_fin[:]).then_inc(YST.h, 16)
                    blk.sync(fst)
                    YST.n += 16

                e_ptev[0] = EV.n

            barrier()

        @blk.sync
        def _(sync):
            sync.wait_ge(YST.h, YST.n)
        barrier()

    return nc


_NC_CACHE = {}


def _get_nc():
    if "nc" not in _NC_CACHE:
        _NC_CACHE["nc"] = build_nc()
    return _NC_CACHE["nc"]


def kernel(L0_attn_x, L0_attn_Wk, L0_attn_Wv, L0_attn_Wo, L0_attn_Wts,
           L0_attn_gate, L0_attn_time_decay, L0_attn_time_first,
           L0_attn_ln_w, L0_attn_ln_b, L0_ffn_gate_w, L0_ffn_up_w,
           L0_ffn_down_w, L0_c, L0_wmod):
    f32 = np.float32
    x = np.ascontiguousarray(np.asarray(L0_attn_x, dtype=f32))
    Wk = np.asarray(L0_attn_Wk, dtype=f32)
    Wv = np.asarray(L0_attn_Wv, dtype=f32)
    Wo = np.asarray(L0_attn_Wo, dtype=f32)
    Wts = np.asarray(L0_attn_Wts, dtype=f32)
    gate = np.asarray(L0_attn_gate, dtype=f32)
    td = np.asarray(L0_attn_time_decay, dtype=f32)
    tf = np.asarray(L0_attn_time_first, dtype=f32)
    lnw = np.asarray(L0_attn_ln_w, dtype=f32)
    lnb = np.asarray(L0_attn_ln_b, dtype=f32)
    gwn = np.ascontiguousarray(np.asarray(L0_ffn_gate_w, dtype=f32))
    uwn = np.ascontiguousarray(np.asarray(L0_ffn_up_w, dtype=f32))
    dwn = np.ascontiguousarray(np.asarray(L0_ffn_down_w, dtype=f32))
    c = np.asarray(L0_c, dtype=f32)
    wmod = np.ascontiguousarray(np.asarray(L0_wmod, dtype=f32))

    wtsT = np.ascontiguousarray(Wts.T)
    wkvT = np.ascontiguousarray(np.concatenate([Wk, Wv], 0).T)
    woT = np.ascontiguousarray(Wo.T)
    eyeF = np.eye(128, dtype=f32)
    ones_r = np.ones((1, 128), dtype=f32)
    ones_c = np.ones((128, 1), dtype=f32)
    tfB = np.ascontiguousarray(np.broadcast_to(tf[None, :], (128, S)))
    tdB = np.ascontiguousarray(np.broadcast_to(td[None, :], (4, S)))
    tdc = np.ascontiguousarray(td[:, None])

    in_maps = []
    for core in range(8):
        b, q = core // 4, core % 4
        mM = np.zeros((4, S), dtype=f32)
        mE = np.zeros((4, S), dtype=f32)
        for j in range(4):
            if j < q:
                mM[j, :] = 1.0
                mE[j, :] = 2048.0 * (q - 1 - j)
        cT = np.ascontiguousarray(c[b].reshape(KC, 128).T)
        in_maps.append(dict(
            x_own=np.ascontiguousarray(x[b, 2048 * q:2048 * (q + 1), :]),
            x_oth=np.ascontiguousarray(x[b, 2048 * ((q + 2) % 4):
                                         2048 * ((q + 2) % 4 + 1), :]),
            wtsT=wtsT, wkvT_in=wkvT, woT_in=woT,
            gate_r=gate[None, :], lnw_r=lnw[None, :], lnb_r=lnb[None, :],
            tdc=tdc, tfB=tfB, tdB=tdB, maskM=mM, maskE=mE,
            cT_in=cT,
            wmod_sh=np.ascontiguousarray(wmod[:, JSH * q:JSH * (q + 1)]),
            gw=gwn, uw=uwn, dw=dwn,
            eyeF=eyeF, ones_r=ones_r, ones_c=ones_c,
        ))

    nc = _get_nc()
    res = run_bass_kernel_spmd(nc, in_maps, core_ids=list(range(8)))
    out = np.empty((B, T, D), dtype=np.float32)
    for core in range(8):
        b, q = core // 4, core % 4
        out[b, 2048 * q:2048 * (q + 1), :] = res.results[core]["y"]
    return out
